# version 28
# baseline (speedup 1.0000x reference)
import os
import numpy as np
import concourse.bass as bass
import concourse.mybir as mybir
from concourse.bass_utils import run_bass_kernel_spmd

F32 = mybir.dt.float32
I32 = mybir.dt.int32
BF16 = mybir.dt.bfloat16
AF = mybir.ActivationFunctionType
ALU = mybir.AluOpType
AX = mybir.AxisListType

S, D, H, HD, HID = 2048, 1024, 16, 64, 256
RANK, DFF = 256, 4096
SC, DC = S // 128, D // 128
NPAIR = H // 2

IN_SPECS = [
    ("L0_attn_a_x", [S, D]), ("L0_attn_a_Wq", [D, D]), ("L0_attn_a_Wk", [D, D]),
    ("L0_attn_a_Wv", [D, D]), ("L0_attn_a_Wo", [D, D]), ("L0_attn_a_gamma", [D]),
    ("L0_attn_a_beta", [D]), ("L0_attn_b_W_K", [D, D]), ("L0_attn_b_W_V", [D, D]),
    ("L0_attn_b_W_Q", [D, D]), ("L0_attn_b_mem_W0", [HD, HID]),
    ("L0_attn_b_mem_W1", [HID, HD]), ("L0_attn_b_k_norm_w", [HD]),
    ("L0_attn_b_q_norm_w", [HD]), ("L0_attn_b_store_norm_w", [D]),
    ("L0_attn_b_retrieve_norm_w", [D]), ("L0_attn_b_alpha_w", [D, H]),
    ("L0_attn_b_alpha_b", [H]), ("L0_attn_b_theta_w", [D, H]),
    ("L0_attn_b_theta_b", [H]), ("L0_attn_b_eta_w", [D, H]), ("L0_attn_b_eta_b", [H]),
    ("L0_ffn_U1", [D, RANK]), ("L0_ffn_V1", [RANK, DFF]), ("L0_ffn_b1", [DFF]),
    ("L0_ffn_U2", [DFF, RANK]), ("L0_ffn_V2", [RANK, D]), ("L0_ffn_b2", [D]),
]

ENGKEYS = ("t", "v", "s", "g", "y")
_TRUNC = [None]


class Builder:
    def __init__(self, dbg=()):
        self.c_x_all = [None] * 8
        nc = bass.Bass("TRN2", target_bir_lowering=False)
        self.nc = nc
        self.dbg = list(dbg)
        self.inp = {}
        for n, sh in IN_SPECS:
            self.inp[n] = nc.dram_tensor(n, sh, F32, kind="ExternalInput").ap()
        self.out = nc.dram_tensor("out", [S, D], F32, kind="ExternalOutput").ap()
        self.dbg_out = {}

        self.ctr = {}
        def sem(name):
            h = nc.alloc_semaphore(name)
            self.ctr[h] = 0
            return h
        self.sem_of = {"t": sem("sPE"), "v": sem("sDVE"), "s": sem("sACT"),
                       "g": sem("sPOOL"), "y": sem("sSY")}
        self.sST = sem("sST")
        self._dma_sems = {}
        self._mksem = sem
        self.q = {k: [] for k in ENGKEYS}
        self.waited = {k: {} for k in ENGKEYS}

        A = nc.alloc_sbuf_tensor
        self.xf = A("xf", [128, DC * S], F32).ap()
        self.xbf = A("xbf", [128, DC * S], BF16).ap()
        self.stage = A("stage", [128, 16384], F32).ap()
        self.arena = A("arena", [128, 6144], F32).ap()
        self.bcast = A("bcast", [128, 2 * S], F32).ap()
        self.brow = A("brow", [128, S], BF16).ap()
        self.consts = A("consts", [128, 1024], BF16).ap()
        self.cols = A("cols", [128, 96], F32).ap()
        self.ident32 = A("ident32", [128, 128], F32).ap()
        self.constf = A("constf", [128, 16], F32).ap()
        self._const_vals = {}
        self.ones128 = self.consts[:, 0:128]
        self.ident = self.consts[:, 128:256]
        self.tri = self.consts[:, 256:384]
        self.B64 = self.consts[:, 384:512]
        self.onesrow = self.consts[0:1, 0:128]
        self.onescol = self.consts[:, 0:1]

        self.stage_bf = self.stage.bitcast(BF16)
        self.arena_bf = self.arena.bitcast(BF16)
        self.bcast_bf = self.bcast.bitcast(BF16)
        self.rlbf_war = None

        self.lnvar_war = None
        self.lnmu_war = None
        self.bcast0_war = None
        self.brow_war = None
        self.ps = [nc.alloc_psum_tensor(f"ps{i}", [128, 1024], F32).ap()
                   for i in range(4)]
        self.ps_bf = [p.bitcast(BF16) for p in self.ps]
        self.ps_rd = [None] * 4
        self.ps_wr = [None] * 4

    def dsem(self, name):
        if name not in self._dma_sems:
            self._dma_sems[name] = self._mksem("sD_" + name)
        return self._dma_sems[name]

    def emit(self, k, fn, waits=(), inc=None, inck=1):
        need = {}
        for t in waits:
            if t is None:
                continue
            s, v = t
            assert v <= self.ctr[s], "wait targets unscheduled inc"
            if v > need.get(s, -1):
                need[s] = v
        for s, v in need.items():
            if s in self._dma_sems.values() and v != self.ctr[s]:
                raise AssertionError(
                    f"non-full wait {v} on DMA sem (full={self.ctr[s]})")
        todo = []
        for s, v in need.items():
            if self.waited[k].get(s, -1) < v:
                todo.append((s, v))
                self.waited[k][s] = v
        tok = None
        semh = None
        if inc is True:
            semh = self.sem_of[k]
        elif inc is None or inc is False:
            semh = None
        else:
            semh = inc
        if semh is not None:
            self.ctr[semh] += inck
            tok = (semh, self.ctr[semh])
        self.q[k].append((todo, fn, semh, inck))
        return tok

    def finalize(self):
        nc = self.nc
        with nc.Block() as blk:
            def mk(k):
                def run(eng):
                    for todo, fn, semh, inck in self.q[k]:
                        for s, v in todo:
                            eng.wait_ge(s, v)
                        ins = fn(eng)
                        if semh is not None:
                            assert ins is not None, "inc on empty emission"
                            ins.then_inc(semh, inck)
                return run
            blk.tensor(mk("t"))
            blk.vector(mk("v"))
            blk.scalar(mk("s"))
            blk.gpsimd(mk("g"))
            blk.sync(mk("y"))
        return nc

    def ps_acquire(self, i):
        return self.ps_rd[i]

    def tap(self, name, ap, toks):
        if name not in self.dbg:
            return None
        o = self.nc.dram_tensor("dbg_" + name, list(ap.shape), ap.dtype,
                                kind="ExternalOutput").ap()
        t = self.emit("y", lambda e, o=o, ap=ap: e.dma_start(out=o, in_=ap),
                      waits=toks, inc=self.dsem("tap"), inck=16)
        self.dbg_out[name] = o
        return t

    def build(self):
        self.emit_consts()
        self.gate_scalar_on_consts()
        self.emit_load_x()
        steps = [("attn_a", self.sub1_attn_a), ("ffn1", lambda: self.sub2_ffn(0)),
                 ("attn_b", self.sub3_attn_b), ("ffn2", lambda: self.sub2_ffn(1))]
        for nm, fn in steps:
            if _TRUNC[0] == nm:
                break
            fn()
            if _TRUNC[0] in ("b1", "b2", "b2p1", "b3", "ffn_P") and nm == "attn_b":
                break
        self.emit_store()
        return self.finalize()

    def emit_consts(self):
        e = self.emit
        c = None
        def g(fn, c_prev):
            return e("g", fn, waits=[c_prev], inc=True)
        c = g(lambda gp: gp.memset(self.consts[:, :], 0.0), c)
        for val in (1e-5, 1e-6, 1e-8, 0.125, -0.5, -1.0, 0.5, 0.02, 2.0,
                    -2.0, 1.0, 0.0):
            i = len(self._const_vals)
            cell = self.constf[:, i:i + 1]
            self._const_vals[val] = cell
            self.nc.const_aps.register(F32, float(val), cell)
            c = g(lambda gp, c2=cell, v2=float(val): gp.memset(c2, v2), c)
        c = g(lambda gp: gp.memset(self.ones128, 1.0), c)
        c = g(lambda gp: gp.affine_select(out=self.ident, in_=self.ident,
                                          compare_op=ALU.not_equal, fill=1.0,
                                          base=0, pattern=[[-1, 128]],
                                          channel_multiplier=1), c)
        c = g(lambda gp: gp.affine_select(out=self.tri, in_=self.tri,
                                          compare_op=ALU.is_ge, fill=1.0,
                                          base=-1, pattern=[[-1, 128]],
                                          channel_multiplier=1), c)
        c = g(lambda gp: gp.memset(self.consts[0:1, 816:880], 1.0), c)
        c = g(lambda gp: gp.memset(self.B64[0:64, 0:64], 1.0), c)
        c = g(lambda gp: gp.memset(self.B64[64:128, 64:128], 1.0), c)
        c = g(lambda gp: gp.memset(self.ident32[:, :], 0.0), c)
        self.c_consts = g(lambda gp: gp.affine_select(
            out=self.ident32, in_=self.ident32, compare_op=ALU.not_equal,
            fill=1.0, base=0, pattern=[[-1, 128]], channel_multiplier=1), c)

    def gate_scalar_on_consts(self):
        self.emit("s", lambda s: s.nop(), waits=[self.c_consts])

    def emit_load_x(self):
        e = self.emit
        x = self.inp["L0_attn_a_x"]
        stage = self.stage
        xr = x.rearrange("(c p) f -> p c f", p=128)
        c_grp = []
        for k4 in range(SC // 4):
            cts = []
            for i in range(4):
                c = k4 * 4 + i
                cts.append(e("y", lambda en, o=stage[:, c * 1024:(c + 1) * 1024],
                             i_=xr[:, c, :]: en.dma_start(out=o, in_=i_),
                             inc=self.dsem("x" + str(k4)), inck=16))
            c_grp.append([cts[3]])
        for k4 in range(SC // 4):
            cw = c_grp[k4]
            for dd in range(DC):
                pi = (k4 * DC + dd) % 2
                pb = self.ps[pi]
                tlast = None
                for c4 in range(4):
                    c = k4 * 4 + c4
                    src = stage[:, c * 1024 + dd * 128: c * 1024 + dd * 128 + 128]
                    w = cw + [self.c_consts, self.ps_rd[pi]] if c4 == 0 else []
                    tlast = e("t", lambda t, o=pb[:, c4 * 128:(c4 + 1) * 128],
                              s=src: t.transpose(out=o, in_=s, identity=self.ident32),
                              waits=w, inc=(True if c4 == 3 else None))
                self.ps_wr[pi] = tlast
                dst = self.xf[:, dd * S + k4 * 512: dd * S + (k4 + 1) * 512]
                self.ps_rd[pi] = e("v", lambda v, d=dst, p=pb[:, 0:512]:
                                   v.tensor_copy(out=d, in_=p),
                                   waits=[tlast], inc=True)
        self.c_xf = self.tok(self.sem_of["v"])

    def tok(self, semh):
        return (semh, self.ctr[semh])

    def ln_stats(self, xsrc_bf, c_src, eps, rms_only, boff=0, rcols_off=None,
                 sq_off_bf=0):
        e = self.emit
        sq = [self.stage_bf[:, sq_off_bf + i * 1024: sq_off_bf + (i + 1) * 1024]
              for i in range(2)]
        c_sq = [None, None]
        c_sqmm = [None, None]
        c_rBs, c_nmrs = [], []
        for th in range(2):
            p0, p1 = self.ps[0], self.ps[1]
            w0 = [self.ps_rd[0], self.ps_rd[1], self.c_consts] + (
                list(c_src) if isinstance(c_src, list) else [c_src])
            for dd in range(DC):
                xs = xsrc_bf[:, dd * S + th * 1024: dd * S + th * 1024 + 1024]
                csd = c_src[dd] if isinstance(c_src, list) else c_src
                c_sq[dd % 2] = e("s", lambda s, o=sq[dd % 2], i_=xs:
                                 s.activation(out=o, in_=i_, func=AF.Square),
                                 waits=[csd, c_sqmm[dd % 2]], inc=True)
                for sub in range(2):
                    if not rms_only:
                        e("t", lambda t, o=p0[:, sub * 512:(sub + 1) * 512],
                          r=xs[:, sub * 512:(sub + 1) * 512], st=(dd == 0),
                          sp=(dd == DC - 1):
                          t.matmul(o, self.ones128, r, start=st, stop=sp),
                          waits=w0 if dd == 0 else [])
                    e("t", lambda t, o=p1[:, sub * 512:(sub + 1) * 512],
                      r=sq[dd % 2][:, sub * 512:(sub + 1) * 512], st=(dd == 0),
                      sp=(dd == DC - 1):
                      t.matmul(o, self.ones128, r, start=st, stop=sp),
                      waits=[c_sq[dd % 2]] + (w0 if dd == 0 else []),
                      inc=(True if sub == 1 else None))
                c_sqmm[dd % 2] = self.tok(self.sem_of["t"])
            c_mm = self.tok(self.sem_of["t"])
            mu = self.arena[:, 4096:5120]
            var = self.arena[:, 5120:6144]
            rB = self.bcast[:, boff + th * 1024: boff + (th + 1) * 1024]
            inv = 1.0 / D
            if rms_only:
                c_v = e("v", lambda v, o=var, i_=p1[:, :]: v.tensor_scalar(
                    out=o, in0=i_, scalar1=inv, scalar2=float(eps),
                    op0=ALU.mult, op1=ALU.add),
                    waits=[c_mm, self.lnvar_war], inc=True)
                self.ps_rd[1] = c_v
                lbias = 0.0
            else:
                c_mu = e("v", lambda v, o=mu, i_=p0[:, :]: v.tensor_scalar(
                    out=o, in0=i_, scalar1=inv, scalar2=None, op0=ALU.mult),
                    waits=[c_mm, self.lnmu_war], inc=True)
                self.ps_rd[0] = c_mu
                c_m2 = e("v", lambda v, o=rB, a=mu: v.tensor_mul(
                    out=o, in0=a, in1=a), waits=[c_mu], inc=True)
                c_v = e("v", lambda v, o=var, i_=p1[:, :], m2=rB:
                        v.scalar_tensor_tensor(out=o, in0=i_, scalar=inv,
                                               in1=m2, op0=ALU.mult,
                                               op1=ALU.subtract),
                        waits=[c_m2], inc=True)
                self.ps_rd[1] = c_v
                lbias = float(eps)
            c_ln = e("s", lambda s, o=var: s.activation(
                out=o, in_=o, func=AF.Ln, bias=lbias, scale=1.0),
                waits=[c_v], inc=True)
            c_rB = e("s", lambda s, o=rB, i_=var: s.activation(
                out=o, in_=i_, func=AF.Exp, scale=-0.5),
                waits=[c_ln, self.bcast0_war], inc=True)
            c_rBs.append(c_rB)
            self.lnvar_war = c_rB
            if not rms_only:
                nmr = self.brow[:, th * 1024:(th + 1) * 1024]
                c_nmr1 = e("v", lambda v, o=nmr, a=mu, b=rB:
                           v.scalar_tensor_tensor(out=o, in0=a, scalar=-1.0,
                                                  in1=b, op0=ALU.mult,
                                                  op1=ALU.mult),
                           waits=[c_rB, self.brow_war], inc=True)
                c_nmrs.append(c_nmr1)
                self.lnmu_war = c_nmr1
        c_rcols = None
        if rcols_off is not None:
            rc = self.cols[:, rcols_off: rcols_off + SC]
            for g in range(2):
                tlast = None
                for i in range(8):
                    c = g * 8 + i
                    src2 = self.bcast[:, boff + c * 128: boff + (c + 1) * 128]
                    w = [c_rBs[g], self.ps_rd[2]] if i == 0 else []
                    tlast = e("t", lambda t, o=self.ps[2][:, i * 128:(i + 1) * 128],
                              s=src2: t.transpose(out=o, in_=s,
                                                  identity=self.ident32),
                              waits=w, inc=(True if i == 7 else None))
                c_rcols = e("v", lambda v, o=rc[:, g * 8:(g + 1) * 8],
                            p=self.ps[2].rearrange("q (c f) -> q c f", c=8):
                            v.tensor_copy(out=o, in_=p[:, :, 0:1].squeeze(2)),
                            waits=[tlast], inc=True)
                self.ps_rd[2] = c_rcols
        return c_rBs, c_nmrs, c_rcols

    def cast_xbf(self, c_war=None):
        c = None
        cl = []
        for dd in range(DC):
            cx = self.c_x_all[dd] if self.c_x_all[dd] is not None else self.c_x
            c = self.emit("v", lambda v, o=self.xbf[:, dd * S:(dd + 1) * S],
                          i_=self.xf[:, dd * S:(dd + 1) * S]:
                          v.tensor_copy(out=o, in_=i_),
                          waits=[cx, c_war, c], inc=True)
            cl.append(c)
        self.c_cast_l = cl
        return c

    def apply_norm_done(self, c):
        self.bcast0_war = c
        self.brow_war = c

    def apply_norm(self, rB_t, nmr_t=None, ws_cols=None, c_prev=None):
        e = self.emit
        c = c_prev
        for dd in range(DC):
            xs = self.xbf[:, dd * S:(dd + 1) * S]
            rBs = self.bcast[:, 0:S]
            if ws_cols is not None:
                c = e("v", lambda v, o=xs, w=ws_cols[dd], b=rBs:
                      v.scalar_tensor_tensor(out=o, in0=o, scalar=w, in1=b,
                                             op0=ALU.mult, op1=ALU.mult),
                      waits=list(rB_t) + [c], inc=True)
            else:
                c = e("v", lambda v, o=xs, b=rBs:
                      v.tensor_mul(out=o, in0=o, in1=b),
                      waits=list(rB_t) + [c], inc=True)
            if nmr_t is not None:
                c = e("v", lambda v, o=xs, b=self.brow[:, 0:S]:
                      v.tensor_add(out=o, in0=o, in1=b),
                      waits=list(nmr_t) + [c], inc=True)
        return c

    def wT_slab(self, par, slab):
        return self.arena_bf[:, par * 3072 + slab * 1024: par * 3072 + slab * 1024 + 1024]

    def weight_pair_prep(self, p, specs):
        e = self.emit
        par = p % 2
        wst = self.arena[:, 3072:4096]
        wbf = self.stage_bf[:, 30784:31808]
        toks = []
        for (nm, slab, perm) in specs:
            w = self.inp[nm]
            if perm:
                wg = w.rearrange("(d2 g) d -> g d2 d", g=16)
                e("y", lambda y, o=wst[0:64, :], i_=wg[2 * p]:
                  y.dma_start(out=o, in_=i_),
                  waits=[self.wstage_free], inc=self.dsem("wp"), inck=16)
                c_d = e("y", lambda y, o=wst[64:128, :], i_=wg[2 * p + 1]:
                        y.dma_start(out=o, in_=i_), inc=self.dsem("wp"),
                        inck=16)
            else:
                c_d = e("y", lambda y, o=wst, i_=w[p * 128:(p + 1) * 128, :]:
                        y.dma_start(out=o, in_=i_),
                        waits=[self.wstage_free], inc=self.dsem("wp"), inck=16)
            c_c = e("v", lambda v, o=wbf, i_=wst: v.tensor_copy(out=o, in_=i_),
                    waits=[c_d, self.wbf_free], inc=True)
            self.wstage_free = c_c
            dst = self.wT_slab(par, slab)
            tok = None
            for half in range(2):
                tl = None
                for i in range(4):
                    ddc = half * 4 + i
                    wt = [c_c, self.ps_rd[3], self.wslab_free[par][slab]] if i == 0 else []
                    tl = e("t", lambda t, o=self.ps_bf[3][:, i * 128:(i + 1) * 128],
                           s=wbf[:, ddc * 128:(ddc + 1) * 128]:
                           t.transpose(out=o, in_=s, identity=self.ident),
                           waits=wt, inc=(True if i == 3 else None))
                tok = e("v", lambda v, o=dst[:, half * 512:(half + 1) * 512],
                        pp=self.ps_bf[3][:, 0:512]: v.tensor_copy(out=o, in_=pp),
                        waits=[tl], inc=True)
                self.ps_rd[3] = tok
            self.wbf_free = tok
            toks.append(tok)
        return toks

    def load_cols1024(self, name, dst_cols, psi=3):
        e = self.emit
        gst = self.stage[0:8, 15904:16032]
        c_d = e("y", lambda y, o=gst, i_=self.inp[name].rearrange("(c p) -> c p", c=8):
                y.dma_start(out=o, in_=i_), waits=[self.gst_free],
                inc=self.dsem("gst"), inck=16)
        ct = e("t", lambda t, o=self.ps[psi][:, 0:8], s=gst:
               t.transpose(out=o, in_=s, identity=self.ident32[0:8, 0:8]),
               waits=[c_d, self.ps_rd[psi], self.c_consts], inc=True)
        cv = e("v", lambda v, o=dst_cols, pp=self.ps[psi][:, 0:8]:
               v.tensor_copy(out=o, in_=pp), waits=[ct], inc=True)
        self.ps_rd[psi] = cv
        self.gst_free = cv
        return cv

    def sub1_attn_a(self):
        e = self.emit
        self.c_x = self.c_xf
        self.wstage_free = None
        self.wbf_free = None
        self.gst_free = None
        self.wslab_free = [[None] * 3, [None] * 3]
        self.rl_war = [None, None]
        c_cast = self.cast_xbf()
        c_rB, c_nmr, _ = self.ln_stats(self.xbf, c_cast, 1e-5, False,
                                       boff=0, sq_off_bf=28736)
        c_h1 = self.apply_norm(c_rB, c_nmr)
        self.apply_norm_done(c_h1)
        self.tap("h1bf", self.xbf, [c_h1])
        vt = self.stage_bf[:, 16384:18496]
        vt3 = vt.rearrange("p (c w) -> p c w", w=132)
        e("g", lambda g: g.memset(vt3[:, :, 64:65], 1.0), waits=[self.c_xf])
        c_vones = e("g", lambda g: g.memset(vt3[:, :, 130:131], 1.0), inc=True)
        c_ones32 = e("g", lambda g: g.memset(self.cols[:, 0:64], 1.0), inc=True)

        qf = self.stage_bf[:, 18496:20544]
        kf = self.stage_bf[:, 20544:22592]
        Esl = [self.stage_bf[:, 22592 + i * 512: 22592 + (i + 1) * 512]
               for i in range(4)]
        u = self.stage[:, 13344:14368]
        rlrow = [self.arena[0:1, 4096:5120], self.arena[0:1, 5120:6144]]

        c_qkv = [None] * DC
        wnames = [("L0_attn_a_Wq", 0, False), ("L0_attn_a_Wk", 1, False),
                  ("L0_attn_a_Wv", 2, True)]
        wtoks = self.weight_pair_prep(0, wnames)
        c_u_rd = None
        pend_epi = [None]
        c_E_rd = [None] * 4
        self.sc_rd = [[None, None], [None, None]]
        for p in range(NPAIR):
            par = p % 2
            wq, wk, wv = (self.wT_slab(par, i) for i in range(3))
            wtq, wtk, wtv = wtoks
            c_q = c_k = None
            for (dstap, wslab, wtok, psi) in ((qf, wq, wtq, 0), (kf, wk, wtk, 1)):
                ce = None
                for tt in range(4):
                    pb = self.ps[psi][:, (tt % 2) * 512:(tt % 2) * 512 + 512]
                    for dd in range(DC):
                        wt = ([wtok, c_h1, self.ps_rd[psi]] +
                              self.sc_rd[psi]) if dd == 0 else []
                        e("t", lambda t, o=pb,
                          l=wslab[:, dd * 128:(dd + 1) * 128],
                          r=self.xbf[:, dd * S + tt * 512: dd * S + tt * 512 + 512],
                          st=(dd == 0), sp=(dd == DC - 1):
                          t.matmul(o, l, r, start=st, stop=sp),
                          waits=wt, inc=(True if dd == DC - 1 else None))
                    cmm = self.tok(self.sem_of["t"])
                    ce = e("v", lambda v, o=dstap[:, tt * 512:(tt + 1) * 512],
                           pp=pb: v.tensor_copy(out=o, in_=pp),
                           waits=[cmm, ce], inc=True)
                    self.ps_rd[psi] = ce
                if psi == 0:
                    c_q = ce
                else:
                    c_k = ce
            if pend_epi[0] is not None:
                pend_epi[0]()
                pend_epi[0] = None
            c_v = None
            for half in range(2):
                for i in range(8):
                    cch = half * 8 + i
                    for dd in range(DC):
                        wt = [wtv, c_h1, self.ps_rd[2], c_vones] if (i == 0 and dd == 0) else []
                        e("t", lambda t, o=self.ps[2][:, i * 128:(i + 1) * 128],
                          l=self.xbf[:, dd * S + cch * 128: dd * S + cch * 128 + 128],
                          r=wv[:, dd * 128:(dd + 1) * 128],
                          st=(dd == 0), sp=(dd == DC - 1):
                          t.matmul(o, l, r, start=st, stop=sp),
                          waits=wt, inc=(True if (dd == DC - 1 and i == 7) else None))
                cmm = self.tok(self.sem_of["t"])
                c_v = e("v", lambda v, h=half:
                        v.tensor_copy(
                            out=vt3[:, h * 8:(h + 1) * 8, :].rearrange(
                                "p c (g w) -> p c g w", w=66)[:, :, :, 0:64],
                            in_=self.ps[2].rearrange("p (c g w) -> p c g w",
                                                     g=2, w=64)),
                        waits=[cmm, c_v], inc=True)
                self.ps_rd[2] = c_v
            self.wslab_free[par] = [c_q, c_k, c_v]
            if p + 1 < NPAIR:
                wtoks = self.weight_pair_prep(p + 1, wnames)
            qkvch = self.stage_bf[:, p * 2048:(p + 1) * 2048]
            for tq in range(4):
                climit = 4 * tq + 4
                c_pv_last = [None, None]
                pend = None
                def emit_pv(cc, cpvws):
                    for hh2 in range(2):
                        wt2 = [cpvws[hh2]]
                        if cc == 0:
                            wt2.append(self.ps_rd[2])
                        e("t", lambda t,
                          o=self.ps[2][0:65, hh2 * 512:hh2 * 512 + 512],
                          l=vt[:, cc * 132 + hh2 * 66: cc * 132 + hh2 * 66 + 65],
                          r=Esl[hh2 * 2 + (cc % 2)][:, 0:512],
                          st=(cc == 0), sp=(cc == climit - 1):
                          t.matmul(o, l, r, start=st, stop=sp),
                          waits=wt2,
                          inc=(True if cc == climit - 1 else None))
                        if cc == climit - 1:
                            c_pv_last[hh2] = self.tok(self.sem_of["t"])
                        c_E_rd[hh2 * 2 + (cc % 2)] = self.tok(self.sem_of["t"])
                for c in range(climit):
                    c_rel = c - 4 * tq
                    vfrom = max(0, c_rel * 128)
                    cpvws = []
                    for hh in range(2):
                        psS = self.ps[hh][:, (c % 2) * 512:(c % 2) * 512 + 512]
                        koff = hh * 64
                        e("t", lambda t, o=psS[:, vfrom:512],
                          l=kf[koff:koff + 64, c * 128:(c + 1) * 128],
                          r=qf[koff:koff + 64, tq * 512 + vfrom: (tq + 1) * 512]:
                          t.matmul(o, l, r, start=True, stop=True),
                          waits=[c_q, c_k, self.sc_rd[hh][c % 2]], inc=True)
                        cmm = self.tok(self.sem_of["t"])
                        sl = Esl[hh * 2 + (c % 2)]
                        cex = e("s", lambda s, o=sl[:, vfrom:512],
                                i_=psS[:, vfrom:512]:
                                s.activation(out=o, in_=i_, func=AF.Exp,
                                             scale=0.125),
                                waits=[cmm, c_E_rd[hh * 2 + (c % 2)]], inc=True)
                        self.sc_rd[hh][c % 2] = cex
                        cpvw = cex
                        if c_rel >= 0:
                            cpvw = e("v", lambda v, o=sl[:, vfrom:vfrom + 128]:
                                     v.tensor_mul(out=o, in0=o, in1=self.tri),
                                     waits=[cex], inc=True)
                            if vfrom > 0:
                                cpvw = e("v", lambda v, o=sl[:, 0:vfrom]:
                                         v.memset(o, 0.0), inc=True)
                        cpvws.append(cpvw)
                    if c == 1 and pend_epi[0] is not None:
                        pend_epi[0]()
                        pend_epi[0] = None
                    if pend is not None:
                        emit_pv(pend[0], pend[1])
                    pend = (c, cpvws)
                emit_pv(pend[0], pend[1])
                def epi(p2=p, tq2=tq, cpl=c_pv_last, qch=qkvch):
                    nonlocal c_u_rd
                    cuA = e("v", lambda v, o=u[0:64, 0:512],
                            pp=self.ps[2][0:64, 0:512]:
                            v.tensor_copy(out=o, in_=pp),
                            waits=[cpl[0], c_u_rd], inc=True)
                    cuB = e("v", lambda v, o=u[64:128, 0:512],
                            pp=self.ps[2][0:64, 512:1024]:
                            v.tensor_copy(out=o, in_=pp),
                            waits=[cpl[1], cuA], inc=True)
                    crl = []
                    for hh in range(2):
                        hi = self.bcast_bf[0:1, 4096 + hh * 512:
                                           4096 + hh * 512 + 512]
                        lo = self.bcast_bf[0:1, 5120 + hh * 512:
                                           5120 + hh * 512 + 512]
                        cl1 = e("s", lambda s, o=rlrow[hh][:, 0:512],
                                pp=self.ps[2][64:65, hh * 512:hh * 512 + 512]:
                                s.activation(out=o, in_=pp, func=AF.Ln),
                                waits=[cpl[hh], self.rl_war[hh]], inc=True)
                        c_rc = e("s", lambda s, o=rlrow[hh][:, 0:512]:
                                s.activation(out=o, in_=o, func=AF.Exp,
                                             scale=-1.0),
                                waits=[cl1], inc=True)
                        c_hi = e("v", lambda v, o=hi, i_=rlrow[hh][:, 0:512]:
                                 v.tensor_copy(out=o, in_=i_),
                                 waits=[c_rc, self.rlbf_war], inc=True)
                        c_lo = e("v", lambda v, o=lo, a=rlrow[hh][:, 0:512],
                                 b=hi:
                                 v.tensor_tensor(out=o, in0=a, in1=b,
                                                 op=ALU.subtract),
                                 waits=[c_hi], inc=True)
                        crl.append((c_lo, hi, lo))
                    for hh in range(2):
                        c_lo, hi, lo = crl[hh]
                        wt = [c_lo]
                        if hh == 0:
                            wt += [self.ps_rd[3], self.c_consts]
                        e("t", lambda t,
                          o=self.ps[3][hh * 64:hh * 64 + 64, 0:512],
                          l=self.onesrow[:, 0:64], r=hi:
                          t.matmul(o, l, r, start=True, stop=False),
                          waits=wt)
                        e("t", lambda t,
                          o=self.ps[3][hh * 64:hh * 64 + 64, 0:512],
                          l=self.onesrow[:, 0:64], r=lo:
                          t.matmul(o, l, r, start=False, stop=True),
                          waits=[], inc=(hh == 1))
                    crlB = self.tok(self.sem_of["t"])
                    self.rl_war = [crl[0][0], crl[1][0]]
                    self.rlbf_war = crlB
                    cq2 = e("v", lambda v,
                            o=qch[:, tq2 * 512:(tq2 + 1) * 512],
                            a=u[:, 0:512], b=self.ps[3][:, 0:512]:
                            v.tensor_mul(out=o, in0=a, in1=b),
                            waits=[cuA, cuB, crlB], inc=True)
                    c_u_rd = cq2
                    self.ps_rd[3] = cq2
                    self.ps_rd[2] = cq2
                    c_qkv[p2] = cq2
                pend_epi[0] = epi
            if p == 0:
                self.tap("qf0", qf, [c_q])
                self.tap("kf0", kf, [c_k])
                self.tap("vt0", vt, [c_v])
                self.tap("u0", u, [c_qkv[0]])
        if pend_epi[0] is not None:
            pend_epi[0]()
            pend_epi[0] = None
        for psi in range(2):
            toks = [t for t in self.sc_rd[psi] if t is not None]
            if toks:
                self.ps_rd[psi] = (toks[0][0], max(t[1] for t in toks))
        self.tap("qkv", self.stage_bf[:, 0:16384], c_qkv)
        self.sub1_epilogue(c_qkv)

    def sub1_epilogue(self, c_qkv):
        e = self.emit
        qkv = self.stage_bf[:, 0:16384]
        c_rB, c_nmr, _ = self.ln_stats(qkv, c_qkv[-1], 1e-5, False,
                                       boff=0, sq_off_bf=28736)
        c_ln = None
        for dd in range(DC):
            xs = qkv[:, dd * S:(dd + 1) * S]
            c_ln = e("v", lambda v, o=xs, b=self.bcast[:, 0:S]:
                     v.tensor_mul(out=o, in0=o, in1=b),
                     waits=list(c_rB) + [c_ln, c_qkv[dd]], inc=True)
            c_ln = e("v", lambda v, o=xs, b=self.brow[:, 0:S]:
                     v.tensor_add(out=o, in0=o, in1=b),
                     waits=list(c_nmr) + [c_ln], inc=True)
        self.apply_norm_done(c_ln)
        self.tap("lnq", qkv, [c_ln])
        gam = self.cols[:, 16:24]
        bet = self.cols[:, 24:32]
        bob = self.cols[:, 32:40]
        c_g = self.load_cols1024("L0_attn_a_gamma", gam)
        c_b = self.load_cols1024("L0_attn_a_beta", bet)
        bet_bf = self.consts[:, 640:648]
        c_bbf = e("v", lambda v: v.tensor_copy(out=bet_bf, in_=bet),
                  waits=[c_b], inc=True)
        woT = self.stage_bf[:, 16384:24576]
        wst = self.arena[:, 3072:4096]
        wbf = self.stage_bf[:, 30784:31808]
        c_woT = None
        for eb in range(DC):
            c_d = e("y", lambda y, o=wst,
                    i_=self.inp["L0_attn_a_Wo"][eb * 128:(eb + 1) * 128, :]:
                    y.dma_start(out=o, in_=i_),
                    waits=[self.wstage_free], inc=self.dsem("wp"), inck=16)
            c_c = e("v", lambda v, o=wbf, i_=wst: v.tensor_copy(out=o, in_=i_),
                    waits=[c_d, self.wbf_free], inc=True)
            self.wstage_free = c_c
            for half in range(2):
                tl = None
                for i in range(4):
                    jj = half * 4 + i
                    wt = [c_c, self.ps_rd[3]] if i == 0 else []
                    tl = e("t", lambda t, o=self.ps_bf[3][:, i * 128:(i + 1) * 128],
                           s=wbf[:, jj * 128:(jj + 1) * 128]:
                           t.transpose(out=o, in_=s, identity=self.ident),
                           waits=wt, inc=(True if i == 3 else None))
                c_woT = e("v", lambda v, h=half, ebb=eb:
                          v.tensor_copy(
                              out=woT.rearrange("p (c q) -> p c q", q=1024)
                              [:, h * 4:(h + 1) * 4, ebb * 128:(ebb + 1) * 128],
                              in_=self.ps_bf[3][:, 0:512].rearrange(
                                  "p (c q) -> p c q", q=128)),
                          waits=[tl, c_qkv[-1] if eb == 0 and half == 0 else None,
                                 c_woT],
                          inc=True)
                self.ps_rd[3] = c_woT
        for ee in range(DC):
            for jj in range(DC):
                wt = [c_woT, c_bbf]
                if ee == 0 and jj == 0:
                    wt.append(self.ps_rd[2])
                e("t", lambda t, o=self.ps[2][:, ee:ee + 1],
                  l=woT[:, jj * 1024 + ee * 128: jj * 1024 + (ee + 1) * 128],
                  r=bet_bf[:, jj:jj + 1], st=(jj == 0), sp=(jj == DC - 1):
                  t.matmul(o, l, r, start=st, stop=sp),
                  waits=wt, inc=(True if (ee == DC - 1 and jj == DC - 1) else None))
        c_bo = e("v", lambda v, o=bob, pp=self.ps[2][:, 0:8]:
                 v.tensor_copy(out=o, in_=pp),
                 waits=[self.tok(self.sem_of["t"])], inc=True)
        self.ps_rd[2] = c_bo
        c_bo_mm = self.tok(self.sem_of["t"])
        c_wg = [None] * DC
        for jj in range(DC):
            c_wg[jj] = e("v", lambda v, o=woT[:, jj * 1024:(jj + 1) * 1024],
                         g2=gam[:, jj:jj + 1]:
                         v.tensor_scalar(out=o, in0=o, scalar1=g2,
                                         scalar2=None, op0=ALU.mult),
                         waits=[c_woT, c_g, c_bo_mm], inc=True)
        c_x1 = [None] * DC
        for ee in range(DC):
            for tt in range(4):
                psi = ee % 2
                pb = self.ps[psi][:, (tt % 2) * 512:(tt % 2) * 512 + 512]
                for jj in range(DC):
                    wt = list(c_wg) + [c_ln] if jj == 0 else []
                    if jj == 0:
                        wt.append(self.ps_rd[psi])
                    e("t", lambda t, o=pb,
                      l=woT[:, jj * 1024 + ee * 128: jj * 1024 + (ee + 1) * 128],
                      r=qkv[:, jj * 2048 + tt * 512: jj * 2048 + tt * 512 + 512],
                      st=(jj == 0), sp=(jj == DC - 1):
                      t.matmul(o, l, r, start=st, stop=sp),
                      waits=wt, inc=(True if jj == DC - 1 else None))
                cmm = self.tok(self.sem_of["t"])
                cx = e("v", lambda v,
                       o=self.xf[:, ee * S + tt * 512: ee * S + tt * 512 + 512],
                       pp=pb, bo=bob[:, ee:ee + 1]:
                       v.scalar_tensor_tensor(out=o, in0=pp, scalar=bo,
                                              in1=o, op0=ALU.add, op1=ALU.add),
                       waits=[cmm, c_bo, c_x1[ee]], inc=True)
                self.ps_rd[psi] = cx
                c_x1[ee] = cx
        self.c_x = c_x1[-1]
        self.c_x_all = c_x1
        self.tap("x1", self.xf, c_x1)

    def load_wslab(self, name, pieces, c_war=None, nbuf=2, ceng="v"):
        e = self.emit
        toks = []
        for i, (src_ap, dst) in enumerate(pieces):
            wi = i % nbuf
            wst = self.stage[:, 9216 + wi * 1024: 9216 + wi * 1024 + 1024]
            w = wst[:, 0:1024]
            if len(src_ap.shape) == 3:
                wdma = w.rearrange("p (c f) -> p c f", c=src_ap.shape[1])
            else:
                wdma = w
            c_d = e("y", lambda y, o=wdma, i_=src_ap: y.dma_start(out=o, in_=i_),
                    waits=[self.wst2_free[wi]] + (
                        c_war if isinstance(c_war, list) else [c_war]),
                    inc=self.dsem("wl" + str(wi)), inck=16)
            if ceng == "s":
                c_c = e("s", lambda s, o=dst, i_=w:
                        s.activation(out=o, in_=i_, func=AF.Copy),
                        waits=[c_d], inc=True)
            else:
                c_c = e("v", lambda v, o=dst, i_=w: v.tensor_copy(out=o, in_=i_),
                        waits=[c_d], inc=True)
            self.wst2_free[wi] = c_c
            toks.append(c_c)
        return toks

    def load_cols(self, name, n, ncols, dst_cols):
        e = self.emit
        gst = self.stage[0:ncols, 15904:15904 + 128]
        c_d = e("y", lambda y, o=gst[:, 0:n],
                i_=self.inp[name].rearrange("(c p) -> c p", c=ncols):
                y.dma_start(out=o, in_=i_), waits=[self.gst_free],
                inc=self.dsem("gst"), inck=16)
        ct = e("t", lambda t, o=self.ps[3][0:n, 0:ncols], s=gst[:, 0:n]:
               t.transpose(out=o, in_=s,
                           identity=self.ident32[0:ncols, 0:ncols]),
               waits=[c_d, self.ps_rd[3], self.c_consts], inc=True)
        cv = e("v", lambda v, o=dst_cols[0:n, :], pp=self.ps[3][0:n, 0:ncols]:
               v.tensor_copy(out=o, in_=pp), waits=[ct], inc=True)
        self.ps_rd[3] = cv
        self.gst_free = cv
        return cv

    def sub2_ffn(self, tag):
        e = self.emit
        self.wst2_free = [None] * 4
        barrier = self.tok(self.sem_of["t"])
        c_cast = self.cast_xbf(c_war=barrier)
        cast_l = self.c_cast_l
        u1 = self.arena_bf[:, 0:2048]
        v2 = self.arena_bf[:, 2048:4096]
        v1 = self.arena_bf[:, 4096:12288]
        u2 = self.stage_bf[:, 0:8192]
        P_sb = self.stage_bf[:, 8192:12288]
        Q_sb = self.stage_bf[:, 12288:16384]
        Hb = [self.stage_bf[:, 16384 + i * 1024: 16384 + (i + 1) * 1024]
              for i in range(2)]
        U1, V1, U2, V2 = (self.inp["L0_ffn_U1"], self.inp["L0_ffn_V1"],
                          self.inp["L0_ffn_U2"], self.inp["L0_ffn_V2"])
        b1c = self.cols[:, 40:72]
        b2c = self.cols[:, 72:80]
        c_b1 = self.load_cols("L0_ffn_b1", 128, 32, b1c)
        c_b2 = self.load_cols("L0_ffn_b2", 128, 8, b2c)
        pieces = []
        u1r = U1.rearrange("(c p) f -> p c f", p=128)
        for g in range(2):
            pieces.append((u1r[:, 4 * g:4 * g + 4, :],
                           u1[:, g * 1024:(g + 1) * 1024]))
        for cc in range(2):
            pieces.append((V2[cc * 128:(cc + 1) * 128, :],
                           v2[:, cc * 1024:(cc + 1) * 1024]))
        wtok = self.load_wslab("ffn_a", pieces, c_war=barrier, nbuf=4,
                               ceng="s")
        c_u1 = wtok[1]; c_v2 = wtok[3]
        s1c = self.cols[:, 80:82]
        for rc in range(2):
            for dd in range(DC):
                wt = []
                if dd == 0:
                    wt = [c_u1] + ([self.ps_rd[2]] if rc == 0 else [])
                e("t", lambda t, o=self.ps[2][:, rc:rc + 1],
                  l=u1[:, dd * 256 + rc * 128: dd * 256 + (rc + 1) * 128],
                  r=self.ones128[:, 0:1],
                  st=(dd == 0), sp=(dd == DC - 1):
                  t.matmul(o, l, r, start=st, stop=sp),
                  waits=wt, inc=(True if (rc == 1 and dd == DC - 1) else None))
        cs1m = self.tok(self.sem_of["t"])
        c_s1 = e("v", lambda v, o=s1c, pp=self.ps[2][:, 0:2]:
                 v.tensor_copy(out=o, in_=pp), waits=[cs1m], inc=True)
        self.ps_rd[2] = c_s1
        c_rB, c_nmr, _ = self.ln_stats(self.xbf, cast_l, 1e-5, False,
                                       boff=0, sq_off_bf=28736)
        tmpP = self.stage[:, 13312:13824]
        c_P = [None, None]
        for rc in range(2):
            for tt in range(4):
                pb = self.ps[rc][:, (tt % 2) * 512:(tt % 2) * 512 + 512]
                for dd in range(DC):
                    wt = [cast_l[dd]]
                    if dd == 0:
                        wt += [c_u1, self.ps_rd[rc]]
                    e("t", lambda t, o=pb,
                      l=u1[:, dd * 256 + rc * 128: dd * 256 + (rc + 1) * 128],
                      r=self.xbf[:, dd * S + tt * 512: dd * S + tt * 512 + 512],
                      st=(dd == 0), sp=(dd == DC - 1):
                      t.matmul(o, l, r, start=st, stop=sp),
                      waits=wt, inc=(True if dd == DC - 1 else None))
                cmm = self.tok(self.sem_of["t"])
                th2 = tt // 2
                cP1 = e("v", lambda v, o=tmpP,
                        a=self.brow[:, tt * 512:(tt + 1) * 512],
                        w=s1c[:, rc:rc + 1], pp=pb:
                        v.scalar_tensor_tensor(out=o, in0=a, scalar=w, in1=pp,
                                               op0=ALU.mult, op1=ALU.add),
                        waits=[cmm, c_s1, self.tmpP_war,
                               c_nmr[th2] if th2 < len(c_nmr) else None],
                        inc=True)
                self.ps_rd[rc] = cP1
                c_P[rc] = e("v", lambda v,
                            o=P_sb[:, rc * 2048 + tt * 512: rc * 2048 + tt * 512 + 512],
                            a=tmpP, b=self.bcast[:, tt * 512:(tt + 1) * 512]:
                            v.tensor_mul(out=o, in0=a, in1=b),
                            waits=[cP1, c_P[rc],
                                   c_rB[th2] if th2 < len(c_rB) else None],
                            inc=True)
                self.tmpP_war = c_P[rc]
        self.apply_norm_done(c_P[1])
        pieces_u2 = []
        u2r = U2.rearrange("(c p) f -> p c f", p=128)
        for g in range(8):
            pieces_u2.append((u2r[:, 4 * g:4 * g + 4, :],
                              u2[:, g * 1024:(g + 1) * 1024]))
        wtok2 = self.load_wslab("ffn_u2", pieces_u2, c_war=barrier, nbuf=4,
                                ceng="s")
        c_u2g = [wtok2[g] for g in range(8)]
        pieces_v1 = []
        for pc in range(4):
            for cc in range(2):
                pieces_v1.append((V1[cc * 128:(cc + 1) * 128,
                                     pc * 1024:(pc + 1) * 1024],
                                  v1[:, cc * 4096 + pc * 1024: cc * 4096 + (pc + 1) * 1024]))
        wtok3 = self.load_wslab("ffn_v1", pieces_v1,
                                c_war=[barrier, c_nmr[-1], c_rB[-1]], nbuf=4,
                                ceng="s")
        c_v1p = [(wtok3[2 * pc], wtok3[2 * pc + 1]) for pc in range(4)]
        self.tap("Pf" if tag == 0 else "Pf2", P_sb, [c_P[0], c_P[1]])
        if _TRUNC[0] == "ffn_P":
            self.c_x = self.c_x_all[-1]
            return
        c_Q = [None, None]
        for th in range(2):
            c_Hev = [None, None]

            def emit_Q(fc):
                hb = fc % 2
                for rc in range(2):
                    for sub in range(2):
                        wt = [c_Hev[hb], c_u2g[fc // 4]]
                        if fc == 0 and rc == 0 and sub == 0:
                            wt += [self.ps_rd[2], self.ps_rd[3]]
                        e("t", lambda t,
                          o=self.ps[2 + rc][:, sub * 512:(sub + 1) * 512],
                          l=u2[:, fc * 256 + rc * 128: fc * 256 + (rc + 1) * 128],
                          r=Hb[hb][:, sub * 512:(sub + 1) * 512],
                          st=(fc == 0), sp=(fc == DFF // 128 - 1):
                          t.matmul(o, l, r, start=st, stop=sp),
                          waits=wt,
                          inc=(True if (rc == 1 and sub == 1) else None))
                self.c_Qmm = getattr(self, "c_Qmm", [None, None])
                self.c_Qmm[hb] = self.tok(self.sem_of["t"])

            for fc in range(DFF // 128):
                hb = fc % 2
                psH = self.ps[hb]
                for cc in range(2):
                    for sub in range(2):
                        wt = []
                        if cc == 0 and sub == 0:
                            wt = [c_v1p[fc // 8][0], c_v1p[fc // 8][1],
                                  c_P[0], c_P[1], c_Hev[hb],
                                  self.ps_rd[hb]]
                        e("t", lambda t, o=psH[:, sub * 512:(sub + 1) * 512],
                          l=v1[:, cc * 4096 + fc * 128: cc * 4096 + (fc + 1) * 128],
                          r=P_sb[:, cc * 2048 + th * 1024 + sub * 512:
                                 cc * 2048 + th * 1024 + sub * 512 + 512],
                          st=(cc == 0), sp=(cc == 1):
                          t.matmul(o, l, r, start=st, stop=sp),
                          waits=wt,
                          inc=(True if (cc == 1 and sub == 1) else None))
                c_Hmm = self.tok(self.sem_of["t"])
                c_Hev[hb] = e("s", lambda s, o=Hb[hb], i_=psH,
                              b=b1c[:, fc:fc + 1]:
                              s.activation(out=o, in_=i_, func=AF.Gelu_apprx_tanh,
                                           bias=b, scale=1.0),
                              waits=[c_Hmm, c_b1,
                                     (self.c_Qmm[hb] if hasattr(self, "c_Qmm") else None)],
                              inc=True)
                self.ps_rd[hb] = c_Hev[hb]
                if fc >= 1:
                    emit_Q(fc - 1)
            emit_Q(DFF // 128 - 1)
            cqmm = self.tok(self.sem_of["t"])
            for rc in range(2):
                c_Q[rc] = e("v", lambda v,
                            o=Q_sb[:, rc * 2048 + th * 1024: rc * 2048 + (th + 1) * 1024],
                            pp=self.ps[2 + rc]: v.tensor_copy(out=o, in_=pp),
                            waits=[cqmm, c_Q[rc]], inc=True)
                self.ps_rd[2 + rc] = c_Q[rc]
            for ee in range(DC):
                psi = ee % 2
                pb = self.ps[psi]
                for cc in range(2):
                    for sub in range(2):
                        wt = []
                        if cc == 0 and sub == 0:
                            wt = [c_v2, c_Q[0], c_Q[1], self.ps_rd[psi]]
                        e("t", lambda t, o=pb[:, sub * 512:(sub + 1) * 512],
                          l=v2[:, cc * 1024 + ee * 128: cc * 1024 + (ee + 1) * 128],
                          r=Q_sb[:, cc * 2048 + th * 1024 + sub * 512:
                                 cc * 2048 + th * 1024 + sub * 512 + 512],
                          st=(cc == 0), sp=(cc == 1):
                          t.matmul(o, l, r, start=st, stop=sp),
                          waits=wt,
                          inc=(True if (cc == 1 and sub == 1) else None))
                cmm = self.tok(self.sem_of["t"])
                cx = e("v", lambda v,
                       o=self.xf[:, ee * S + th * 1024: ee * S + (th + 1) * 1024],
                       pp=pb, bo=b2c[:, ee:ee + 1]:
                       v.scalar_tensor_tensor(out=o, in0=pp, scalar=bo,
                                              in1=o, op0=ALU.add, op1=ALU.add),
                       waits=[cmm, c_b2, self.c_x_all[ee]], inc=True)
                self.ps_rd[psi] = cx
                self.c_x_all[ee] = cx
        del self.c_Qmm
        self.c_x = self.c_x_all[-1]
        self.tap("x2" if tag == 0 else "x4", self.xf, list(self.c_x_all))

    def sub3_attn_b(self):
        e = self.emit
        self.rk_war = None
        self.nw1_war = None
        self.ret_tmp_war = None
        self.tmpo_war = None
        self.sq2_war = [None, None]
        self.p1q_war = [None, None]
        self.gg_free = None
        barrier = self.tok(self.sem_of["t"])
        c_cast = self.cast_xbf(c_war=barrier)
        c_rB, c_nmr, _ = self.ln_stats(self.xbf, c_cast, 1e-5, False,
                                       boff=0, sq_off_bf=28736)
        c_xo = self.apply_norm(c_rB, c_nmr)
        self.apply_norm_done(c_xo)
        self.tap("xo", self.xbf, [c_xo])
        c_rn, _, c_rcols = self.ln_stats(self.xbf, c_xo, 1e-6, True,
                                         boff=0, rcols_off=0, sq_off_bf=28736)
        self.c_rcols_tok = c_rcols
        self.tap("rnB", self.bcast[:, 0:S], c_rn)
        wscol = self.cols[:, 16:24]
        wrcol = self.cols[:, 24:32]
        c_ws = self.load_cols("L0_attn_b_store_norm_w", 128, 8, wscol)
        c_wr = self.load_cols("L0_attn_b_retrieve_norm_w", 128, 8, wrcol)
        knw = self.cols[:, 32:33]
        qnw = self.cols[:, 33:34]
        c_knw = self.load_headcol("L0_attn_b_k_norm_w", knw)
        c_qnw = self.load_headcol("L0_attn_b_q_norm_w", qnw)
        self.c_magic = self.emit(
            "g", lambda g: g.memset(self.stage[:, 14088:14092], 1.3211836172961055e+19),
            waits=[barrier], inc=True)
        c_mem = self.load_mem_weights()
        c_atw = self.load_atw(wscol, c_ws, c_war=barrier)
        wk = self.arena_bf[:, 0:8192]
        wv = self.stage_bf[:, 0:8192]
        c_wk = self.load_w1024("L0_attn_b_W_K", wk, wscol, c_ws, c_war=barrier)
        c_wv = self.load_w1024("L0_attn_b_W_V", wv, wscol, c_ws, c_war=barrier)
        c_at = self.proj_alpha_theta(c_atw, c_xo, c_rn, c_rcols)
        self.tap("at", self.stage_bf[:, 24576:25088], [c_at])
        c_am = self.a_mean_row(c_at)
        if _TRUNC[0] == "b1":
            return
        c_nw0, c_nw1 = self.grad_pairs(c_wk, c_wv, c_mem, c_xo, c_rn, c_knw,
                                       c_at, c_am)
        if c_nw0 is None:
            return
        self.tap("nw0", self.arena_bf[:, 8832:10880], [c_nw0])
        self.tap("nw1", self.stage_bf[:, 25088:27136], [c_nw1])
        if (_TRUNC[0] or "").startswith("b2") or _TRUNC[0] == "b3":
            return
        self.retrieve_pairs(c_nw0, c_nw1, c_xo, c_rn, c_wr, wrcol, c_qnw,
                            barrier)
        self.tap("x3", self.xf, list(self.c_x_all))

    def load_headcol(self, name, dst):
        e = self.emit
        gst = self.stage[0:1, 15904:15968]
        c_d = e("y", lambda y, o=gst,
                i_=self.inp[name].unsqueeze(0):
                y.dma_start(out=o, in_=i_), waits=[self.gst_free],
                inc=self.dsem("gst"), inck=16)
        ct = e("t", lambda t, o=self.ps[3][0:64, 0:1], s=gst:
               t.transpose(out=o, in_=s, identity=self.ident32[0:1, 0:1]),
               waits=[c_d, self.ps_rd[3]], inc=True)
        cv = e("v", lambda v, o=dst[0:64, :], pp=self.ps[3][0:64, 0:1]:
               v.tensor_copy(out=o, in_=pp), waits=[ct], inc=True)
        cv = e("v", lambda v, o=dst[64:128, :], i_=dst[0:64, :]:
               v.tensor_copy(out=o, in_=i_), waits=[cv], inc=True)
        self.ps_rd[3] = cv
        self.gst_free = cv
        return cv

    def load_w1024(self, name, slab, ws_cols, c_ws, c_war=None):
        e = self.emit
        W = self.inp[name]
        c_c = None
        for dd in range(DC):
            wi = dd % 2
            wst = self.stage[:, 9216 + wi * 1024: 9216 + wi * 1024 + 1024]
            c_d = e("y", lambda y, o=wst, i_=W[dd * 128:(dd + 1) * 128, :]:
                    y.dma_start(out=o, in_=i_),
                    waits=[self.wst2_free[wi], c_war],
                    inc=self.dsem("wl" + str(wi)), inck=16)
            c_c = e("s", lambda s, o=slab[:, dd * 1024:(dd + 1) * 1024],
                    i_=wst, w=ws_cols[:, dd:dd + 1]:
                    s.activation(out=o, in_=i_, func=AF.Copy, scale=w),
                    waits=[c_d, c_ws, c_war, c_c], inc=True)
            self.wst2_free[wi] = c_c
        return c_c

    def load_mem_weights(self):
        e = self.emit
        w0bd = self.arena_bf[:, 8192:8704]
        w1s = self.arena_bf[:, 8704:8832]
        w1bd = self.stage_bf[:, 27648:28160]
        st0 = self.stage[0:64, 15392:15648]
        cz0 = e("g", lambda g, o=w0bd: g.memset(o, 0.0), waits=[self.gst_free],
                inc=True)
        cz1 = e("g", lambda g, o=w1bd: g.memset(o, 0.0), waits=[cz0], inc=True)
        c_d0 = e("y", lambda y, o=st0, i_=self.inp["L0_attn_b_mem_W0"]:
                 y.dma_start(out=o, in_=i_), waits=[self.gst_free],
                 inc=self.dsem("mem"), inck=16)
        c0 = e("v", lambda v, o=w0bd[0:64, 0:256], i_=st0:
               v.tensor_copy(out=o, in_=i_), waits=[c_d0, cz0], inc=True)
        c0 = e("v", lambda v, o=w0bd[64:128, 256:512], i_=w0bd[0:64, 0:256]:
               v.tensor_copy(out=o, in_=i_), waits=[c0], inc=True)
        st1 = self.stage[:, 15648:15776]
        c_d1 = e("y", lambda y, o=st1.rearrange("p (c f) -> p c f", c=2),
                 i_=self.inp["L0_attn_b_mem_W1"].rearrange("(c p) f -> p c f", p=128):
                 y.dma_start(out=o, in_=i_), waits=[self.tok(self.dsem("mem"))],
                 inc=self.dsem("mem"), inck=16)
        c1 = e("v", lambda v, o=w1s, i_=st1: v.tensor_copy(out=o, in_=i_),
               waits=[c_d1], inc=True)
        tl = None
        for hc in range(2):
            tl = e("t", lambda t, o=self.ps_bf[3][0:64, hc * 128:(hc + 1) * 128],
                   s=w1s[:, hc * 64:(hc + 1) * 64]:
                   t.transpose(out=o, in_=s, identity=self.ident),
                   waits=([c1, self.ps_rd[3]] if hc == 0 else []),
                   inc=(hc == 1))
        c2 = e("v", lambda v, o=w1bd[0:64, 0:256], pp=self.ps_bf[3][0:64, 0:256]:
               v.tensor_copy(out=o, in_=pp), waits=[tl, cz1], inc=True)
        c2 = e("v", lambda v, o=w1bd[64:128, 256:512], i_=w1bd[0:64, 0:256]:
               v.tensor_copy(out=o, in_=i_), waits=[c2], inc=True)
        self.ps_rd[3] = c2
        self.gst_free = c2
        e("y", lambda y: y.dma_start(
            out=self.stage[:, 16032:16160].rearrange("p (c f) -> p c f", c=8),
            in_=self.inp["L0_attn_b_eta_w"].rearrange("(c p) f -> p c f", p=128)),
          waits=[c2], inc=self.dsem("eta"), inck=16)
        e("y", lambda y: y.dma_start(
            out=self.stage[0:1, 16160:16176],
            in_=self.inp["L0_attn_b_eta_b"].unsqueeze(0)),
          waits=[self.tok(self.dsem("eta"))], inc=self.dsem("eta"), inck=16)
        return (c0, c1, c2)

    def load_atw(self, wscol, c_ws, c_war):
        e = self.emit
        atw = self.arena_bf[:, 11136:11392]
        st = self.stage[:, 15392:15424]
        c_all = None
        for dd in range(DC):
            cds = []
            for k, nm in ((0, "L0_attn_b_alpha_w"), (1, "L0_attn_b_theta_w")):
                cds.append(e("y", lambda y, o=st[:, k * 16:(k + 1) * 16],
                             i_=self.inp[nm][dd * 128:(dd + 1) * 128, :]:
                             y.dma_start(out=o, in_=i_),
                             waits=[c_all, c_war, self.gst_free],
                             inc=self.dsem("atw"), inck=16))
            c_all = e("v", lambda v, o=atw[:, dd * 32:(dd + 1) * 32], i_=st,
                      w=wscol[:, dd:dd + 1]:
                      v.tensor_scalar(out=o, in0=i_, scalar1=w, scalar2=None,
                                      op0=ALU.mult),
                      waits=cds + [c_ws, c_all], inc=True)
        br = self.stage[0:1, 15424:15456]
        c_b1 = e("y", lambda y, o=br[:, 0:16],
                 i_=self.inp["L0_attn_b_alpha_b"].unsqueeze(0):
                 y.dma_start(out=o, in_=i_), waits=[c_all],
                 inc=self.dsem("bias"), inck=16)
        c_b2 = e("y", lambda y, o=br[:, 16:32],
                 i_=self.inp["L0_attn_b_theta_b"].unsqueeze(0):
                 y.dma_start(out=o, in_=i_), inc=self.dsem("bias"), inck=16)
        brbf = self.consts[0:1, 648:680]
        c_bb = e("v", lambda v, o=brbf, i_=br: v.tensor_copy(out=o, in_=i_),
                 waits=[c_b1, c_b2], inc=True)
        cbm = e("t", lambda t, o=self.ps[3][:, 0:32], l=self.onesrow,
                r=brbf: t.matmul(o, l, r, start=True, stop=True),
                waits=[c_bb, self.ps_rd[3], self.c_consts], inc=True)
        self.biasB = self.cols[:, 40:72]
        c_bB = e("v", lambda v, o=self.biasB, pp=self.ps[3][:, 0:32]:
                 v.tensor_copy(out=o, in_=pp), waits=[cbm], inc=True)
        self.ps_rd[3] = c_bB
        self.c_biasB = c_bB
        return c_all

    def proj_alpha_theta(self, c_atw, c_xo, c_rn, c_rcols):
        e = self.emit
        at = self.stage_bf[:, 24576:25088]
        atw = self.arena_bf[:, 11136:11392]
        c_ev = []
        for tc in range(SC):
            psi = tc % 4
            pb = self.ps[psi][:, 0:32]
            for dd in range(DC):
                wt = [c_atw, c_xo, self.ps_rd[psi]] if dd == 0 else []
                e("t", lambda t, o=pb,
                  l=self.xbf[:, dd * S + tc * 128: dd * S + (tc + 1) * 128],
                  r=atw[:, dd * 32:(dd + 1) * 32], st=(dd == 0),
                  sp=(dd == DC - 1):
                  t.matmul(o, l, r, start=st, stop=sp),
                  waits=wt, inc=(True if dd == DC - 1 else None))
            cmm = self.tok(self.sem_of["t"])
            cv = e("v", lambda v, o=at[:, tc * 32:(tc + 1) * 32], pp=pb,
                   rn=self.cols[:, tc:tc + 1], bb=self.biasB:
                   v.scalar_tensor_tensor(out=o, in0=pp, scalar=rn, in1=bb,
                                          op0=ALU.mult, op1=ALU.add),
                   waits=[cmm, c_rcols, self.c_biasB], inc=True)
            self.ps_rd[psi] = cv
            c_ev.append(cv)
        c_at = e("s", lambda s, o=at: s.activation(out=o, in_=o,
                                                   func=AF.Sigmoid),
                 waits=c_ev, inc=True)
        return c_at

    def a_mean_row(self, c_at):
        e = self.emit
        at = self.stage_bf[:, 24576:25088]
        onescol_bf = self.ones128[:, 0:1]
        for tc in range(SC):
            wt = [c_at, self.c_consts]
            if tc == 0:
                wt.append(self.ps_rd[3])
            e("t", lambda t, o=self.ps[3][0:1, 0:32],
              l=onescol_bf, r=at[:, tc * 32:(tc + 1) * 32],
              st=(tc == 0), sp=(tc == SC - 1):
              t.matmul(o, l, r, start=st, stop=sp),
              waits=wt, inc=(True if tc == SC - 1 else None))
        cs = self.tok(self.sem_of["t"])
        amr = self.arena[0:1, 5120:5152]
        cv = e("v", lambda v, o=amr, pp=self.ps[3][0:1, 0:32]:
               v.tensor_scalar(out=o, in0=pp, scalar1=1.0 / S, scalar2=None,
                               op0=ALU.mult), waits=[cs], inc=True)
        self.ps_rd[3] = cv
        ambf = self.consts[0:1, 680:712]
        cb = e("v", lambda v, o=ambf, i_=amr: v.tensor_copy(out=o, in_=i_),
               waits=[cv], inc=True)
        selA = self.B64[0:1, 0:128]
        selB = self.consts[0:1, 752:880]
        for p in range(NPAIR):
            for hh in range(2):
                wt = [cb]
                if p == 0 and hh == 0:
                    wt.append(self.ps_rd[2])
                e("t", lambda t, o=self.ps[2][:, 48 + p:49 + p],
                  l=(selA if hh == 0 else selB),
                  r=ambf[:, 2 * p + hh:2 * p + hh + 1],
                  st=(hh == 0), sp=(hh == 1):
                  t.matmul(o, l, r, start=st, stop=sp),
                  waits=wt, inc=(True if (p == NPAIR - 1 and hh == 1) else None))
        cm = self.tok(self.sem_of["t"])
        fam = self.cols[:, 72:80]
        cf = e("v", lambda v, o=fam, pp=self.ps[2][:, 48:56]:
               v.tensor_scalar(out=o, in0=pp, scalar1=-1.0, scalar2=1.0,
                               op0=ALU.mult, op1=ALU.add),
               waits=[cm], inc=True)
        self.ps_rd[2] = cf
        return cf

    def mem_oct_A(self, p, o, c_kf, c_mem):
        e = self.emit
        kf = self.stage_bf[:, 8192:10240]
        par = o % 2
        a1t = self.stage_bf[:, 16384 + par * 1024: 17408 + par * 1024]
        a1f = self.stage_bf[:, 18432 + par * 1024: 19456 + par * 1024]
        gg = self.stage_bf[:, 20480 + par * 1024: 21504 + par * 1024]
        a2 = self.stage_bf[:, 14336:16384]
        w0bd = self.arena_bf[:, 8192:8704]
        w1s = self.arena_bf[:, 8704:8832]
        qs = o * 256
        c_a1f = None
        for k in range(4):
            hh, hc = k // 2, k % 2
            psi = 1 - (k // 2)
            pb = self.ps[psi][:, (k % 2) * 512:(k % 2) * 512 + 256]
            wt = []
            if k == 0:
                wt = [c_kf, c_mem[0], self.ps_rd[1], self.oct_a1f_war[par]]
            elif k == 2:
                wt = [self.ps_rd[0]]
            e("t", lambda t, o2=pb,
              l=w0bd[hh * 64:(hh + 1) * 64,
                     hh * 256 + hc * 128: hh * 256 + (hc + 1) * 128],
              r=kf[hh * 64:(hh + 1) * 64, qs:qs + 256]:
              t.matmul(o2, l, r, start=True, stop=True),
              waits=wt, inc=(k % 2 == 1))
            if k % 2 == 1:
                cmmp = self.tok(self.sem_of["t"])
                psi2 = 1 - (k // 2)
                c_a1f = e("s", lambda s,
                          o2=a1f[:, (k - 1) * 256:(k + 1) * 256].rearrange(
                              "p (b c) -> p b c", b=2),
                          i_=self.ps[psi2].rearrange(
                              "p (b c) -> p b c", b=2)[:, :, 0:256]:
                          s.activation(out=o2, in_=i_, func=AF.Gelu),
                          waits=[cmmp, c_a1f], inc=True)
                self.ps_rd[psi2] = c_a1f
        c_a1t = c_gg = None
        cmm2l = []
        for sc in range(2):
            cch = o * 2 + sc
            psi = 1 - (sc % 2)
            e("t", lambda t, o2=self.ps[psi][:, 0:512],
              l=kf[:, cch * 128:(cch + 1) * 128], r=w0bd:
              t.matmul(o2, l, r, start=True, stop=True),
              waits=[self.ps_rd[psi], self.oct_a1t_war[par],
                     self.oct_gg_war[par]], inc=True)
            cmm2l.append(self.tok(self.sem_of["t"]))
        for sc in range(2):
            psi = 1 - (sc % 2)
            c_a1t = e("s", lambda s,
                      o2=a1t[:, sc * 512:(sc + 1) * 512],
                      i_=self.ps[psi][:, 0:512]:
                      s.activation(out=o2, in_=i_, func=AF.Gelu),
                      waits=[cmm2l[sc]], inc=True)
        for sc in range(2):
            psi = 1 - (sc % 2)
            c_gg = e("s", lambda s,
                     o2=gg[:, sc * 512:(sc + 1) * 512],
                     i_=self.ps[psi][:, 0:512]:
                     s.activation(out=o2, in_=i_, func=AF.Derivative_Gelu),
                     waits=[c_a1t], inc=True)
            self.ps_rd[psi] = c_gg
        for sc in range(2):
            for hh in range(2):
                for hc in range(2):
                    wt = [c_a1f, c_mem[1]]
                    if sc == 0 and hh == 0 and hc == 0:
                        wt.append(self.ps_rd[3])
                    e("t", lambda t,
                      o2=self.ps[3][:, sc * 128 + hh * 64: sc * 128 + (hh + 1) * 64],
                      l=a1f[:, (hh * 2 + hc) * 256 + sc * 128:
                            (hh * 2 + hc) * 256 + (sc + 1) * 128],
                      r=w1s[:, hc * 64:(hc + 1) * 64],
                      st=(hc == 0), sp=(hc == 1):
                      t.matmul(o2, l, r, start=st, stop=sp),
                      waits=wt,
                      inc=(True if (sc == 1 and hh == 1 and hc == 1) else None))
        cmm = self.tok(self.sem_of["t"])
        c_a2 = e("s", lambda s, o2=a2[:, qs:qs + 256], pp=self.ps[3][:, 0:256]:
                 s.activation(out=o2, in_=pp, func=AF.Copy),
                 waits=[cmm, self.oct_a2_war], inc=True)
        c_sq = e("s", lambda s, o2=a1f[:, 0:256], pp=self.ps[3][:, 0:256]:
                 s.activation(out=o2, in_=pp, func=AF.Square),
                 waits=[c_a2], inc=True)
        self.ps_rd[3] = c_sq
        return {"c_a1t": c_a1t, "c_gg": c_gg, "c_a1f": c_a1f, "c_a2": c_a2,
                "c_sq": c_sq, "cmm_a2": cmm}

    def mem_oct_B(self, p, o, st_, c_kt, c_vn, c_at):
        e = self.emit
        par = o % 2
        kt = self.stage_bf[:, 10240:12288]
        vt = self.stage_bf[:, 12288:14336]
        a2 = self.stage_bf[:, 14336:16384]
        at = self.stage_bf[:, 24576:25088]
        tmp = self.stage_bf[:, 18432 + par * 1024: 19456 + par * 1024]
        rmc = self.cols[:, 80:84]
        dyc = self.cols[:, 84:88]
        qs = o * 256
        c_r = e("v", lambda v, o2=rmc.unsqueeze(2),
                i_=tmp[:, 0:256].rearrange("p (g w) -> p g w", w=64):
                v.reduce_sum(out=o2, in_=i_, axis=AX.X),
                waits=[st_["c_sq"]], inc=True)
        c_r = e("s", lambda s, o2=rmc: s.activation(out=o2, in_=o2, func=AF.Ln,
                                                    bias=1e-8, scale=1.0 / HD),
                waits=[c_r], inc=True)
        c_rr = e("s", lambda s, o2=rmc: s.activation(out=o2, in_=o2,
                                                     func=AF.Exp, scale=-0.5),
                 waits=[c_r], inc=True)
        rrB = rmc.unsqueeze(2).broadcast_to([128, 4, 64])
        a2q4 = a2[:, qs:qs + 256].rearrange("p (g w) -> p g w", w=64)
        c_hn = e("v", lambda v: v.tensor_mul(out=a2q4, in0=a2q4, in1=rrB),
                 waits=[c_rr, st_["c_a2"]], inc=True)
        vq = vt[:, qs:qs + 256]
        c_dp = e("v", lambda v, o2=vq, a=a2[:, qs:qs + 256]:
                 v.tensor_add(out=o2, in0=o2, in1=a),
                 waits=[c_hn, c_vn], inc=True)
        thv = at.rearrange("p (c w) -> p c w", w=32)[:, o * 2:o * 2 + 2,
                                                    16 + 2 * p:18 + 2 * p]
        thB = thv.unsqueeze(3).broadcast_to([128, 2, 2, 64])
        vqh = vq.rearrange("p (c h w) -> p c h w", h=2, w=64)
        c_dp = e("v", lambda v: v.tensor_mul(out=vqh, in0=vqh, in1=thB),
                 waits=[c_dp, c_at], inc=True)
        c_dy = e("v", lambda v, o2=tmp[:, 256:512], a=vq,
                 b=a2[:, qs:qs + 256]:
                 v.tensor_mul(out=o2, in0=a, in1=b),
                 waits=[c_dp], inc=True)
        c_dy = e("v", lambda v, o2=dyc,
                 i_=tmp[:, 256:512].rearrange("p (g w) -> p g w", w=64):
                 v.reduce_sum(out=o2.unsqueeze(2), in_=i_, axis=AX.X),
                 waits=[c_dy], inc=True)
        dyB = dyc.unsqueeze(2).broadcast_to([128, 4, 64])
        c_t3 = e("v", lambda v, o2=tmp[:, 512:768].rearrange(
            "p (g w) -> p g w", w=64), a=a2q4:
            v.scalar_tensor_tensor(out=o2, in0=a, scalar=1.0 / HD, in1=dyB,
                                   op0=ALU.mult, op1=ALU.mult),
            waits=[c_dy], inc=True)
        c_dh = e("v", lambda v, o2=vq, b=tmp[:, 512:768]:
                 v.tensor_tensor(out=o2, in0=o2, in1=b, op=ALU.subtract),
                 waits=[c_t3], inc=True)
        vq3 = vq.rearrange("p (g w) -> p g w", w=64)
        c_dh = e("v", lambda v: v.scalar_tensor_tensor(
            out=vq3, in0=vq3, scalar=0.02, in1=rrB, op0=ALU.mult,
            op1=ALU.mult), waits=[c_dh, c_rr], inc=True)
        return c_dh

    def mem_oct_Cfg(self, p, o, c_dh, c_mem):
        e = self.emit
        par = o % 2
        vt = self.stage_bf[:, 12288:14336]
        gg = self.stage_bf[:, 20480 + par * 1024: 21504 + par * 1024]
        dh0 = self.stage_bf[:, 22528 + par * 1024: 23552 + par * 1024]
        dhf = self.stage_bf[:, 31232 + par * 256: 31488 + par * 256]
        w1bd = self.stage_bf[:, 27648:28160]
        ctl = None
        for sc in range(2):
            cch = o * 2 + sc
            wt = [c_dh, self.ps_rd[2] if sc == 0 else None,
                  self.oct_dhf_war[par] if sc == 0 else None]
            ctl = e("t", lambda t, o2=self.ps_bf[2][:, sc * 256:sc * 256 + 128],
                    s=vt[:, cch * 128:(cch + 1) * 128]:
                    t.transpose(out=o2, in_=s, identity=self.ident),
                    waits=wt, inc=(sc == 1))
        c_dhf = e("v", lambda v, o2=dhf.rearrange("p (c f) -> p c f", f=128),
                  pp=self.ps_bf[2][:, 0:512].rearrange(
                      "p (c f) -> p c f", f=256)[:, :, 0:128]:
                  v.tensor_copy(out=o2, in_=pp), waits=[ctl], inc=True)
        self.ps_rd[2] = c_dhf
        c_dh0 = None
        for sc in range(2):
            psi = 1 - (sc % 2)
            psx = self.ps[psi]
            e("t", lambda t, o2=psx[:, 0:512],
              l=dhf[:, sc * 128:(sc + 1) * 128], r=w1bd:
              t.matmul(o2, l, r, start=True, stop=True),
              waits=[c_dhf, c_mem[2], self.ps_rd[psi],
                     self.oct_dh0_war[par] if sc == 0 else None],
              inc=True)
            cmm2 = self.tok(self.sem_of["t"])
            c_dh0 = e("v", lambda v,
                      o2=dh0[:, sc * 512:(sc + 1) * 512],
                      pp=psx[:, 0:512], g2=gg[:, sc * 512:(sc + 1) * 512]:
                      v.tensor_tensor(out=o2, in0=pp, in1=g2, op=ALU.mult),
                      waits=[cmm2, c_dh0], inc=True)
            self.ps_rd[psi] = c_dh0
        self.oct_gg_war[par] = c_dh0
        return c_dh0

    def mem_oct_Ch(self, p, o, st_, c_dh, c_dh0, c_kt):
        e = self.emit
        par = o % 2
        kt = self.stage_bf[:, 10240:12288]
        vt = self.stage_bf[:, 12288:14336]
        a1t = self.stage_bf[:, 16384 + par * 1024: 17408 + par * 1024]
        dh0 = self.stage_bf[:, 22528 + par * 1024: 23552 + par * 1024]
        for sc in range(2):
            cch = o * 2 + sc
            for hh in range(2):
                for hc in range(2):
                    last = (sc == 1 and hh == 1 and hc == 1)
                    e("t", lambda t,
                      o2=self.ps[2][:, 512 + (hh * 2 + hc) * 64:
                                    512 + (hh * 2 + hc + 1) * 64],
                      l=a1t[:, sc * 512 + hh * 256 + hc * 128:
                            sc * 512 + hh * 256 + (hc + 1) * 128],
                      r=vt[:, cch * 128 + hh * 64: cch * 128 + (hh + 1) * 64],
                      st=(o == 0 and sc == 0 and hh == 0 and hc == 0),
                      sp=(o == 7 and sc == 1 and hh == 1 and hc == 1):
                      t.matmul(o2, l, r, start=st, stop=sp),
                      waits=[st_["c_a1t"], c_dh] + (
                          [self.g1_ps_war] if (o == 0 and sc == 0 and
                                               hh == 0 and hc == 0) else []),
                      inc=(True if last else None))
            for hh in range(2):
                wt = [c_dh0, c_kt]
                if o == 0 and sc == 0:
                    wt.append(self.g0_ps_war)
                e("t", lambda t,
                  o2=self.ps[3][0:64, 512 + hh * 256: 768 + hh * 256],
                  l=kt[:, cch * 128 + hh * 64: cch * 128 + (hh + 1) * 64],
                  r=dh0[:, sc * 512 + hh * 256: sc * 512 + (hh + 1) * 256],
                  st=(o == 0 and sc == 0 and hh == 0),
                  sp=(o == 7 and sc == 1 and hh == 1):
                  t.matmul(o2, l, r, start=st, stop=sp),
                  waits=wt, inc=(o == 7 and sc == 1 and hh == 1))
        c_gmm = self.tok(self.sem_of["t"])
        self.oct_a1t_war[par] = c_gmm
        self.oct_a1f_war[par] = c_dh
        self.oct_a2_war = c_gmm
        self.oct_dhf_war[par] = c_gmm
        self.oct_dh0_war[par] = c_gmm
        return c_gmm

    def grad_pairs(self, c_wk, c_wv, c_mem, c_xo, c_rn, c_knw, c_at, c_am):
        e = self.emit
        wk = self.arena_bf[:, 0:8192]
        wv = self.stage_bf[:, 0:8192]
        kf = self.stage_bf[:, 8192:10240]
        kt = self.stage_bf[:, 10240:12288]
        vt = self.stage_bf[:, 12288:14336]
        gg = self.stage_bf[:, 20480:22528]
        c_nw0 = c_nw1 = None
        c_kf_rd = None
        c_kt_war = None
        c_vt_war = None
        self.oct_a1t_war = {0: None, 1: None}
        self.oct_a1f_war = {0: None, 1: None}
        self.oct_gg_war = {0: None, 1: None}
        self.oct_a2_war = None
        self.oct_dhf_war = {0: None, 1: None}
        self.oct_dh0_war = {0: None, 1: None}
        self.g0_ps_war = self.ps_rd[3]
        self.g1_ps_war = self.ps_rd[2]
        self.g0sb_war = None
        npair = 1 if (_TRUNC[0] or "").startswith("b2") else NPAIR
        stepmax = 99
        tr = _TRUNC[0] or ""
        if tr.startswith("b2s"):
            stepmax = int(tr[3:])
        for p in range(npair):
            csqs = []
            ck = None
            for tt in range(4):
                pb = self.ps[0][:, (tt % 2) * 512:(tt % 2) * 512 + 512]
                for dd in range(DC):
                    wt = [c_wk, c_xo, self.ps_rd[0], c_kf_rd] if dd == 0 else []
                    e("t", lambda t, o=pb,
                      l=wk[:, dd * 1024 + p * 128: dd * 1024 + (p + 1) * 128],
                      r=self.xbf[:, dd * S + tt * 512: dd * S + tt * 512 + 512],
                      st=(dd == 0), sp=(dd == DC - 1):
                      t.matmul(o, l, r, start=st, stop=sp),
                      waits=wt, inc=(True if dd == DC - 1 else None))
                cmm = self.tok(self.sem_of["t"])
                ck = e("v", lambda v, o=kf[:, tt * 512:(tt + 1) * 512], pp=pb,
                       rn=self.bcast[:, tt * 512:(tt + 1) * 512]:
                       v.scalar_tensor_tensor(out=o, in0=pp, scalar=1.0,
                                              in1=rn, op0=ALU.mult,
                                              op1=ALU.mult),
                       waits=[cmm, ck] + (list(c_rn) if tt == 0 else []),
                       inc=True)
                self.ps_rd[0] = ck
                csqs.append(e("v", lambda v, o=gg[:, tt * 512:(tt + 1) * 512],
                              i_=kf[:, tt * 512:(tt + 1) * 512]:
                              v.tensor_mul(out=o, in0=i_, in1=i_),
                              waits=[ck] + ([self.oct_gg_war[0],
                                             self.oct_gg_war[1]]
                                            if tt == 0 else []),
                              inc=True))
            self.bcast0_war = ck
            if stepmax < 1:
                return None, None
            cnk = None
            for half in range(2):
                for sub in range(2):
                    wt = [csqs[half * 2 + sub], self.c_consts]
                    if sub == 0:
                        wt.append(self.ps_rd[1])
                    e("t", lambda t, o=self.ps[1][:, sub * 512:(sub + 1) * 512],
                      l=self.B64,
                      r=gg[:, half * 1024 + sub * 512: half * 1024 + (sub + 1) * 512]:
                      t.matmul(o, l, r, start=True, stop=True),
                      waits=wt, inc=(sub == 1))
                cbm = self.tok(self.sem_of["t"])
                cl = e("s", lambda s, h2=half: s.activation(
                    out=self.bcast[:, 2048 + h2 * 1024: 2048 + (h2 + 1) * 1024],
                    in_=self.ps[1][:, 0:1024],
                    func=AF.Ln, bias=1e-6, scale=1.0 / HD),
                    waits=[cbm, self.rk_war], inc=True)
                self.ps_rd[1] = cl
                cr = e("s", lambda s, h2=half: s.activation(
                    out=self.bcast[:, 2048 + h2 * 1024: 2048 + (h2 + 1) * 1024],
                    in_=self.bcast[:, 2048 + h2 * 1024: 2048 + (h2 + 1) * 1024],
                    func=AF.Exp, scale=-0.5), waits=[cl], inc=True)
                cnk = e("v", lambda v, o=kf[:, half * 1024:(half + 1) * 1024],
                        w=self.cols[:, 32:33],
                        r=self.bcast[:, 2048 + half * 1024: 2048 + (half + 1) * 1024]:
                        v.scalar_tensor_tensor(out=o, in0=o, scalar=w, in1=r,
                                               op0=ALU.mult, op1=ALU.mult),
                        waits=[cr, c_knw, cnk], inc=True)
                self.rk_war = cnk
            if stepmax < 2:
                return None, None
            vfst = gg
            cvf = [None] * 4
            for tt in range(4):
                pb = self.ps[3][:, (tt % 2) * 512:(tt % 2) * 512 + 512]
                for dd in range(DC):
                    if dd == 0:
                        if tt == 0:
                            wt = [c_wv, c_xo, self.ps_rd[3]]
                        elif tt == 1:
                            wt = [self.g0_ps_war]
                        else:
                            wt = [cvf[tt - 2]]
                    else:
                        wt = []
                    e("t", lambda t, o=pb,
                      l=wv[:, dd * 1024 + p * 128: dd * 1024 + (p + 1) * 128],
                      r=self.xbf[:, dd * S + tt * 512: dd * S + tt * 512 + 512],
                      st=(dd == 0), sp=(dd == DC - 1):
                      t.matmul(o, l, r, start=st, stop=sp),
                      waits=wt, inc=(True if dd == DC - 1 else None))
                cmm = self.tok(self.sem_of["t"])
                cvf[tt] = e("v", lambda v, o=vfst[:, tt * 512:(tt + 1) * 512],
                            pp=pb: v.tensor_copy(out=o, in_=pp),
                            waits=[cmm, cvf[tt - 1] if tt else None], inc=True)
                self.ps_rd[3] = cvf[tt]
            self.g0_ps_war = cvf[3]
            cv2 = None
            for g in range(4):
                tl = None
                for i in range(4):
                    cch = g * 4 + i
                    wt = [cvf[g], self.ps_rd[2] if i == 0 else None]
                    tl = e("t", lambda t,
                           o=self.ps_bf[2][:, i * 256:i * 256 + 128],
                           s=vfst[:, cch * 128:(cch + 1) * 128]:
                           t.transpose(out=o, in_=s, identity=self.ident),
                           waits=wt, inc=(i == 3))
                cv2 = e("v", lambda v,
                        o=vt.rearrange("p (c f) -> p c f", f=128)[:, g * 4:(g + 1) * 4, :],
                        pp=self.ps_bf[2][:, 0:1024].rearrange(
                            "p (c f) -> p c f", f=256)[:, :, 0:128]:
                        v.tensor_copy(out=o, in_=pp),
                        waits=[tl, cv2, c_vt_war if g == 0 else None],
                        inc=True)
                self.ps_rd[2] = cv2
            rnview = self.cols[:, 0:16].unsqueeze(2).broadcast_to([128, 16, 128])
            cvn = e("v", lambda v,
                    o=vt.rearrange("p (c f) -> p c f", f=128):
                    v.tensor_mul(out=o, in0=o, in1=rnview),
                    waits=[cv2, self.c_rcols_tok], inc=True)

            ctk = None
            for g in range(4):
                tl = None
                for i in range(4):
                    cch = g * 4 + i
                    wt = [cnk, self.ps_rd[2] if i == 0 else None,
                          c_kt_war if (g == 0 and i == 0) else None]
                    tl = e("t", lambda t,
                           o=self.ps_bf[2][:, i * 256:i * 256 + 128],
                           s=kf[:, cch * 128:(cch + 1) * 128]:
                           t.transpose(out=o, in_=s, identity=self.ident),
                           waits=wt, inc=(i == 3))
                ctk = e("v", lambda v,
                        o=kt.rearrange("p (c f) -> p c f", f=128)[:, g * 4:(g + 1) * 4, :],
                        pp=self.ps_bf[2][:, 0:1024].rearrange(
                            "p (c f) -> p c f", f=256)[:, :, 0:128]:
                        v.tensor_copy(out=o, in_=pp), waits=[tl, ctk], inc=True)
                self.ps_rd[2] = ctk
                cvn = e("v", lambda v, o=vt[:, g * 512:(g + 1) * 512],
                        b=kt[:, g * 512:(g + 1) * 512]:
                        v.tensor_tensor(out=o, in0=b, in1=o, op=ALU.subtract),
                        waits=[ctk, cvn], inc=True)
            if stepmax < 3:
                return None, None
            if stepmax < 4:
                return None, None
            nocts = 8
            if (_TRUNC[0] or "").startswith("b2q"):
                nocts = 2 * (int(_TRUNC[0][3:]) + 1)
            sts, chs, dh0s = {}, {}, {}
            cg = None
            for o in range(nocts):
                sts[o] = self.mem_oct_A(p, o, cnk, c_mem)
                if o >= 1:
                    dh0s[o - 1] = self.mem_oct_Cfg(p, o - 1, chs[o - 1], c_mem)
                chs[o] = self.mem_oct_B(p, o, sts[o], ctk, cvn, c_at)
                if o >= 1:
                    cg = self.mem_oct_Ch(p, o - 1, sts[o - 1], chs[o - 1],
                                         dh0s[o - 1], ctk)
            ol = nocts - 1
            dh0s[ol] = self.mem_oct_Cfg(p, ol, chs[ol], c_mem)
            cg = self.mem_oct_Ch(p, ol, sts[ol], chs[ol], dh0s[ol], ctk)
            if (_TRUNC[0] or "").startswith("b2q"):
                return None, None
            if stepmax < 8:
                return None, None
            c_nw0, c_nw1 = self.clip_pair(p, cg, c_am, c_mem)
            c_kf_rd = cg
            c_kt_war = cg
            c_vt_war = cg
        return c_nw0, c_nw1

    rk_war = None

    def clip_pair(self, p, c_gmm, c_am, c_mem):
        e = self.emit
        w0bd = self.arena_bf[:, 8192:8704]
        w1s = self.arena_bf[:, 8704:8832]
        nw0 = self.arena_bf[:, 8832:10880]
        nw1 = self.stage_bf[:, 25088:27136]
        gt0 = self.stage[:, 14592:14848]
        gt1 = self.stage[:, 14848:15104]
        tmp = self.stage[:, 15104:15360]
        fam = self.cols[:, 72:80]
        cga = e("v", lambda v, o=gt0[0:64, :], pp=self.ps[3][0:64, 512:768]:
                v.tensor_copy(out=o, in_=pp),
                waits=[c_gmm, self.gt_war], inc=True)
        cga = e("v", lambda v, o=gt0[64:128, :], pp=self.ps[3][0:64, 768:1024]:
                v.tensor_copy(out=o, in_=pp), waits=[cga], inc=True)
        cgb = e("v", lambda v, o=gt1, pp=self.ps[2][:, 512:768]:
                v.tensor_copy(out=o, in_=pp), waits=[cga], inc=True)
        self.g0_ps_war = cga
        self.g1_ps_war = cgb
        self.ps_rd[2] = cgb
        csq = e("v", lambda v, o=tmp, a=gt0: v.tensor_mul(out=o, in0=a, in1=a),
                waits=[cga], inc=True)
        cc0 = e("v", lambda v, o=self.cols[:, 88:89].unsqueeze(2),
                i_=tmp.rearrange("p (g w) -> p g w", g=1):
                v.reduce_sum(out=o, in_=i_, axis=AX.X), waits=[csq], inc=True)
        csq = e("v", lambda v, o=tmp, a=gt1: v.tensor_mul(out=o, in0=a, in1=a),
                waits=[cgb, cc0], inc=True)
        cc1 = e("v", lambda v, o=self.cols[:, 89:91].unsqueeze(2),
                i_=tmp.rearrange("p (g w) -> p g w", g=2):
                v.reduce_sum(out=o, in_=i_, axis=AX.X), waits=[csq], inc=True)
        ccb = e("v", lambda v, o=self.consts[:, 712:715],
                i_=self.cols[:, 88:91]: v.tensor_copy(out=o, in_=i_),
                waits=[cc0, cc1], inc=True)
        scol = self.consts[:, 712:715]
        for i, (lh, rc) in enumerate((
                (self.B64[:, 0:1], 0), (self.B64[:, 64:65], 0),
                (self.ones128[:, 0:1], 1), (self.ones128[:, 0:1], 2))):
            wt = [ccb, self.c_consts]
            if i == 0:
                wt.append(self.ps_rd[3])
            e("t", lambda t, o=self.ps[3][0:1, i:i + 1], l=lh,
              r=scol[:, rc:rc + 1]:
              t.matmul(o, l, r, start=True, stop=True),
              waits=wt, inc=(i == 3))
        cmm = self.tok(self.sem_of["t"])
        sqr = self.stage[0:1, 15488:15504]
        c_q = e("v", lambda v, o=sqr[:, 0:4], pp=self.ps[3][0:1, 0:4]:
                v.tensor_copy(out=o, in_=pp), waits=[cmm], inc=True)
        self.ps_rd[3] = c_q
        c_q = e("v", lambda v, o=sqr[:, 4:6],
                a=sqr.rearrange("o (g w) -> o g w", w=2)[:, 0:1, :].squeeze(1),
                b=sqr.rearrange("o (g w) -> o g w", w=2)[:, 1:2, :].squeeze(1):
                v.tensor_add(out=o, in0=a, in1=b), waits=[c_q], inc=True)
        c_q = e("s", lambda s, o=sqr[:, 4:6]: s.activation(
            out=o, in_=o, func=AF.Sqrt), waits=[c_q], inc=True)
        c_q = e("v", lambda v, o=sqr[:, 4:6]: v.tensor_scalar(
            out=o, in0=o, scalar1=1e-6, scalar2=None, op0=ALU.add),
            waits=[c_q], inc=True)
        c_q = e("v", lambda v, o=sqr[:, 4:6]: v.reciprocal(out=o, in_=o),
                waits=[c_q], inc=True)
        c_q = e("v", lambda v, o=sqr[:, 4:6]: v.tensor_scalar(
            out=o, in0=o, scalar1=-10.0, scalar2=-1.0, op0=ALU.mult,
            op1=ALU.max), waits=[c_q], inc=True)
        cfbf = self.consts[0:1, 720:722]
        c_cf = e("v", lambda v, o=cfbf, i_=sqr[:, 4:6]:
                 v.tensor_copy(out=o, in_=i_), waits=[c_q], inc=True)
        for hh in range(2):
            wt = [c_cf, self.c_consts]
            if hh == 0:
                wt.append(self.ps_rd[3])
            e("t", lambda t, o=self.ps[3][:, 8:9],
              l=(self.B64[0:1, 0:128] if hh == 0 else
                 self.consts[0:1, 752:880]),
              r=cfbf[:, hh:hh + 1], st=(hh == 0), sp=(hh == 1):
              t.matmul(o, l, r, start=st, stop=sp),
              waits=wt, inc=(hh == 1))
        ambf = self.consts[0:1, 680:712]
        for i2, cell in enumerate((cfbf[:, 0:1], cfbf[:, 1:2],
                                   ambf[:, 2 * p:2 * p + 1],
                                   ambf[:, 2 * p + 1:2 * p + 2])):
            e("t", lambda t, o=self.ps[3][:, 12 + i2:13 + i2],
              l=self.ones128[0:1, :], r=cell:
              t.matmul(o, l, r, start=True, stop=True),
              waits=([c_cf] if i2 == 0 else []), inc=(i2 == 3))
        cmm2 = self.tok(self.sem_of["t"])
        ncf = self.cols[:, 91:92]
        c_cc = e("v", lambda v, o=ncf, pp=self.ps[3][:, 8:9]:
                 v.tensor_copy(out=o, in_=pp), waits=[cmm2], inc=True)
        ncfu = self.cols[:, 92:94]
        c_cu = e("v", lambda v, o=ncfu, pp=self.ps[3][:, 12:14]:
                 v.tensor_copy(out=o, in_=pp), waits=[c_cc], inc=True)
        famu = self.cols[:, 94:96]
        c_fu = e("v", lambda v, o=famu, pp=self.ps[3][:, 14:16]:
                 v.tensor_scalar(out=o, in0=pp, scalar1=-1.0, scalar2=1.0,
                                 op0=ALU.mult, op1=ALU.add),
                 waits=[c_cu], inc=True)
        self.ps_rd[3] = c_fu
        c_tm = e("v", lambda v, o=tmp, i_=gt0, cc=ncf:
                 v.tensor_scalar(out=o, in0=i_, scalar1=cc, scalar2=None,
                                 op0=ALU.mult), waits=[c_cc], inc=True)
        c_nw0 = None
        for hh in range(2):
            c_nw0 = e("v", lambda v,
                      o=nw0[hh * 64:(hh + 1) * 64, p * 256:(p + 1) * 256],
                      w=w0bd[hh * 64:(hh + 1) * 64, hh * 256:(hh + 1) * 256],
                      f=fam[hh * 64:(hh + 1) * 64, p:p + 1],
                      tm=tmp[hh * 64:(hh + 1) * 64, :]:
                      v.scalar_tensor_tensor(out=o, in0=w, scalar=f, in1=tm,
                                             op0=ALU.mult, op1=ALU.add),
                      waits=[c_tm, c_am, c_mem[0], c_nw0], inc=True)
        tmpw2 = self.stage[:, 15360:15488]
        c_nw1 = None
        for h2 in range(2):
            c_tm = e("v", lambda v,
                     o=tmp[:, h2 * 128:(h2 + 1) * 128],
                     i_=gt1[:, h2 * 128:(h2 + 1) * 128], cc=ncfu[:, h2:h2 + 1]:
                     v.tensor_scalar(out=o, in0=i_, scalar1=cc, scalar2=None,
                                     op0=ALU.mult),
                     waits=[c_nw0, cgb, c_cu], inc=True)
            c_f2 = e("v", lambda v, o=tmpw2, i_=w1s, f=famu[:, h2:h2 + 1]:
                     v.tensor_scalar(out=o, in0=i_, scalar1=f, scalar2=None,
                                     op0=ALU.mult),
                     waits=[c_tm, c_fu, c_mem[1], self.f2_war], inc=True)
            c_nw1 = e("v", lambda v,
                      o=nw1[:, p * 256 + h2 * 128: p * 256 + (h2 + 1) * 128],
                      tm=tmp[:, h2 * 128:(h2 + 1) * 128], w2=tmpw2:
                      v.tensor_add(out=o, in0=tm, in1=w2),
                      waits=[c_f2, self.nw1_war, c_nw1], inc=True)
            self.f2_war = c_nw1
        self.gt_war = c_nw1
        self.f2_war = c_nw1
        if p == 0:
            self.tap("gt0", gt0, [c_nw0])
            self.tap("gt1", gt1, [c_nw1])
        return c_nw0, c_nw1

    gt_war = None
    vf_war = None
    rsq_war = None
    f2_war = None
    nw1_war = None
    nw1_war = None

    def retrieve_pairs(self, c_nw0, c_nw1, c_xo, c_rn, c_wr, wrcol, c_qnw,
                       barrier):
        e = self.emit
        wq = self.arena_bf[:, 0:8192]
        qfb = [self.stage_bf[:, 8192:10240], self.stage_bf[:, 4608:6656]]
        nw0 = self.arena_bf[:, 8832:10880]
        nw1 = self.stage_bf[:, 25088:27136]
        h0f = self.stage_bf[:, 16384:24576]
        sqr = self.stage_bf[:, 20480:22528] if False else None
        c_wq = self.load_w1024("L0_attn_b_W_Q", wq, wrcol, c_wr,
                               c_war=c_nw1)
        c_qf_rd = None
        c_h0_war = [None] * 4
        tmpsq = self.stage_bf[:, 2048:4096]
        self.qf_ck = None
        self.qf_sq = {}
        self.qrms_war = None
        self.sq2_war = [None, None]
        self.ps_rd_1h = [self.ps_rd[1], self.ps_rd[1]]

        def emit_qf(p2, tt):
            qf2 = qfb[p2 % 2]
            pb = self.ps[0][:, (tt % 2) * 512:(tt % 2) * 512 + 512]
            for dd in range(DC):
                wt = [c_wq, c_xo, self.ps_rd[0], c_qf_rd] if dd == 0 else []
                e("t", lambda t, o=pb,
                  l=wq[:, dd * 1024 + p2 * 128: dd * 1024 + (p2 + 1) * 128],
                  r=self.xbf[:, dd * S + tt * 512: dd * S + tt * 512 + 512],
                  st=(dd == 0), sp=(dd == DC - 1):
                  t.matmul(o, l, r, start=st, stop=sp),
                  waits=wt, inc=(True if dd == DC - 1 else None))
            cmm = self.tok(self.sem_of["t"])
            ck = e("v", lambda v, o=qf2[:, tt * 512:(tt + 1) * 512], pp=pb,
                   rn=self.bcast[:, tt * 512:(tt + 1) * 512]:
                   v.scalar_tensor_tensor(out=o, in0=pp, scalar=1.0,
                                          in1=rn, op0=ALU.mult,
                                          op1=ALU.mult),
                   waits=[cmm, self.qf_ck] + list(c_rn), inc=True)
            self.qf_ck = ck
            self.ps_rd[0] = ck
            self.qf_sq[(p2, tt)] = e(
                "v", lambda v, o=tmpsq[:, tt * 512:(tt + 1) * 512],
                i_=qf2[:, tt * 512:(tt + 1) * 512]:
                v.tensor_mul(out=o, in0=i_, in1=i_),
                waits=[ck, self.qrms_war], inc=True)
            return ck

        def emit_fin(p2, qf2, cnk2, tt, p1q, cev, csq2):
            wt = [csq2, self.ps_rd_1h[tt % 2]]
            e("t", lambda t, o=self.ps[1][:, (tt % 2) * 512:
                                          (tt % 2) * 512 + 512],
              l=self.B64,
              r=self.stage_bf[:, 1024 + (tt % 2) * 512:
                              1536 + (tt % 2) * 512]:
              t.matmul(o, l, r, start=True, stop=True),
              waits=wt, inc=True)
            self.sq2_war[tt % 2] = self.tok(self.sem_of["t"])
            cbm = self.tok(self.sem_of["t"])
            r2B = self.bcast[:, 2048 + (tt % 2) * 512:
                             2048 + (tt % 2) * 512 + 512]
            cl = e("s", lambda s, o=r2B,
                   i_=self.ps[1][:, (tt % 2) * 512:(tt % 2) * 512 + 512]:
                   s.activation(out=o, in_=i_, func=AF.Ln, bias=1e-8,
                                scale=1.0 / HD),
                   waits=[cbm, self.rk_war], inc=True)
            self.ps_rd_1h[tt % 2] = cl
            self.ps_rd[1] = cl
            cr = e("s", lambda s, o=r2B: s.activation(
                out=o, in_=o, func=AF.Exp, scale=-0.5),
                waits=[cl], inc=True)
            tmpo = self.stage_bf[:, 4096:4608]
            co = e("v", lambda v, o=tmpo, a=p1q, b=r2B:
                   v.tensor_mul(out=o, in0=a, in1=b),
                   waits=[cr, self.tmpo_war], inc=True)
            co = e("v", lambda v, o=tmpo, b=qf2[:, tt * 512:(tt + 1) * 512]:
                   v.tensor_add(out=o, in0=o, in1=b),
                   waits=[co, cnk2], inc=True)
            cx = e("v", lambda v,
                   o=self.xf[:, p2 * S + tt * 512: p2 * S + tt * 512 + 512],
                   b=tmpo: v.tensor_add(out=o, in0=o, in1=b),
                   waits=[co, self.c_x_all[p2]], inc=True)
            self.c_x_all[p2] = cx
            self.tmpo_war = cx
            self.p1q_war[tt % 2] = csq2
            self.rk_war = co
            return cev, cr, p1q, r2B

        pend = []
        for p in range(NPAIR):
            qf = qfb[p % 2]
            for tt in range(4):
                if (p, tt) not in self.qf_sq:
                    emit_qf(p, tt)
            ck = self.qf_ck
            cnk = None
            for half in range(2):
                for sub in range(2):
                    wt = [self.qf_sq[(p, half * 2 + sub)], self.c_consts]
                    if sub == 0:
                        wt.append(self.ps_rd[1])
                    e("t", lambda t, o=self.ps[1][:, sub * 512:(sub + 1) * 512],
                      l=self.B64,
                      r=tmpsq[:, half * 1024 + sub * 512: half * 1024 + (sub + 1) * 512]:
                      t.matmul(o, l, r, start=True, stop=True),
                      waits=wt, inc=(sub == 1))
                cbm = self.tok(self.sem_of["t"])
                cl = e("s", lambda s, h2=half: s.activation(
                    out=self.bcast[:, 2048 + h2 * 1024: 2048 + (h2 + 1) * 1024],
                    in_=self.ps[1][:, 0:1024],
                    func=AF.Ln, bias=1e-6, scale=1.0 / HD),
                    waits=[cbm, self.rk_war], inc=True)
                self.ps_rd[1] = cl
                cr = e("s", lambda s, h2=half: s.activation(
                    out=self.bcast[:, 2048 + h2 * 1024: 2048 + (h2 + 1) * 1024],
                    in_=self.bcast[:, 2048 + h2 * 1024: 2048 + (h2 + 1) * 1024],
                    func=AF.Exp, scale=-0.5), waits=[cl], inc=True)
                cnk = e("v", lambda v, o=qf[:, half * 1024:(half + 1) * 1024],
                        w=self.cols[:, 33:34],
                        r=self.bcast[:, 2048 + half * 1024: 2048 + (half + 1) * 1024]:
                        v.scalar_tensor_tensor(out=o, in0=o, scalar=w, in1=r,
                                               op0=ALU.mult, op1=ALU.mult),
                        waits=[cr, c_qnw, cnk], inc=True)
                self.rk_war = cnk
            self.qrms_war = self.tok(self.sem_of["t"])
            self.ps_rd_1h = [self.ps_rd[1], self.ps_rd[1]]
            h0ps = [self.ps[2][:, 0:512], self.ps[2][:, 512:1024],
                    self.ps[0][:, 0:512], self.ps[0][:, 512:1024]]
            c_h0 = [None] * 4

            def emit_h0(tt):
                for k in range(4):
                    hh, hc = k // 2, k % 2
                    pb = h0ps[k]
                    if tt == 0:
                        wt = [cnk, c_nw0, c_h0_war[k],
                              self.ps_rd[2] if k == 0 else None,
                              self.ps_rd[0] if k == 2 else None]
                    else:
                        wt = [c_h0[k]]
                    cmmi = e("t", lambda t, o=pb,
                             l=nw0[hh * 64:(hh + 1) * 64,
                                   p * 256 + hc * 128: p * 256 + (hc + 1) * 128],
                             r=qf[hh * 64:(hh + 1) * 64, tt * 512:(tt + 1) * 512]:
                             t.matmul(o, l, r, start=True, stop=True),
                             waits=wt, inc=True)
                    c_h0[k] = e("s", lambda s,
                                o=h0f[:, k * 2048 + tt * 512: k * 2048 + (tt + 1) * 512],
                                i_=pb: s.activation(out=o, in_=i_, func=AF.Gelu),
                                waits=[cmmi, c_h0[k]], inc=True)
                self.ps_rd[2] = c_h0[1]
                self.ps_rd[0] = c_h0[3]

            def emit_pre1(tt):
                pb = self.ps[3][:, (tt % 2) * 512:(tt % 2) * 512 + 512]
                for h2 in range(2):
                    for hc in range(2):
                        wt = [c_h0[h2 * 2 + hc], c_nw1]
                        if h2 == 0 and hc == 0:
                            wt.append(self.ps_rd[3])
                        e("t", lambda t, o=pb[h2 * 64:(h2 + 1) * 64, :],
                          l=nw1[:, p * 256 + (h2 * 2 + hc) * 64:
                                p * 256 + (h2 * 2 + hc + 1) * 64],
                          r=h0f[:, (h2 * 2 + hc) * 2048 + tt * 512:
                                (h2 * 2 + hc) * 2048 + (tt + 1) * 512],
                          st=(hc == 0), sp=(hc == 1):
                          t.matmul(o, l, r, start=st, stop=sp),
                          waits=wt, inc=(True if (h2 == 1 and hc == 1) else None))
                cmm = self.tok(self.sem_of["t"])
                p1q = self.stage_bf[:, (tt % 2) * 512:(tt % 2) * 512 + 512]
                cev = e("v", lambda v, o=p1q, pp=pb: v.tensor_copy(out=o, in_=pp),
                        waits=[cmm, self.p1q_war[tt % 2]], inc=True)
                self.ps_rd[3] = cev
                sq2 = self.stage_bf[:, 1024 + (tt % 2) * 512:
                                    1536 + (tt % 2) * 512]
                csq2 = e("v", lambda v, o=sq2, i_=p1q:
                         v.tensor_mul(out=o, in0=i_, in1=i_),
                         waits=[cev, self.sq2_war[tt % 2]], inc=True)
                return pb, p1q, cev, csq2

            if pend:
                emit_fin(*pend[0])
            emit_h0(0)
            if len(pend) > 1:
                emit_fin(*pend[1])
            pend = []
            emit_h0(1)
            emit_h0(2)
            pr = {}
            pr[0] = emit_pre1(0)
            pr[1] = emit_pre1(1)
            emit_h0(3)
            emit_fin(p, qf, cnk, 0, pr[0][1], pr[0][2], pr[0][3])
            emit_fin(p, qf, cnk, 1, pr[1][1], pr[1][2], pr[1][3])
            if p + 1 < NPAIR:
                emit_qf(p + 1, 0)
            pr[2] = emit_pre1(2)
            pr[3] = emit_pre1(3)
            if p + 1 < NPAIR:
                emit_qf(p + 1, 1)
            c_qf_rd = self.tok(self.sem_of["t"])
            for k in range(4):
                c_h0_war[k] = self.tok(self.sem_of["t"])
            self.ret_tmp_war = self.tok(self.sem_of["t"])
            self.bcast0_war = ck
            if p + 1 < NPAIR:
                pend = [(p, qf, cnk, 2, pr[2][1], pr[2][2], pr[2][3]),
                        (p, qf, cnk, 3, pr[3][1], pr[3][2], pr[3][3])]
            else:
                emit_fin(p, qf, cnk, 2, pr[2][1], pr[2][2], pr[2][3])
                _, cr, p1q, r2B = emit_fin(p, qf, cnk, 3, pr[3][1], pr[3][2],
                                           pr[3][3])
                if p == int(os.environ.get("TAPPAIR", NPAIR - 1)):
                    tpq = self.tap("p1q3", p1q, [pr[3][2]])
                    trb = self.tap("r2b3", r2B, [cr])
                    if trb is not None:
                        self.p1q_war[3 % 2] = trb
                        self.rk_war = trb
        self.c_x = self.c_x_all[-1]

    ret_tmp_war = None
    tmpP_war = None
    tmpo_war = None
    ps_rd_1h = None
    sq2_war = None

    def emit_store(self):
        e = self.emit
        for c in range(SC):
            pi = 2 + (c % 2)
            pb = self.ps[pi]
            tlast = None
            for dd in range(DC):
                src = self.xf[:, dd * S + c * 128: dd * S + (c + 1) * 128]
                w = [self.c_xf, self.ps_rd[pi]] if dd == 0 else []
                tlast = e("t", lambda t, o=pb[:, dd * 128:(dd + 1) * 128], s=src:
                          t.transpose(out=o, in_=s, identity=self.ident32),
                          waits=w, inc=(True if dd == DC - 1 else None))
            self.ps_wr[pi] = tlast
            dst = self.arena[:, (c % 2) * 1024:(c % 2) * 1024 + 1024]
            war = self._store_dma[c - 2] if c >= 2 else None
            cv = e("v", lambda v, d=dst, p=pb[:, :]: v.tensor_copy(out=d, in_=p),
                   waits=[tlast, war], inc=True)
            self.ps_rd[pi] = cv
            dma = e("y", lambda y, c=c, d=dst: y.dma_start(
                out=self.out.rearrange("(c p) f -> c p f", p=128)[c], in_=d),
                waits=[cv], inc=self.dsem("st" + str(c % 2)), inck=16)
            self._store_dma.append(dma)
        tailw = []
        for s2m in self._dma_sems.values():
            if self.ctr[s2m] > 0:
                tailw.append((s2m, self.ctr[s2m]))
        e("y", lambda y: y.nop(), waits=tailw)

    _store_dma = None

    def build_full(self):
        self._store_dma = []
        return self.build()


def build_program(dbg=()):
    b = Builder(dbg)
    nc = b.build_full()
    return nc, b


def kernel(**inputs):
    nc, b = build_program()
    xs = np.ascontiguousarray(inputs["L0_attn_a_x"], dtype=np.float32)
    in_maps = []
    for core in range(8):
        m = {}
        for name, _ in IN_SPECS:
            if name == "L0_attn_a_x":
                m[name] = np.ascontiguousarray(xs[core])
            else:
                m[name] = np.ascontiguousarray(inputs[name], dtype=np.float32)
        in_maps.append(m)
    res = run_bass_kernel_spmd(nc, in_maps, core_ids=list(range(8)))
    return np.stack([np.asarray(r["out"]) for r in res.results], axis=0)


# version 29
# speedup vs baseline: 1.0190x; 1.0190x over previous
import os
import numpy as np
import concourse.bass as bass
import concourse.mybir as mybir
from concourse.bass_utils import run_bass_kernel_spmd

F32 = mybir.dt.float32
I32 = mybir.dt.int32
BF16 = mybir.dt.bfloat16
AF = mybir.ActivationFunctionType
ALU = mybir.AluOpType
AX = mybir.AxisListType

S, D, H, HD, HID = 2048, 1024, 16, 64, 256
RANK, DFF = 256, 4096
SC, DC = S // 128, D // 128
NPAIR = H // 2

IN_SPECS = [
    ("L0_attn_a_x", [S, D]), ("L0_attn_a_Wq", [D, D]), ("L0_attn_a_Wk", [D, D]),
    ("L0_attn_a_Wv", [D, D]), ("L0_attn_a_Wo", [D, D]), ("L0_attn_a_gamma", [D]),
    ("L0_attn_a_beta", [D]), ("L0_attn_b_W_K", [D, D]), ("L0_attn_b_W_V", [D, D]),
    ("L0_attn_b_W_Q", [D, D]), ("L0_attn_b_mem_W0", [HD, HID]),
    ("L0_attn_b_mem_W1", [HID, HD]), ("L0_attn_b_k_norm_w", [HD]),
    ("L0_attn_b_q_norm_w", [HD]), ("L0_attn_b_store_norm_w", [D]),
    ("L0_attn_b_retrieve_norm_w", [D]), ("L0_attn_b_alpha_w", [D, H]),
    ("L0_attn_b_alpha_b", [H]), ("L0_attn_b_theta_w", [D, H]),
    ("L0_attn_b_theta_b", [H]), ("L0_attn_b_eta_w", [D, H]), ("L0_attn_b_eta_b", [H]),
    ("L0_ffn_U1", [D, RANK]), ("L0_ffn_V1", [RANK, DFF]), ("L0_ffn_b1", [DFF]),
    ("L0_ffn_U2", [DFF, RANK]), ("L0_ffn_V2", [RANK, D]), ("L0_ffn_b2", [D]),
]

ENGKEYS = ("t", "v", "s", "g", "y")
_TRUNC = [None]


class Builder:
    def __init__(self, dbg=()):
        self.c_x_all = [None] * 8
        nc = bass.Bass("TRN2", target_bir_lowering=False)
        self.nc = nc
        self.dbg = list(dbg)
        self.inp = {}
        for n, sh in IN_SPECS:
            self.inp[n] = nc.dram_tensor(n, sh, F32, kind="ExternalInput").ap()
        self.out = nc.dram_tensor("out", [S, D], F32, kind="ExternalOutput").ap()
        self.dbg_out = {}

        self.ctr = {}
        def sem(name):
            h = nc.alloc_semaphore(name)
            self.ctr[h] = 0
            return h
        self.sem_of = {"t": sem("sPE"), "v": sem("sDVE"), "s": sem("sACT"),
                       "g": sem("sPOOL"), "y": sem("sSY")}
        self.sST = sem("sST")
        self._dma_sems = {}
        self._mksem = sem
        self.q = {k: [] for k in ENGKEYS}
        self.waited = {k: {} for k in ENGKEYS}

        A = nc.alloc_sbuf_tensor
        self.xf = A("xf", [128, DC * S], F32).ap()
        self.xbf = A("xbf", [128, DC * S], BF16).ap()
        self.stage = A("stage", [128, 16384], F32).ap()
        self.arena = A("arena", [128, 6144], F32).ap()
        self.bcast = A("bcast", [128, 2 * S], F32).ap()
        self.brow = A("brow", [128, S], BF16).ap()
        self.consts = A("consts", [128, 1024], BF16).ap()
        self.cols = A("cols", [128, 96], F32).ap()
        self.ident32 = A("ident32", [128, 128], F32).ap()
        self.constf = A("constf", [128, 16], F32).ap()
        self._const_vals = {}
        self.ones128 = self.consts[:, 0:128]
        self.ident = self.consts[:, 128:256]
        self.tri = self.consts[:, 256:384]
        self.B64 = self.consts[:, 384:512]
        self.onesrow = self.consts[0:1, 0:128]
        self.onescol = self.consts[:, 0:1]

        self.stage_bf = self.stage.bitcast(BF16)
        self.arena_bf = self.arena.bitcast(BF16)
        self.bcast_bf = self.bcast.bitcast(BF16)
        self.rlbf_war = None

        self.lnvar_war = None
        self.lnmu_war = None
        self.bcast0_war = None
        self.brow_war = None
        self.ps = [nc.alloc_psum_tensor(f"ps{i}", [128, 1024], F32).ap()
                   for i in range(4)]
        self.ps_bf = [p.bitcast(BF16) for p in self.ps]
        self.ps_rd = [None] * 4
        self.ps_wr = [None] * 4

    def dsem(self, name):
        if name not in self._dma_sems:
            self._dma_sems[name] = self._mksem("sD_" + name)
        return self._dma_sems[name]

    def emit(self, k, fn, waits=(), inc=None, inck=1):
        need = {}
        for t in waits:
            if t is None:
                continue
            s, v = t
            assert v <= self.ctr[s], "wait targets unscheduled inc"
            if v > need.get(s, -1):
                need[s] = v
        for s, v in need.items():
            if s in self._dma_sems.values() and v != self.ctr[s]:
                raise AssertionError(
                    f"non-full wait {v} on DMA sem (full={self.ctr[s]})")
        todo = []
        for s, v in need.items():
            if self.waited[k].get(s, -1) < v:
                todo.append((s, v))
                self.waited[k][s] = v
        tok = None
        semh = None
        if inc is True:
            semh = self.sem_of[k]
        elif inc is None or inc is False:
            semh = None
        else:
            semh = inc
        if semh is not None:
            self.ctr[semh] += inck
            tok = (semh, self.ctr[semh])
        self.q[k].append((todo, fn, semh, inck))
        return tok

    def finalize(self):
        nc = self.nc
        with nc.Block() as blk:
            def mk(k):
                def run(eng):
                    for todo, fn, semh, inck in self.q[k]:
                        for s, v in todo:
                            eng.wait_ge(s, v)
                        ins = fn(eng)
                        if semh is not None:
                            assert ins is not None, "inc on empty emission"
                            ins.then_inc(semh, inck)
                return run
            blk.tensor(mk("t"))
            blk.vector(mk("v"))
            blk.scalar(mk("s"))
            blk.gpsimd(mk("g"))
            blk.sync(mk("y"))
        return nc

    def ps_acquire(self, i):
        return self.ps_rd[i]

    def tap(self, name, ap, toks):
        if name not in self.dbg:
            return None
        o = self.nc.dram_tensor("dbg_" + name, list(ap.shape), ap.dtype,
                                kind="ExternalOutput").ap()
        t = self.emit("y", lambda e, o=o, ap=ap: e.dma_start(out=o, in_=ap),
                      waits=toks, inc=self.dsem("tap"), inck=16)
        self.dbg_out[name] = o
        return t

    def build(self):
        self.emit_consts()
        self.gate_scalar_on_consts()
        self.emit_load_x()
        steps = [("attn_a", self.sub1_attn_a), ("ffn1", lambda: self.sub2_ffn(0)),
                 ("attn_b", self.sub3_attn_b), ("ffn2", lambda: self.sub2_ffn(1))]
        for nm, fn in steps:
            if _TRUNC[0] == nm:
                break
            fn()
            if _TRUNC[0] in ("b1", "b2", "b2p1", "b3", "ffn_P") and nm == "attn_b":
                break
        self.emit_store()
        return self.finalize()

    def emit_consts(self):
        e = self.emit
        c = None
        def g(fn, c_prev):
            return e("g", fn, waits=[c_prev], inc=True)
        c = g(lambda gp: gp.memset(self.consts[:, :], 0.0), c)
        for val in (1e-5, 1e-6, 1e-8, 0.125, -0.5, -1.0, 0.5, 0.02, 2.0,
                    -2.0, 1.0, 0.0):
            i = len(self._const_vals)
            cell = self.constf[:, i:i + 1]
            self._const_vals[val] = cell
            self.nc.const_aps.register(F32, float(val), cell)
            c = g(lambda gp, c2=cell, v2=float(val): gp.memset(c2, v2), c)
        c = g(lambda gp: gp.memset(self.ones128, 1.0), c)
        c = g(lambda gp: gp.affine_select(out=self.ident, in_=self.ident,
                                          compare_op=ALU.not_equal, fill=1.0,
                                          base=0, pattern=[[-1, 128]],
                                          channel_multiplier=1), c)
        c = g(lambda gp: gp.affine_select(out=self.tri, in_=self.tri,
                                          compare_op=ALU.is_ge, fill=1.0,
                                          base=-1, pattern=[[-1, 128]],
                                          channel_multiplier=1), c)
        c = g(lambda gp: gp.memset(self.consts[0:1, 816:880], 1.0), c)
        c = g(lambda gp: gp.memset(self.B64[0:64, 0:64], 1.0), c)
        c = g(lambda gp: gp.memset(self.B64[64:128, 64:128], 1.0), c)
        c = g(lambda gp: gp.memset(self.ident32[:, :], 0.0), c)
        self.c_consts = g(lambda gp: gp.affine_select(
            out=self.ident32, in_=self.ident32, compare_op=ALU.not_equal,
            fill=1.0, base=0, pattern=[[-1, 128]], channel_multiplier=1), c)

    def gate_scalar_on_consts(self):
        self.emit("s", lambda s: s.nop(), waits=[self.c_consts])

    def emit_load_x(self):
        e = self.emit
        x = self.inp["L0_attn_a_x"]
        stage = self.stage
        xr = x.rearrange("(c p) f -> p c f", p=128)
        c_grp = []
        for k4 in range(SC // 4):
            cts = []
            for i in range(4):
                c = k4 * 4 + i
                cts.append(e("y", lambda en, o=stage[:, c * 1024:(c + 1) * 1024],
                             i_=xr[:, c, :]: en.dma_start(out=o, in_=i_),
                             inc=self.dsem("x" + str(k4)), inck=16))
            c_grp.append([cts[3]])
        for k4 in range(SC // 4):
            cw = c_grp[k4]
            for dd in range(DC):
                pi = (k4 * DC + dd) % 2
                pb = self.ps[pi]
                tlast = None
                for c4 in range(4):
                    c = k4 * 4 + c4
                    src = stage[:, c * 1024 + dd * 128: c * 1024 + dd * 128 + 128]
                    w = cw + [self.c_consts, self.ps_rd[pi]] if c4 == 0 else []
                    tlast = e("t", lambda t, o=pb[:, c4 * 128:(c4 + 1) * 128],
                              s=src: t.transpose(out=o, in_=s, identity=self.ident32),
                              waits=w, inc=(True if c4 == 3 else None))
                self.ps_wr[pi] = tlast
                dst = self.xf[:, dd * S + k4 * 512: dd * S + (k4 + 1) * 512]
                self.ps_rd[pi] = e("v", lambda v, d=dst, p=pb[:, 0:512]:
                                   v.tensor_copy(out=d, in_=p),
                                   waits=[tlast], inc=True)
        self.c_xf = self.tok(self.sem_of["v"])

    def tok(self, semh):
        return (semh, self.ctr[semh])

    def ln_stats(self, xsrc_bf, c_src, eps, rms_only, boff=0, rcols_off=None,
                 sq_off_bf=0):
        e = self.emit
        sq = [self.stage_bf[:, sq_off_bf + i * 1024: sq_off_bf + (i + 1) * 1024]
              for i in range(2)]
        c_sq = [None, None]
        c_sqmm = [None, None]
        c_rBs, c_nmrs = [], []
        for th in range(2):
            p0, p1 = self.ps[0], self.ps[1]
            w0 = [self.ps_rd[0], self.ps_rd[1], self.c_consts] + (
                list(c_src) if isinstance(c_src, list) else [c_src])
            for dd in range(DC):
                xs = xsrc_bf[:, dd * S + th * 1024: dd * S + th * 1024 + 1024]
                csd = c_src[dd] if isinstance(c_src, list) else c_src
                c_sq[dd % 2] = e("v", lambda v, o=sq[dd % 2], i_=xs:
                                 v.tensor_mul(out=o, in0=i_, in1=i_),
                                 waits=[csd, c_sqmm[dd % 2]], inc=True)
                for sub in range(2):
                    if not rms_only:
                        e("t", lambda t, o=p0[:, sub * 512:(sub + 1) * 512],
                          r=xs[:, sub * 512:(sub + 1) * 512], st=(dd == 0),
                          sp=(dd == DC - 1):
                          t.matmul(o, self.ones128, r, start=st, stop=sp),
                          waits=w0 if dd == 0 else [])
                    e("t", lambda t, o=p1[:, sub * 512:(sub + 1) * 512],
                      r=sq[dd % 2][:, sub * 512:(sub + 1) * 512], st=(dd == 0),
                      sp=(dd == DC - 1):
                      t.matmul(o, self.ones128, r, start=st, stop=sp),
                      waits=[c_sq[dd % 2]] + (w0 if dd == 0 else []),
                      inc=(True if sub == 1 else None))
                c_sqmm[dd % 2] = self.tok(self.sem_of["t"])
            c_mm = self.tok(self.sem_of["t"])
            mu = self.arena[:, 4096:5120]
            var = self.arena[:, 5120:6144]
            rB = self.bcast[:, boff + th * 1024: boff + (th + 1) * 1024]
            inv = 1.0 / D
            if rms_only:
                c_v = e("v", lambda v, o=var, i_=p1[:, :]: v.tensor_scalar(
                    out=o, in0=i_, scalar1=inv, scalar2=float(eps),
                    op0=ALU.mult, op1=ALU.add),
                    waits=[c_mm, self.lnvar_war], inc=True)
                self.ps_rd[1] = c_v
                lbias = 0.0
            else:
                c_mu = e("v", lambda v, o=mu, i_=p0[:, :]: v.tensor_scalar(
                    out=o, in0=i_, scalar1=inv, scalar2=None, op0=ALU.mult),
                    waits=[c_mm, self.lnmu_war], inc=True)
                self.ps_rd[0] = c_mu
                c_m2 = e("v", lambda v, o=rB, a=mu: v.tensor_mul(
                    out=o, in0=a, in1=a), waits=[c_mu], inc=True)
                c_v = e("v", lambda v, o=var, i_=p1[:, :], m2=rB:
                        v.scalar_tensor_tensor(out=o, in0=i_, scalar=inv,
                                               in1=m2, op0=ALU.mult,
                                               op1=ALU.subtract),
                        waits=[c_m2], inc=True)
                self.ps_rd[1] = c_v
                lbias = float(eps)
            c_ln = e("s", lambda s, o=var: s.activation(
                out=o, in_=o, func=AF.Ln, bias=lbias, scale=1.0),
                waits=[c_v], inc=True)
            c_rB = e("s", lambda s, o=rB, i_=var: s.activation(
                out=o, in_=i_, func=AF.Exp, scale=-0.5),
                waits=[c_ln, self.bcast0_war], inc=True)
            c_rBs.append(c_rB)
            self.lnvar_war = c_rB
            if not rms_only:
                nmr = self.brow[:, th * 1024:(th + 1) * 1024]
                c_nmr1 = e("v", lambda v, o=nmr, a=mu, b=rB:
                           v.scalar_tensor_tensor(out=o, in0=a, scalar=-1.0,
                                                  in1=b, op0=ALU.mult,
                                                  op1=ALU.mult),
                           waits=[c_rB, self.brow_war], inc=True)
                c_nmrs.append(c_nmr1)
                self.lnmu_war = c_nmr1
        c_rcols = None
        if rcols_off is not None:
            rc = self.cols[:, rcols_off: rcols_off + SC]
            for g in range(2):
                tlast = None
                for i in range(8):
                    c = g * 8 + i
                    src2 = self.bcast[:, boff + c * 128: boff + (c + 1) * 128]
                    w = [c_rBs[g], self.ps_rd[2]] if i == 0 else []
                    tlast = e("t", lambda t, o=self.ps[2][:, i * 128:(i + 1) * 128],
                              s=src2: t.transpose(out=o, in_=s,
                                                  identity=self.ident32),
                              waits=w, inc=(True if i == 7 else None))
                c_rcols = e("v", lambda v, o=rc[:, g * 8:(g + 1) * 8],
                            p=self.ps[2].rearrange("q (c f) -> q c f", c=8):
                            v.tensor_copy(out=o, in_=p[:, :, 0:1].squeeze(2)),
                            waits=[tlast], inc=True)
                self.ps_rd[2] = c_rcols
        return c_rBs, c_nmrs, c_rcols

    def cast_xbf(self, c_war=None):
        c = None
        cl = []
        for dd in range(DC):
            cx = self.c_x_all[dd] if self.c_x_all[dd] is not None else self.c_x
            c = self.emit("v", lambda v, o=self.xbf[:, dd * S:(dd + 1) * S],
                          i_=self.xf[:, dd * S:(dd + 1) * S]:
                          v.tensor_copy(out=o, in_=i_),
                          waits=[cx, c_war, c], inc=True)
            cl.append(c)
        self.c_cast_l = cl
        return c

    def apply_norm_done(self, c):
        self.bcast0_war = c
        self.brow_war = c

    def apply_norm(self, rB_t, nmr_t=None, ws_cols=None, c_prev=None):
        e = self.emit
        c = c_prev
        for dd in range(DC):
            xs = self.xbf[:, dd * S:(dd + 1) * S]
            rBs = self.bcast[:, 0:S]
            if ws_cols is not None:
                c = e("v", lambda v, o=xs, w=ws_cols[dd], b=rBs:
                      v.scalar_tensor_tensor(out=o, in0=o, scalar=w, in1=b,
                                             op0=ALU.mult, op1=ALU.mult),
                      waits=list(rB_t) + [c], inc=True)
            else:
                c = e("v", lambda v, o=xs, b=rBs:
                      v.tensor_mul(out=o, in0=o, in1=b),
                      waits=list(rB_t) + [c], inc=True)
            if nmr_t is not None:
                c = e("v", lambda v, o=xs, b=self.brow[:, 0:S]:
                      v.tensor_add(out=o, in0=o, in1=b),
                      waits=list(nmr_t) + [c], inc=True)
        return c

    def wT_slab(self, par, slab):
        return self.arena_bf[:, par * 3072 + slab * 1024: par * 3072 + slab * 1024 + 1024]

    def weight_pair_prep(self, p, specs):
        e = self.emit
        par = p % 2
        wst = self.arena[:, 3072:4096]
        wbf = self.stage_bf[:, 30784:31808]
        toks = []
        for (nm, slab, perm) in specs:
            w = self.inp[nm]
            if perm:
                wg = w.rearrange("(d2 g) d -> g d2 d", g=16)
                e("y", lambda y, o=wst[0:64, :], i_=wg[2 * p]:
                  y.dma_start(out=o, in_=i_),
                  waits=[self.wstage_free], inc=self.dsem("wp"), inck=16)
                c_d = e("y", lambda y, o=wst[64:128, :], i_=wg[2 * p + 1]:
                        y.dma_start(out=o, in_=i_), inc=self.dsem("wp"),
                        inck=16)
            else:
                c_d = e("y", lambda y, o=wst, i_=w[p * 128:(p + 1) * 128, :]:
                        y.dma_start(out=o, in_=i_),
                        waits=[self.wstage_free], inc=self.dsem("wp"), inck=16)
            c_c = e("v", lambda v, o=wbf, i_=wst: v.tensor_copy(out=o, in_=i_),
                    waits=[c_d, self.wbf_free], inc=True)
            self.wstage_free = c_c
            dst = self.wT_slab(par, slab)
            tok = None
            for half in range(2):
                tl = None
                for i in range(4):
                    ddc = half * 4 + i
                    wt = [c_c, self.ps_rd[3], self.wslab_free[par][slab]] if i == 0 else []
                    tl = e("t", lambda t, o=self.ps_bf[3][:, i * 128:(i + 1) * 128],
                           s=wbf[:, ddc * 128:(ddc + 1) * 128]:
                           t.transpose(out=o, in_=s, identity=self.ident),
                           waits=wt, inc=(True if i == 3 else None))
                tok = e("v", lambda v, o=dst[:, half * 512:(half + 1) * 512],
                        pp=self.ps_bf[3][:, 0:512]: v.tensor_copy(out=o, in_=pp),
                        waits=[tl], inc=True)
                self.ps_rd[3] = tok
            self.wbf_free = tok
            toks.append(tok)
        return toks

    def load_cols1024(self, name, dst_cols, psi=3):
        e = self.emit
        gst = self.stage[0:8, 15904:16032]
        c_d = e("y", lambda y, o=gst, i_=self.inp[name].rearrange("(c p) -> c p", c=8):
                y.dma_start(out=o, in_=i_), waits=[self.gst_free],
                inc=self.dsem("gst"), inck=16)
        ct = e("t", lambda t, o=self.ps[psi][:, 0:8], s=gst:
               t.transpose(out=o, in_=s, identity=self.ident32[0:8, 0:8]),
               waits=[c_d, self.ps_rd[psi], self.c_consts], inc=True)
        cv = e("v", lambda v, o=dst_cols, pp=self.ps[psi][:, 0:8]:
               v.tensor_copy(out=o, in_=pp), waits=[ct], inc=True)
        self.ps_rd[psi] = cv
        self.gst_free = cv
        return cv

    def sub1_attn_a(self):
        e = self.emit
        self.c_x = self.c_xf
        self.wstage_free = None
        self.wbf_free = None
        self.gst_free = None
        self.wslab_free = [[None] * 3, [None] * 3]
        self.rl_war = [None, None]
        c_cast = self.cast_xbf()
        c_rB, c_nmr, _ = self.ln_stats(self.xbf, c_cast, 1e-5, False,
                                       boff=0, sq_off_bf=28736)
        c_h1 = self.apply_norm(c_rB, c_nmr)
        self.apply_norm_done(c_h1)
        self.tap("h1bf", self.xbf, [c_h1])
        vt = self.stage_bf[:, 16384:18496]
        vt3 = vt.rearrange("p (c w) -> p c w", w=132)
        e("g", lambda g: g.memset(vt3[:, :, 64:65], 1.0), waits=[self.c_xf])
        c_vones = e("g", lambda g: g.memset(vt3[:, :, 130:131], 1.0), inc=True)
        c_ones32 = e("g", lambda g: g.memset(self.cols[:, 0:64], 1.0), inc=True)

        qf = self.stage_bf[:, 18496:20544]
        kf = self.stage_bf[:, 20544:22592]
        Esl = [self.stage_bf[:, 22592 + i * 512: 22592 + (i + 1) * 512]
               for i in range(4)]
        u = self.stage[:, 13344:14368]
        rlrow = [self.arena[0:1, 4096:5120], self.arena[0:1, 5120:6144]]

        c_qkv = [None] * DC
        wnames = [("L0_attn_a_Wq", 0, False), ("L0_attn_a_Wk", 1, False),
                  ("L0_attn_a_Wv", 2, True)]
        wtoks = self.weight_pair_prep(0, wnames)
        c_u_rd = None
        c_E_rd = [None] * 4
        self.sc_rd = [[None, None], [None, None]]
        for p in range(NPAIR):
            par = p % 2
            wq, wk, wv = (self.wT_slab(par, i) for i in range(3))
            wtq, wtk, wtv = wtoks
            c_q = c_k = None
            for (dstap, wslab, wtok, psi) in ((qf, wq, wtq, 0), (kf, wk, wtk, 1)):
                ce = None
                for tt in range(4):
                    pb = self.ps[psi][:, (tt % 2) * 512:(tt % 2) * 512 + 512]
                    for dd in range(DC):
                        wt = ([wtok, c_h1, self.ps_rd[psi]] +
                              self.sc_rd[psi]) if dd == 0 else []
                        e("t", lambda t, o=pb,
                          l=wslab[:, dd * 128:(dd + 1) * 128],
                          r=self.xbf[:, dd * S + tt * 512: dd * S + tt * 512 + 512],
                          st=(dd == 0), sp=(dd == DC - 1):
                          t.matmul(o, l, r, start=st, stop=sp),
                          waits=wt, inc=(True if dd == DC - 1 else None))
                    cmm = self.tok(self.sem_of["t"])
                    ce = e("v", lambda v, o=dstap[:, tt * 512:(tt + 1) * 512],
                           pp=pb: v.tensor_copy(out=o, in_=pp),
                           waits=[cmm, ce], inc=True)
                    self.ps_rd[psi] = ce
                if psi == 0:
                    c_q = ce
                else:
                    c_k = ce
            c_v = None
            for half in range(2):
                for i in range(8):
                    cch = half * 8 + i
                    for dd in range(DC):
                        wt = [wtv, c_h1, self.ps_rd[2], c_vones] if (i == 0 and dd == 0) else []
                        e("t", lambda t, o=self.ps[2][:, i * 128:(i + 1) * 128],
                          l=self.xbf[:, dd * S + cch * 128: dd * S + cch * 128 + 128],
                          r=wv[:, dd * 128:(dd + 1) * 128],
                          st=(dd == 0), sp=(dd == DC - 1):
                          t.matmul(o, l, r, start=st, stop=sp),
                          waits=wt, inc=(True if (dd == DC - 1 and i == 7) else None))
                cmm = self.tok(self.sem_of["t"])
                c_v = e("v", lambda v, h=half:
                        v.tensor_copy(
                            out=vt3[:, h * 8:(h + 1) * 8, :].rearrange(
                                "p c (g w) -> p c g w", w=66)[:, :, :, 0:64],
                            in_=self.ps[2].rearrange("p (c g w) -> p c g w",
                                                     g=2, w=64)),
                        waits=[cmm, c_v], inc=True)
                self.ps_rd[2] = c_v
            self.wslab_free[par] = [c_q, c_k, c_v]
            if p + 1 < NPAIR:
                wtoks = self.weight_pair_prep(p + 1, wnames)
            qkvch = self.stage_bf[:, p * 2048:(p + 1) * 2048]
            for tq in range(4):
                climit = 4 * tq + 4
                c_pv_last = [None, None]
                pend = None
                def emit_pv(cc, cpvws):
                    for hh2 in range(2):
                        wt2 = [cpvws[hh2]]
                        if cc == 0:
                            wt2.append(self.ps_rd[2])
                        e("t", lambda t,
                          o=self.ps[2][0:65, hh2 * 512:hh2 * 512 + 512],
                          l=vt[:, cc * 132 + hh2 * 66: cc * 132 + hh2 * 66 + 65],
                          r=Esl[hh2 * 2 + (cc % 2)][:, 0:512],
                          st=(cc == 0), sp=(cc == climit - 1):
                          t.matmul(o, l, r, start=st, stop=sp),
                          waits=wt2,
                          inc=(True if cc == climit - 1 else None))
                        if cc == climit - 1:
                            c_pv_last[hh2] = self.tok(self.sem_of["t"])
                        c_E_rd[hh2 * 2 + (cc % 2)] = self.tok(self.sem_of["t"])
                for c in range(climit):
                    c_rel = c - 4 * tq
                    vfrom = max(0, c_rel * 128)
                    cpvws = []
                    for hh in range(2):
                        psS = self.ps[hh][:, (c % 2) * 512:(c % 2) * 512 + 512]
                        koff = hh * 64
                        e("t", lambda t, o=psS[:, vfrom:512],
                          l=kf[koff:koff + 64, c * 128:(c + 1) * 128],
                          r=qf[koff:koff + 64, tq * 512 + vfrom: (tq + 1) * 512]:
                          t.matmul(o, l, r, start=True, stop=True),
                          waits=[c_q, c_k, self.sc_rd[hh][c % 2]], inc=True)
                        cmm = self.tok(self.sem_of["t"])
                        sl = Esl[hh * 2 + (c % 2)]
                        cex = e("s", lambda s, o=sl[:, vfrom:512],
                                i_=psS[:, vfrom:512]:
                                s.activation(out=o, in_=i_, func=AF.Exp,
                                             scale=0.125),
                                waits=[cmm, c_E_rd[hh * 2 + (c % 2)]], inc=True)
                        self.sc_rd[hh][c % 2] = cex
                        cpvw = cex
                        if c_rel >= 0:
                            cpvw = e("v", lambda v, o=sl[:, vfrom:vfrom + 128]:
                                     v.tensor_mul(out=o, in0=o, in1=self.tri),
                                     waits=[cex], inc=True)
                            if vfrom > 0:
                                cpvw = e("v", lambda v, o=sl[:, 0:vfrom]:
                                         v.memset(o, 0.0), inc=True)
                        cpvws.append(cpvw)
                    if pend is not None:
                        emit_pv(pend[0], pend[1])
                    pend = (c, cpvws)
                emit_pv(pend[0], pend[1])
                cuA = e("v", lambda v, o=u[0:64, 0:512],
                        pp=self.ps[2][0:64, 0:512]:
                        v.tensor_copy(out=o, in_=pp),
                        waits=[c_pv_last[0], c_u_rd], inc=True)
                cuB = e("v", lambda v, o=u[64:128, 0:512],
                        pp=self.ps[2][0:64, 512:1024]:
                        v.tensor_copy(out=o, in_=pp),
                        waits=[c_pv_last[1], cuA], inc=True)
                crl = []
                for hh in range(2):
                    hi = self.bcast_bf[0:1, 4096 + hh * 512:
                                       4096 + hh * 512 + 512]
                    lo = self.bcast_bf[0:1, 5120 + hh * 512:
                                       5120 + hh * 512 + 512]
                    cl1 = e("s", lambda s, o=rlrow[hh][:, 0:512],
                            pp=self.ps[2][64:65, hh * 512:hh * 512 + 512]:
                            s.activation(out=o, in_=pp, func=AF.Ln),
                            waits=[c_pv_last[hh], self.rl_war[hh]], inc=True)
                    c_rc = e("s", lambda s, o=rlrow[hh][:, 0:512]:
                            s.activation(out=o, in_=o, func=AF.Exp,
                                         scale=-1.0),
                            waits=[cl1], inc=True)
                    c_hi = e("v", lambda v, o=hi, i_=rlrow[hh][:, 0:512]:
                             v.tensor_copy(out=o, in_=i_),
                             waits=[c_rc, self.rlbf_war], inc=True)
                    c_lo = e("v", lambda v, o=lo, a=rlrow[hh][:, 0:512], b=hi:
                             v.tensor_tensor(out=o, in0=a, in1=b,
                                             op=ALU.subtract),
                             waits=[c_hi], inc=True)
                    crl.append((c_lo, hi, lo))
                for hh in range(2):
                    c_lo, hi, lo = crl[hh]
                    wt = [c_lo]
                    if hh == 0:
                        wt += [self.ps_rd[3], self.c_consts]
                    e("t", lambda t,
                      o=self.ps[3][hh * 64:hh * 64 + 64, 0:512],
                      l=self.onesrow[:, 0:64], r=hi:
                      t.matmul(o, l, r, start=True, stop=False),
                      waits=wt)
                    e("t", lambda t,
                      o=self.ps[3][hh * 64:hh * 64 + 64, 0:512],
                      l=self.onesrow[:, 0:64], r=lo:
                      t.matmul(o, l, r, start=False, stop=True),
                      waits=[], inc=(hh == 1))
                crlB = self.tok(self.sem_of["t"])
                self.rl_war = [crl[0][0], crl[1][0]]
                self.rlbf_war = crlB
                cq2 = e("v", lambda v, o=qkvch[:, tq * 512:(tq + 1) * 512],
                        a=u[:, 0:512], b=self.ps[3][:, 0:512]:
                        v.tensor_mul(out=o, in0=a, in1=b),
                        waits=[cuA, cuB, crlB], inc=True)
                c_u_rd = cq2
                self.ps_rd[3] = cq2
                self.ps_rd[2] = cq2
                c_qkv[p] = cq2
            if p == 0:
                self.tap("qf0", qf, [c_q])
                self.tap("kf0", kf, [c_k])
                self.tap("vt0", vt, [c_v])
                self.tap("u0", u, [c_qkv[0]])
        for psi in range(2):
            toks = [t for t in self.sc_rd[psi] if t is not None]
            if toks:
                self.ps_rd[psi] = (toks[0][0], max(t[1] for t in toks))
        self.tap("qkv", self.stage_bf[:, 0:16384], c_qkv)
        self.sub1_epilogue(c_qkv)

    def sub1_epilogue(self, c_qkv):
        e = self.emit
        qkv = self.stage_bf[:, 0:16384]
        c_rB, c_nmr, _ = self.ln_stats(qkv, c_qkv[-1], 1e-5, False,
                                       boff=0, sq_off_bf=28736)
        c_ln = None
        for dd in range(DC):
            xs = qkv[:, dd * S:(dd + 1) * S]
            c_ln = e("v", lambda v, o=xs, b=self.bcast[:, 0:S]:
                     v.tensor_mul(out=o, in0=o, in1=b),
                     waits=list(c_rB) + [c_ln, c_qkv[dd]], inc=True)
            c_ln = e("v", lambda v, o=xs, b=self.brow[:, 0:S]:
                     v.tensor_add(out=o, in0=o, in1=b),
                     waits=list(c_nmr) + [c_ln], inc=True)
        self.apply_norm_done(c_ln)
        self.tap("lnq", qkv, [c_ln])
        gam = self.cols[:, 16:24]
        bet = self.cols[:, 24:32]
        bob = self.cols[:, 32:40]
        c_g = self.load_cols1024("L0_attn_a_gamma", gam)
        c_b = self.load_cols1024("L0_attn_a_beta", bet)
        bet_bf = self.consts[:, 640:648]
        c_bbf = e("v", lambda v: v.tensor_copy(out=bet_bf, in_=bet),
                  waits=[c_b], inc=True)
        woT = self.stage_bf[:, 16384:24576]
        wst = self.arena[:, 3072:4096]
        wbf = self.stage_bf[:, 30784:31808]
        c_woT = None
        for eb in range(DC):
            c_d = e("y", lambda y, o=wst,
                    i_=self.inp["L0_attn_a_Wo"][eb * 128:(eb + 1) * 128, :]:
                    y.dma_start(out=o, in_=i_),
                    waits=[self.wstage_free], inc=self.dsem("wp"), inck=16)
            c_c = e("v", lambda v, o=wbf, i_=wst: v.tensor_copy(out=o, in_=i_),
                    waits=[c_d, self.wbf_free], inc=True)
            self.wstage_free = c_c
            for half in range(2):
                tl = None
                for i in range(4):
                    jj = half * 4 + i
                    wt = [c_c, self.ps_rd[3]] if i == 0 else []
                    tl = e("t", lambda t, o=self.ps_bf[3][:, i * 128:(i + 1) * 128],
                           s=wbf[:, jj * 128:(jj + 1) * 128]:
                           t.transpose(out=o, in_=s, identity=self.ident),
                           waits=wt, inc=(True if i == 3 else None))
                c_woT = e("v", lambda v, h=half, ebb=eb:
                          v.tensor_copy(
                              out=woT.rearrange("p (c q) -> p c q", q=1024)
                              [:, h * 4:(h + 1) * 4, ebb * 128:(ebb + 1) * 128],
                              in_=self.ps_bf[3][:, 0:512].rearrange(
                                  "p (c q) -> p c q", q=128)),
                          waits=[tl, c_qkv[-1] if eb == 0 and half == 0 else None,
                                 c_woT],
                          inc=True)
                self.ps_rd[3] = c_woT
        for ee in range(DC):
            for jj in range(DC):
                wt = [c_woT, c_bbf]
                if ee == 0 and jj == 0:
                    wt.append(self.ps_rd[2])
                e("t", lambda t, o=self.ps[2][:, ee:ee + 1],
                  l=woT[:, jj * 1024 + ee * 128: jj * 1024 + (ee + 1) * 128],
                  r=bet_bf[:, jj:jj + 1], st=(jj == 0), sp=(jj == DC - 1):
                  t.matmul(o, l, r, start=st, stop=sp),
                  waits=wt, inc=(True if (ee == DC - 1 and jj == DC - 1) else None))
        c_bo = e("v", lambda v, o=bob, pp=self.ps[2][:, 0:8]:
                 v.tensor_copy(out=o, in_=pp),
                 waits=[self.tok(self.sem_of["t"])], inc=True)
        self.ps_rd[2] = c_bo
        c_bo_mm = self.tok(self.sem_of["t"])
        c_wg = [None] * DC
        for jj in range(DC):
            c_wg[jj] = e("v", lambda v, o=woT[:, jj * 1024:(jj + 1) * 1024],
                         g2=gam[:, jj:jj + 1]:
                         v.tensor_scalar(out=o, in0=o, scalar1=g2,
                                         scalar2=None, op0=ALU.mult),
                         waits=[c_woT, c_g, c_bo_mm], inc=True)
        c_x1 = [None] * DC
        for ee in range(DC):
            for tt in range(4):
                psi = ee % 2
                pb = self.ps[psi][:, (tt % 2) * 512:(tt % 2) * 512 + 512]
                for jj in range(DC):
                    wt = list(c_wg) + [c_ln] if jj == 0 else []
                    if jj == 0:
                        wt.append(self.ps_rd[psi])
                    e("t", lambda t, o=pb,
                      l=woT[:, jj * 1024 + ee * 128: jj * 1024 + (ee + 1) * 128],
                      r=qkv[:, jj * 2048 + tt * 512: jj * 2048 + tt * 512 + 512],
                      st=(jj == 0), sp=(jj == DC - 1):
                      t.matmul(o, l, r, start=st, stop=sp),
                      waits=wt, inc=(True if jj == DC - 1 else None))
                cmm = self.tok(self.sem_of["t"])
                cx = e("v", lambda v,
                       o=self.xf[:, ee * S + tt * 512: ee * S + tt * 512 + 512],
                       pp=pb, bo=bob[:, ee:ee + 1]:
                       v.scalar_tensor_tensor(out=o, in0=pp, scalar=bo,
                                              in1=o, op0=ALU.add, op1=ALU.add),
                       waits=[cmm, c_bo, c_x1[ee]], inc=True)
                self.ps_rd[psi] = cx
                c_x1[ee] = cx
        self.c_x = c_x1[-1]
        self.c_x_all = c_x1
        self.tap("x1", self.xf, c_x1)

    def load_wslab(self, name, pieces, c_war=None, nbuf=2, ceng="v"):
        e = self.emit
        toks = []
        for i, (src_ap, dst) in enumerate(pieces):
            wi = i % nbuf
            wst = self.stage[:, 9216 + wi * 1024: 9216 + wi * 1024 + 1024]
            w = wst[:, 0:1024]
            if len(src_ap.shape) == 3:
                wdma = w.rearrange("p (c f) -> p c f", c=src_ap.shape[1])
            else:
                wdma = w
            c_d = e("y", lambda y, o=wdma, i_=src_ap: y.dma_start(out=o, in_=i_),
                    waits=[self.wst2_free[wi]] + (
                        c_war if isinstance(c_war, list) else [c_war]),
                    inc=self.dsem("wl" + str(wi)), inck=16)
            if ceng == "s":
                c_c = e("s", lambda s, o=dst, i_=w:
                        s.activation(out=o, in_=i_, func=AF.Copy),
                        waits=[c_d], inc=True)
            else:
                c_c = e("v", lambda v, o=dst, i_=w: v.tensor_copy(out=o, in_=i_),
                        waits=[c_d], inc=True)
            self.wst2_free[wi] = c_c
            toks.append(c_c)
        return toks

    def load_cols(self, name, n, ncols, dst_cols):
        e = self.emit
        gst = self.stage[0:ncols, 15904:15904 + 128]
        c_d = e("y", lambda y, o=gst[:, 0:n],
                i_=self.inp[name].rearrange("(c p) -> c p", c=ncols):
                y.dma_start(out=o, in_=i_), waits=[self.gst_free],
                inc=self.dsem("gst"), inck=16)
        ct = e("t", lambda t, o=self.ps[3][0:n, 0:ncols], s=gst[:, 0:n]:
               t.transpose(out=o, in_=s,
                           identity=self.ident32[0:ncols, 0:ncols]),
               waits=[c_d, self.ps_rd[3], self.c_consts], inc=True)
        cv = e("v", lambda v, o=dst_cols[0:n, :], pp=self.ps[3][0:n, 0:ncols]:
               v.tensor_copy(out=o, in_=pp), waits=[ct], inc=True)
        self.ps_rd[3] = cv
        self.gst_free = cv
        return cv

    def sub2_ffn(self, tag):
        e = self.emit
        self.wst2_free = [None] * 4
        barrier = self.tok(self.sem_of["t"])
        c_cast = self.cast_xbf(c_war=barrier)
        cast_l = self.c_cast_l
        u1 = self.arena_bf[:, 0:2048]
        v2 = self.arena_bf[:, 2048:4096]
        v1 = self.arena_bf[:, 4096:12288]
        u2 = self.stage_bf[:, 0:8192]
        P_sb = self.stage_bf[:, 8192:12288]
        Q_sb = self.stage_bf[:, 12288:16384]
        Hb = [self.stage_bf[:, 16384 + i * 1024: 16384 + (i + 1) * 1024]
              for i in range(2)]
        U1, V1, U2, V2 = (self.inp["L0_ffn_U1"], self.inp["L0_ffn_V1"],
                          self.inp["L0_ffn_U2"], self.inp["L0_ffn_V2"])
        b1c = self.cols[:, 40:72]
        b2c = self.cols[:, 72:80]
        c_b1 = self.load_cols("L0_ffn_b1", 128, 32, b1c)
        c_b2 = self.load_cols("L0_ffn_b2", 128, 8, b2c)
        pieces = []
        u1r = U1.rearrange("(c p) f -> p c f", p=128)
        for g in range(2):
            pieces.append((u1r[:, 4 * g:4 * g + 4, :],
                           u1[:, g * 1024:(g + 1) * 1024]))
        for cc in range(2):
            pieces.append((V2[cc * 128:(cc + 1) * 128, :],
                           v2[:, cc * 1024:(cc + 1) * 1024]))
        wtok = self.load_wslab("ffn_a", pieces, c_war=barrier, nbuf=4,
                               ceng="s")
        c_u1 = wtok[1]; c_v2 = wtok[3]
        s1c = self.cols[:, 80:82]
        for rc in range(2):
            for dd in range(DC):
                wt = []
                if dd == 0:
                    wt = [c_u1] + ([self.ps_rd[2]] if rc == 0 else [])
                e("t", lambda t, o=self.ps[2][:, rc:rc + 1],
                  l=u1[:, dd * 256 + rc * 128: dd * 256 + (rc + 1) * 128],
                  r=self.ones128[:, 0:1],
                  st=(dd == 0), sp=(dd == DC - 1):
                  t.matmul(o, l, r, start=st, stop=sp),
                  waits=wt, inc=(True if (rc == 1 and dd == DC - 1) else None))
        cs1m = self.tok(self.sem_of["t"])
        c_s1 = e("v", lambda v, o=s1c, pp=self.ps[2][:, 0:2]:
                 v.tensor_copy(out=o, in_=pp), waits=[cs1m], inc=True)
        self.ps_rd[2] = c_s1
        c_rB, c_nmr, _ = self.ln_stats(self.xbf, cast_l, 1e-5, False,
                                       boff=0, sq_off_bf=28736)
        tmpP = self.stage[:, 13312:13824]
        c_P = [None, None]
        for rc in range(2):
            for tt in range(4):
                pb = self.ps[rc][:, (tt % 2) * 512:(tt % 2) * 512 + 512]
                for dd in range(DC):
                    wt = [cast_l[dd]]
                    if dd == 0:
                        wt += [c_u1, self.ps_rd[rc]]
                    e("t", lambda t, o=pb,
                      l=u1[:, dd * 256 + rc * 128: dd * 256 + (rc + 1) * 128],
                      r=self.xbf[:, dd * S + tt * 512: dd * S + tt * 512 + 512],
                      st=(dd == 0), sp=(dd == DC - 1):
                      t.matmul(o, l, r, start=st, stop=sp),
                      waits=wt, inc=(True if dd == DC - 1 else None))
                cmm = self.tok(self.sem_of["t"])
                th2 = tt // 2
                cP1 = e("v", lambda v, o=tmpP,
                        a=self.brow[:, tt * 512:(tt + 1) * 512],
                        w=s1c[:, rc:rc + 1], pp=pb:
                        v.scalar_tensor_tensor(out=o, in0=a, scalar=w, in1=pp,
                                               op0=ALU.mult, op1=ALU.add),
                        waits=[cmm, c_s1, self.tmpP_war,
                               c_nmr[th2] if th2 < len(c_nmr) else None],
                        inc=True)
                self.ps_rd[rc] = cP1
                c_P[rc] = e("v", lambda v,
                            o=P_sb[:, rc * 2048 + tt * 512: rc * 2048 + tt * 512 + 512],
                            a=tmpP, b=self.bcast[:, tt * 512:(tt + 1) * 512]:
                            v.tensor_mul(out=o, in0=a, in1=b),
                            waits=[cP1, c_P[rc],
                                   c_rB[th2] if th2 < len(c_rB) else None],
                            inc=True)
                self.tmpP_war = c_P[rc]
        self.apply_norm_done(c_P[1])
        pieces_u2 = []
        u2r = U2.rearrange("(c p) f -> p c f", p=128)
        for g in range(8):
            pieces_u2.append((u2r[:, 4 * g:4 * g + 4, :],
                              u2[:, g * 1024:(g + 1) * 1024]))
        wtok2 = self.load_wslab("ffn_u2", pieces_u2, c_war=barrier, nbuf=4,
                                ceng="s")
        c_u2g = [wtok2[g] for g in range(8)]
        pieces_v1 = []
        for pc in range(4):
            for cc in range(2):
                pieces_v1.append((V1[cc * 128:(cc + 1) * 128,
                                     pc * 1024:(pc + 1) * 1024],
                                  v1[:, cc * 4096 + pc * 1024: cc * 4096 + (pc + 1) * 1024]))
        wtok3 = self.load_wslab("ffn_v1", pieces_v1,
                                c_war=[barrier, c_nmr[-1], c_rB[-1]], nbuf=4,
                                ceng="s")
        c_v1p = [(wtok3[2 * pc], wtok3[2 * pc + 1]) for pc in range(4)]
        self.tap("Pf" if tag == 0 else "Pf2", P_sb, [c_P[0], c_P[1]])
        if _TRUNC[0] == "ffn_P":
            self.c_x = self.c_x_all[-1]
            return
        c_Q = [None, None]
        for th in range(2):
            c_Hev = [None, None]

            def emit_Q(fc):
                hb = fc % 2
                for rc in range(2):
                    for sub in range(2):
                        wt = [c_Hev[hb], c_u2g[fc // 4]]
                        if fc == 0 and rc == 0 and sub == 0:
                            wt += [self.ps_rd[2], self.ps_rd[3]]
                        e("t", lambda t,
                          o=self.ps[2 + rc][:, sub * 512:(sub + 1) * 512],
                          l=u2[:, fc * 256 + rc * 128: fc * 256 + (rc + 1) * 128],
                          r=Hb[hb][:, sub * 512:(sub + 1) * 512],
                          st=(fc == 0), sp=(fc == DFF // 128 - 1):
                          t.matmul(o, l, r, start=st, stop=sp),
                          waits=wt,
                          inc=(True if (rc == 1 and sub == 1) else None))
                self.c_Qmm = getattr(self, "c_Qmm", [None, None])
                self.c_Qmm[hb] = self.tok(self.sem_of["t"])

            for fc in range(DFF // 128):
                hb = fc % 2
                psH = self.ps[hb]
                for cc in range(2):
                    for sub in range(2):
                        wt = []
                        if cc == 0 and sub == 0:
                            wt = [c_v1p[fc // 8][0], c_v1p[fc // 8][1],
                                  c_P[0], c_P[1], c_Hev[hb],
                                  self.ps_rd[hb]]
                        e("t", lambda t, o=psH[:, sub * 512:(sub + 1) * 512],
                          l=v1[:, cc * 4096 + fc * 128: cc * 4096 + (fc + 1) * 128],
                          r=P_sb[:, cc * 2048 + th * 1024 + sub * 512:
                                 cc * 2048 + th * 1024 + sub * 512 + 512],
                          st=(cc == 0), sp=(cc == 1):
                          t.matmul(o, l, r, start=st, stop=sp),
                          waits=wt,
                          inc=(True if (cc == 1 and sub == 1) else None))
                c_Hmm = self.tok(self.sem_of["t"])
                c_Hev[hb] = e("s", lambda s, o=Hb[hb], i_=psH,
                              b=b1c[:, fc:fc + 1]:
                              s.activation(out=o, in_=i_, func=AF.Gelu_apprx_tanh,
                                           bias=b, scale=1.0),
                              waits=[c_Hmm, c_b1,
                                     (self.c_Qmm[hb] if hasattr(self, "c_Qmm") else None)],
                              inc=True)
                self.ps_rd[hb] = c_Hev[hb]
                if fc >= 1:
                    emit_Q(fc - 1)
            emit_Q(DFF // 128 - 1)
            cqmm = self.tok(self.sem_of["t"])
            for rc in range(2):
                c_Q[rc] = e("v", lambda v,
                            o=Q_sb[:, rc * 2048 + th * 1024: rc * 2048 + (th + 1) * 1024],
                            pp=self.ps[2 + rc]: v.tensor_copy(out=o, in_=pp),
                            waits=[cqmm, c_Q[rc]], inc=True)
                self.ps_rd[2 + rc] = c_Q[rc]
            for ee in range(DC):
                psi = ee % 2
                pb = self.ps[psi]
                for cc in range(2):
                    for sub in range(2):
                        wt = []
                        if cc == 0 and sub == 0:
                            wt = [c_v2, c_Q[0], c_Q[1], self.ps_rd[psi]]
                        e("t", lambda t, o=pb[:, sub * 512:(sub + 1) * 512],
                          l=v2[:, cc * 1024 + ee * 128: cc * 1024 + (ee + 1) * 128],
                          r=Q_sb[:, cc * 2048 + th * 1024 + sub * 512:
                                 cc * 2048 + th * 1024 + sub * 512 + 512],
                          st=(cc == 0), sp=(cc == 1):
                          t.matmul(o, l, r, start=st, stop=sp),
                          waits=wt,
                          inc=(True if (cc == 1 and sub == 1) else None))
                cmm = self.tok(self.sem_of["t"])
                cx = e("v", lambda v,
                       o=self.xf[:, ee * S + th * 1024: ee * S + (th + 1) * 1024],
                       pp=pb, bo=b2c[:, ee:ee + 1]:
                       v.scalar_tensor_tensor(out=o, in0=pp, scalar=bo,
                                              in1=o, op0=ALU.add, op1=ALU.add),
                       waits=[cmm, c_b2, self.c_x_all[ee]], inc=True)
                self.ps_rd[psi] = cx
                self.c_x_all[ee] = cx
        del self.c_Qmm
        self.c_x = self.c_x_all[-1]
        self.tap("x2" if tag == 0 else "x4", self.xf, list(self.c_x_all))

    def sub3_attn_b(self):
        e = self.emit
        self.rk_war = None
        self.nw1_war = None
        self.ret_tmp_war = None
        self.tmpo_war = None
        self.sq2_war = [None, None]
        self.p1q_war = [None, None]
        self.gg_free = None
        barrier = self.tok(self.sem_of["t"])
        c_cast = self.cast_xbf(c_war=barrier)
        c_rB, c_nmr, _ = self.ln_stats(self.xbf, c_cast, 1e-5, False,
                                       boff=0, sq_off_bf=28736)
        c_xo = self.apply_norm(c_rB, c_nmr)
        self.apply_norm_done(c_xo)
        self.tap("xo", self.xbf, [c_xo])
        c_rn, _, c_rcols = self.ln_stats(self.xbf, c_xo, 1e-6, True,
                                         boff=0, rcols_off=0, sq_off_bf=28736)
        self.c_rcols_tok = c_rcols
        self.tap("rnB", self.bcast[:, 0:S], c_rn)
        wscol = self.cols[:, 16:24]
        wrcol = self.cols[:, 24:32]
        c_ws = self.load_cols("L0_attn_b_store_norm_w", 128, 8, wscol)
        c_wr = self.load_cols("L0_attn_b_retrieve_norm_w", 128, 8, wrcol)
        knw = self.cols[:, 32:33]
        qnw = self.cols[:, 33:34]
        c_knw = self.load_headcol("L0_attn_b_k_norm_w", knw)
        c_qnw = self.load_headcol("L0_attn_b_q_norm_w", qnw)
        self.c_magic = self.emit(
            "g", lambda g: g.memset(self.stage[:, 14088:14092], 1.3211836172961055e+19),
            waits=[barrier], inc=True)
        c_mem = self.load_mem_weights()
        c_atw = self.load_atw(wscol, c_ws, c_war=barrier)
        wk = self.arena_bf[:, 0:8192]
        wv = self.stage_bf[:, 0:8192]
        c_wk = self.load_w1024("L0_attn_b_W_K", wk, wscol, c_ws, c_war=barrier)
        c_wv = self.load_w1024("L0_attn_b_W_V", wv, wscol, c_ws, c_war=barrier)
        c_at = self.proj_alpha_theta(c_atw, c_xo, c_rn, c_rcols)
        self.tap("at", self.stage_bf[:, 24576:25088], [c_at])
        c_am = self.a_mean_row(c_at)
        if _TRUNC[0] == "b1":
            return
        c_nw0, c_nw1 = self.grad_pairs(c_wk, c_wv, c_mem, c_xo, c_rn, c_knw,
                                       c_at, c_am)
        if c_nw0 is None:
            return
        self.tap("nw0", self.arena_bf[:, 8832:10880], [c_nw0])
        self.tap("nw1", self.stage_bf[:, 25088:27136], [c_nw1])
        if (_TRUNC[0] or "").startswith("b2") or _TRUNC[0] == "b3":
            return
        self.retrieve_pairs(c_nw0, c_nw1, c_xo, c_rn, c_wr, wrcol, c_qnw,
                            barrier)
        self.tap("x3", self.xf, list(self.c_x_all))

    def load_headcol(self, name, dst):
        e = self.emit
        gst = self.stage[0:1, 15904:15968]
        c_d = e("y", lambda y, o=gst,
                i_=self.inp[name].unsqueeze(0):
                y.dma_start(out=o, in_=i_), waits=[self.gst_free],
                inc=self.dsem("gst"), inck=16)
        ct = e("t", lambda t, o=self.ps[3][0:64, 0:1], s=gst:
               t.transpose(out=o, in_=s, identity=self.ident32[0:1, 0:1]),
               waits=[c_d, self.ps_rd[3]], inc=True)
        cv = e("v", lambda v, o=dst[0:64, :], pp=self.ps[3][0:64, 0:1]:
               v.tensor_copy(out=o, in_=pp), waits=[ct], inc=True)
        cv = e("v", lambda v, o=dst[64:128, :], i_=dst[0:64, :]:
               v.tensor_copy(out=o, in_=i_), waits=[cv], inc=True)
        self.ps_rd[3] = cv
        self.gst_free = cv
        return cv

    def load_w1024(self, name, slab, ws_cols, c_ws, c_war=None):
        e = self.emit
        W = self.inp[name]
        c_c = None
        for dd in range(DC):
            wi = dd % 2
            wst = self.stage[:, 9216 + wi * 1024: 9216 + wi * 1024 + 1024]
            c_d = e("y", lambda y, o=wst, i_=W[dd * 128:(dd + 1) * 128, :]:
                    y.dma_start(out=o, in_=i_),
                    waits=[self.wst2_free[wi], c_war],
                    inc=self.dsem("wl" + str(wi)), inck=16)
            c_c = e("s", lambda s, o=slab[:, dd * 1024:(dd + 1) * 1024],
                    i_=wst, w=ws_cols[:, dd:dd + 1]:
                    s.activation(out=o, in_=i_, func=AF.Copy, scale=w),
                    waits=[c_d, c_ws, c_war, c_c], inc=True)
            self.wst2_free[wi] = c_c
        return c_c

    def load_mem_weights(self):
        e = self.emit
        w0bd = self.arena_bf[:, 8192:8704]
        w1s = self.arena_bf[:, 8704:8832]
        w1bd = self.stage_bf[:, 27648:28160]
        st0 = self.stage[0:64, 15392:15648]
        cz0 = e("g", lambda g, o=w0bd: g.memset(o, 0.0), waits=[self.gst_free],
                inc=True)
        cz1 = e("g", lambda g, o=w1bd: g.memset(o, 0.0), waits=[cz0], inc=True)
        c_d0 = e("y", lambda y, o=st0, i_=self.inp["L0_attn_b_mem_W0"]:
                 y.dma_start(out=o, in_=i_), waits=[self.gst_free],
                 inc=self.dsem("mem"), inck=16)
        c0 = e("v", lambda v, o=w0bd[0:64, 0:256], i_=st0:
               v.tensor_copy(out=o, in_=i_), waits=[c_d0, cz0], inc=True)
        c0 = e("v", lambda v, o=w0bd[64:128, 256:512], i_=w0bd[0:64, 0:256]:
               v.tensor_copy(out=o, in_=i_), waits=[c0], inc=True)
        st1 = self.stage[:, 15648:15776]
        c_d1 = e("y", lambda y, o=st1.rearrange("p (c f) -> p c f", c=2),
                 i_=self.inp["L0_attn_b_mem_W1"].rearrange("(c p) f -> p c f", p=128):
                 y.dma_start(out=o, in_=i_), waits=[self.tok(self.dsem("mem"))],
                 inc=self.dsem("mem"), inck=16)
        c1 = e("v", lambda v, o=w1s, i_=st1: v.tensor_copy(out=o, in_=i_),
               waits=[c_d1], inc=True)
        tl = None
        for hc in range(2):
            tl = e("t", lambda t, o=self.ps_bf[3][0:64, hc * 128:(hc + 1) * 128],
                   s=w1s[:, hc * 64:(hc + 1) * 64]:
                   t.transpose(out=o, in_=s, identity=self.ident),
                   waits=([c1, self.ps_rd[3]] if hc == 0 else []),
                   inc=(hc == 1))
        c2 = e("v", lambda v, o=w1bd[0:64, 0:256], pp=self.ps_bf[3][0:64, 0:256]:
               v.tensor_copy(out=o, in_=pp), waits=[tl, cz1], inc=True)
        c2 = e("v", lambda v, o=w1bd[64:128, 256:512], i_=w1bd[0:64, 0:256]:
               v.tensor_copy(out=o, in_=i_), waits=[c2], inc=True)
        self.ps_rd[3] = c2
        self.gst_free = c2
        e("y", lambda y: y.dma_start(
            out=self.stage[:, 16032:16160].rearrange("p (c f) -> p c f", c=8),
            in_=self.inp["L0_attn_b_eta_w"].rearrange("(c p) f -> p c f", p=128)),
          waits=[c2], inc=self.dsem("eta"), inck=16)
        e("y", lambda y: y.dma_start(
            out=self.stage[0:1, 16160:16176],
            in_=self.inp["L0_attn_b_eta_b"].unsqueeze(0)),
          waits=[self.tok(self.dsem("eta"))], inc=self.dsem("eta"), inck=16)
        return (c0, c1, c2)

    def load_atw(self, wscol, c_ws, c_war):
        e = self.emit
        atw = self.arena_bf[:, 11136:11392]
        st = self.stage[:, 15392:15424]
        c_all = None
        for dd in range(DC):
            cds = []
            for k, nm in ((0, "L0_attn_b_alpha_w"), (1, "L0_attn_b_theta_w")):
                cds.append(e("y", lambda y, o=st[:, k * 16:(k + 1) * 16],
                             i_=self.inp[nm][dd * 128:(dd + 1) * 128, :]:
                             y.dma_start(out=o, in_=i_),
                             waits=[c_all, c_war, self.gst_free],
                             inc=self.dsem("atw"), inck=16))
            c_all = e("v", lambda v, o=atw[:, dd * 32:(dd + 1) * 32], i_=st,
                      w=wscol[:, dd:dd + 1]:
                      v.tensor_scalar(out=o, in0=i_, scalar1=w, scalar2=None,
                                      op0=ALU.mult),
                      waits=cds + [c_ws, c_all], inc=True)
        br = self.stage[0:1, 15424:15456]
        c_b1 = e("y", lambda y, o=br[:, 0:16],
                 i_=self.inp["L0_attn_b_alpha_b"].unsqueeze(0):
                 y.dma_start(out=o, in_=i_), waits=[c_all],
                 inc=self.dsem("bias"), inck=16)
        c_b2 = e("y", lambda y, o=br[:, 16:32],
                 i_=self.inp["L0_attn_b_theta_b"].unsqueeze(0):
                 y.dma_start(out=o, in_=i_), inc=self.dsem("bias"), inck=16)
        brbf = self.consts[0:1, 648:680]
        c_bb = e("v", lambda v, o=brbf, i_=br: v.tensor_copy(out=o, in_=i_),
                 waits=[c_b1, c_b2], inc=True)
        cbm = e("t", lambda t, o=self.ps[3][:, 0:32], l=self.onesrow,
                r=brbf: t.matmul(o, l, r, start=True, stop=True),
                waits=[c_bb, self.ps_rd[3], self.c_consts], inc=True)
        self.biasB = self.cols[:, 40:72]
        c_bB = e("v", lambda v, o=self.biasB, pp=self.ps[3][:, 0:32]:
                 v.tensor_copy(out=o, in_=pp), waits=[cbm], inc=True)
        self.ps_rd[3] = c_bB
        self.c_biasB = c_bB
        return c_all

    def proj_alpha_theta(self, c_atw, c_xo, c_rn, c_rcols):
        e = self.emit
        at = self.stage_bf[:, 24576:25088]
        atw = self.arena_bf[:, 11136:11392]
        c_ev = []
        for tc in range(SC):
            psi = tc % 4
            pb = self.ps[psi][:, 0:32]
            for dd in range(DC):
                wt = [c_atw, c_xo, self.ps_rd[psi]] if dd == 0 else []
                e("t", lambda t, o=pb,
                  l=self.xbf[:, dd * S + tc * 128: dd * S + (tc + 1) * 128],
                  r=atw[:, dd * 32:(dd + 1) * 32], st=(dd == 0),
                  sp=(dd == DC - 1):
                  t.matmul(o, l, r, start=st, stop=sp),
                  waits=wt, inc=(True if dd == DC - 1 else None))
            cmm = self.tok(self.sem_of["t"])
            cv = e("v", lambda v, o=at[:, tc * 32:(tc + 1) * 32], pp=pb,
                   rn=self.cols[:, tc:tc + 1], bb=self.biasB:
                   v.scalar_tensor_tensor(out=o, in0=pp, scalar=rn, in1=bb,
                                          op0=ALU.mult, op1=ALU.add),
                   waits=[cmm, c_rcols, self.c_biasB], inc=True)
            self.ps_rd[psi] = cv
            c_ev.append(cv)
        c_at = e("s", lambda s, o=at: s.activation(out=o, in_=o,
                                                   func=AF.Sigmoid),
                 waits=c_ev, inc=True)
        return c_at

    def a_mean_row(self, c_at):
        e = self.emit
        at = self.stage_bf[:, 24576:25088]
        onescol_bf = self.ones128[:, 0:1]
        for tc in range(SC):
            wt = [c_at, self.c_consts]
            if tc == 0:
                wt.append(self.ps_rd[3])
            e("t", lambda t, o=self.ps[3][0:1, 0:32],
              l=onescol_bf, r=at[:, tc * 32:(tc + 1) * 32],
              st=(tc == 0), sp=(tc == SC - 1):
              t.matmul(o, l, r, start=st, stop=sp),
              waits=wt, inc=(True if tc == SC - 1 else None))
        cs = self.tok(self.sem_of["t"])
        amr = self.arena[0:1, 5120:5152]
        cv = e("v", lambda v, o=amr, pp=self.ps[3][0:1, 0:32]:
               v.tensor_scalar(out=o, in0=pp, scalar1=1.0 / S, scalar2=None,
                               op0=ALU.mult), waits=[cs], inc=True)
        self.ps_rd[3] = cv
        ambf = self.consts[0:1, 680:712]
        cb = e("v", lambda v, o=ambf, i_=amr: v.tensor_copy(out=o, in_=i_),
               waits=[cv], inc=True)
        selA = self.B64[0:1, 0:128]
        selB = self.consts[0:1, 752:880]
        for p in range(NPAIR):
            for hh in range(2):
                wt = [cb]
                if p == 0 and hh == 0:
                    wt.append(self.ps_rd[2])
                e("t", lambda t, o=self.ps[2][:, 48 + p:49 + p],
                  l=(selA if hh == 0 else selB),
                  r=ambf[:, 2 * p + hh:2 * p + hh + 1],
                  st=(hh == 0), sp=(hh == 1):
                  t.matmul(o, l, r, start=st, stop=sp),
                  waits=wt, inc=(True if (p == NPAIR - 1 and hh == 1) else None))
        cm = self.tok(self.sem_of["t"])
        fam = self.cols[:, 72:80]
        cf = e("v", lambda v, o=fam, pp=self.ps[2][:, 48:56]:
               v.tensor_scalar(out=o, in0=pp, scalar1=-1.0, scalar2=1.0,
                               op0=ALU.mult, op1=ALU.add),
               waits=[cm], inc=True)
        self.ps_rd[2] = cf
        return cf

    def mem_oct_A(self, p, o, c_kf, c_mem):
        e = self.emit
        kf = self.stage_bf[:, 8192:10240]
        par = o % 2
        a1t = self.stage_bf[:, 16384 + par * 1024: 17408 + par * 1024]
        a1f = self.stage_bf[:, 18432 + par * 1024: 19456 + par * 1024]
        gg = self.stage_bf[:, 20480 + par * 1024: 21504 + par * 1024]
        a2 = self.stage_bf[:, 14336:16384]
        w0bd = self.arena_bf[:, 8192:8704]
        w1s = self.arena_bf[:, 8704:8832]
        qs = o * 256
        c_a1f = None
        for k in range(4):
            hh, hc = k // 2, k % 2
            psi = 1 - (k // 2)
            pb = self.ps[psi][:, (k % 2) * 512:(k % 2) * 512 + 256]
            wt = []
            if k == 0:
                wt = [c_kf, c_mem[0], self.ps_rd[1], self.oct_a1f_war[par]]
            elif k == 2:
                wt = [self.ps_rd[0]]
            e("t", lambda t, o2=pb,
              l=w0bd[hh * 64:(hh + 1) * 64,
                     hh * 256 + hc * 128: hh * 256 + (hc + 1) * 128],
              r=kf[hh * 64:(hh + 1) * 64, qs:qs + 256]:
              t.matmul(o2, l, r, start=True, stop=True),
              waits=wt, inc=(k % 2 == 1))
            if k % 2 == 1:
                cmmp = self.tok(self.sem_of["t"])
                psi2 = 1 - (k // 2)
                c_a1f = e("s", lambda s,
                          o2=a1f[:, (k - 1) * 256:(k + 1) * 256].rearrange(
                              "p (b c) -> p b c", b=2),
                          i_=self.ps[psi2].rearrange(
                              "p (b c) -> p b c", b=2)[:, :, 0:256]:
                          s.activation(out=o2, in_=i_, func=AF.Gelu),
                          waits=[cmmp, c_a1f], inc=True)
                self.ps_rd[psi2] = c_a1f
        c_a1t = c_gg = None
        cmm2l = []
        for sc in range(2):
            cch = o * 2 + sc
            psi = 1 - (sc % 2)
            e("t", lambda t, o2=self.ps[psi][:, 0:512],
              l=kf[:, cch * 128:(cch + 1) * 128], r=w0bd:
              t.matmul(o2, l, r, start=True, stop=True),
              waits=[self.ps_rd[psi], self.oct_a1t_war[par],
                     self.oct_gg_war[par]], inc=True)
            cmm2l.append(self.tok(self.sem_of["t"]))
        for sc in range(2):
            psi = 1 - (sc % 2)
            c_a1t = e("s", lambda s,
                      o2=a1t[:, sc * 512:(sc + 1) * 512],
                      i_=self.ps[psi][:, 0:512]:
                      s.activation(out=o2, in_=i_, func=AF.Gelu),
                      waits=[cmm2l[sc]], inc=True)
        for sc in range(2):
            psi = 1 - (sc % 2)
            c_gg = e("s", lambda s,
                     o2=gg[:, sc * 512:(sc + 1) * 512],
                     i_=self.ps[psi][:, 0:512]:
                     s.activation(out=o2, in_=i_, func=AF.Derivative_Gelu),
                     waits=[c_a1t], inc=True)
            self.ps_rd[psi] = c_gg
        for sc in range(2):
            for hh in range(2):
                for hc in range(2):
                    wt = [c_a1f, c_mem[1]]
                    if sc == 0 and hh == 0 and hc == 0:
                        wt.append(self.ps_rd[3])
                    e("t", lambda t,
                      o2=self.ps[3][:, sc * 128 + hh * 64: sc * 128 + (hh + 1) * 64],
                      l=a1f[:, (hh * 2 + hc) * 256 + sc * 128:
                            (hh * 2 + hc) * 256 + (sc + 1) * 128],
                      r=w1s[:, hc * 64:(hc + 1) * 64],
                      st=(hc == 0), sp=(hc == 1):
                      t.matmul(o2, l, r, start=st, stop=sp),
                      waits=wt,
                      inc=(True if (sc == 1 and hh == 1 and hc == 1) else None))
        cmm = self.tok(self.sem_of["t"])
        c_a2 = e("s", lambda s, o2=a2[:, qs:qs + 256], pp=self.ps[3][:, 0:256]:
                 s.activation(out=o2, in_=pp, func=AF.Copy),
                 waits=[cmm, self.oct_a2_war], inc=True)
        c_sq = e("s", lambda s, o2=a1f[:, 0:256], pp=self.ps[3][:, 0:256]:
                 s.activation(out=o2, in_=pp, func=AF.Square),
                 waits=[c_a2], inc=True)
        self.ps_rd[3] = c_sq
        return {"c_a1t": c_a1t, "c_gg": c_gg, "c_a1f": c_a1f, "c_a2": c_a2,
                "c_sq": c_sq, "cmm_a2": cmm}

    def mem_oct_B(self, p, o, st_, c_kt, c_vn, c_at):
        e = self.emit
        par = o % 2
        kt = self.stage_bf[:, 10240:12288]
        vt = self.stage_bf[:, 12288:14336]
        a2 = self.stage_bf[:, 14336:16384]
        at = self.stage_bf[:, 24576:25088]
        tmp = self.stage_bf[:, 18432 + par * 1024: 19456 + par * 1024]
        rmc = self.cols[:, 80:84]
        dyc = self.cols[:, 84:88]
        qs = o * 256
        c_r = e("v", lambda v, o2=rmc.unsqueeze(2),
                i_=tmp[:, 0:256].rearrange("p (g w) -> p g w", w=64):
                v.reduce_sum(out=o2, in_=i_, axis=AX.X),
                waits=[st_["c_sq"]], inc=True)
        c_r = e("s", lambda s, o2=rmc: s.activation(out=o2, in_=o2, func=AF.Ln,
                                                    bias=1e-8, scale=1.0 / HD),
                waits=[c_r], inc=True)
        c_rr = e("s", lambda s, o2=rmc: s.activation(out=o2, in_=o2,
                                                     func=AF.Exp, scale=-0.5),
                 waits=[c_r], inc=True)
        rrB = rmc.unsqueeze(2).broadcast_to([128, 4, 64])
        a2q4 = a2[:, qs:qs + 256].rearrange("p (g w) -> p g w", w=64)
        c_hn = e("v", lambda v: v.tensor_mul(out=a2q4, in0=a2q4, in1=rrB),
                 waits=[c_rr, st_["c_a2"]], inc=True)
        vq = vt[:, qs:qs + 256]
        c_dp = e("v", lambda v, o2=vq, a=a2[:, qs:qs + 256]:
                 v.tensor_add(out=o2, in0=o2, in1=a),
                 waits=[c_hn, c_vn], inc=True)
        thv = at.rearrange("p (c w) -> p c w", w=32)[:, o * 2:o * 2 + 2,
                                                    16 + 2 * p:18 + 2 * p]
        thB = thv.unsqueeze(3).broadcast_to([128, 2, 2, 64])
        vqh = vq.rearrange("p (c h w) -> p c h w", h=2, w=64)
        c_dp = e("v", lambda v: v.tensor_mul(out=vqh, in0=vqh, in1=thB),
                 waits=[c_dp, c_at], inc=True)
        c_dy = e("v", lambda v, o2=tmp[:, 256:512], a=vq,
                 b=a2[:, qs:qs + 256]:
                 v.tensor_mul(out=o2, in0=a, in1=b),
                 waits=[c_dp], inc=True)
        c_dy = e("v", lambda v, o2=dyc,
                 i_=tmp[:, 256:512].rearrange("p (g w) -> p g w", w=64):
                 v.reduce_sum(out=o2.unsqueeze(2), in_=i_, axis=AX.X),
                 waits=[c_dy], inc=True)
        dyB = dyc.unsqueeze(2).broadcast_to([128, 4, 64])
        c_t3 = e("v", lambda v, o2=tmp[:, 512:768].rearrange(
            "p (g w) -> p g w", w=64), a=a2q4:
            v.scalar_tensor_tensor(out=o2, in0=a, scalar=1.0 / HD, in1=dyB,
                                   op0=ALU.mult, op1=ALU.mult),
            waits=[c_dy], inc=True)
        c_dh = e("v", lambda v, o2=vq, b=tmp[:, 512:768]:
                 v.tensor_tensor(out=o2, in0=o2, in1=b, op=ALU.subtract),
                 waits=[c_t3], inc=True)
        vq3 = vq.rearrange("p (g w) -> p g w", w=64)
        c_dh = e("v", lambda v: v.scalar_tensor_tensor(
            out=vq3, in0=vq3, scalar=0.02, in1=rrB, op0=ALU.mult,
            op1=ALU.mult), waits=[c_dh, c_rr], inc=True)
        return c_dh

    def mem_oct_Cfg(self, p, o, c_dh, c_mem):
        e = self.emit
        par = o % 2
        vt = self.stage_bf[:, 12288:14336]
        gg = self.stage_bf[:, 20480 + par * 1024: 21504 + par * 1024]
        dh0 = self.stage_bf[:, 22528 + par * 1024: 23552 + par * 1024]
        dhf = self.stage_bf[:, 31232 + par * 256: 31488 + par * 256]
        w1bd = self.stage_bf[:, 27648:28160]
        ctl = None
        for sc in range(2):
            cch = o * 2 + sc
            wt = [c_dh, self.ps_rd[2] if sc == 0 else None,
                  self.oct_dhf_war[par] if sc == 0 else None]
            ctl = e("t", lambda t, o2=self.ps_bf[2][:, sc * 256:sc * 256 + 128],
                    s=vt[:, cch * 128:(cch + 1) * 128]:
                    t.transpose(out=o2, in_=s, identity=self.ident),
                    waits=wt, inc=(sc == 1))
        c_dhf = e("v", lambda v, o2=dhf.rearrange("p (c f) -> p c f", f=128),
                  pp=self.ps_bf[2][:, 0:512].rearrange(
                      "p (c f) -> p c f", f=256)[:, :, 0:128]:
                  v.tensor_copy(out=o2, in_=pp), waits=[ctl], inc=True)
        self.ps_rd[2] = c_dhf
        c_dh0 = None
        for sc in range(2):
            psi = 1 - (sc % 2)
            psx = self.ps[psi]
            e("t", lambda t, o2=psx[:, 0:512],
              l=dhf[:, sc * 128:(sc + 1) * 128], r=w1bd:
              t.matmul(o2, l, r, start=True, stop=True),
              waits=[c_dhf, c_mem[2], self.ps_rd[psi],
                     self.oct_dh0_war[par] if sc == 0 else None],
              inc=True)
            cmm2 = self.tok(self.sem_of["t"])
            c_dh0 = e("v", lambda v,
                      o2=dh0[:, sc * 512:(sc + 1) * 512],
                      pp=psx[:, 0:512], g2=gg[:, sc * 512:(sc + 1) * 512]:
                      v.tensor_tensor(out=o2, in0=pp, in1=g2, op=ALU.mult),
                      waits=[cmm2, c_dh0], inc=True)
            self.ps_rd[psi] = c_dh0
        self.oct_gg_war[par] = c_dh0
        return c_dh0

    def mem_oct_Ch(self, p, o, st_, c_dh, c_dh0, c_kt):
        e = self.emit
        par = o % 2
        kt = self.stage_bf[:, 10240:12288]
        vt = self.stage_bf[:, 12288:14336]
        a1t = self.stage_bf[:, 16384 + par * 1024: 17408 + par * 1024]
        dh0 = self.stage_bf[:, 22528 + par * 1024: 23552 + par * 1024]
        for sc in range(2):
            cch = o * 2 + sc
            for hh in range(2):
                for hc in range(2):
                    last = (sc == 1 and hh == 1 and hc == 1)
                    e("t", lambda t,
                      o2=self.ps[2][:, 512 + (hh * 2 + hc) * 64:
                                    512 + (hh * 2 + hc + 1) * 64],
                      l=a1t[:, sc * 512 + hh * 256 + hc * 128:
                            sc * 512 + hh * 256 + (hc + 1) * 128],
                      r=vt[:, cch * 128 + hh * 64: cch * 128 + (hh + 1) * 64],
                      st=(o == 0 and sc == 0 and hh == 0 and hc == 0),
                      sp=(o == 7 and sc == 1 and hh == 1 and hc == 1):
                      t.matmul(o2, l, r, start=st, stop=sp),
                      waits=[st_["c_a1t"], c_dh] + (
                          [self.g1_ps_war] if (o == 0 and sc == 0 and
                                               hh == 0 and hc == 0) else []),
                      inc=(True if last else None))
            for hh in range(2):
                wt = [c_dh0, c_kt]
                if o == 0 and sc == 0:
                    wt.append(self.g0_ps_war)
                e("t", lambda t,
                  o2=self.ps[3][0:64, 512 + hh * 256: 768 + hh * 256],
                  l=kt[:, cch * 128 + hh * 64: cch * 128 + (hh + 1) * 64],
                  r=dh0[:, sc * 512 + hh * 256: sc * 512 + (hh + 1) * 256],
                  st=(o == 0 and sc == 0 and hh == 0),
                  sp=(o == 7 and sc == 1 and hh == 1):
                  t.matmul(o2, l, r, start=st, stop=sp),
                  waits=wt, inc=(o == 7 and sc == 1 and hh == 1))
        c_gmm = self.tok(self.sem_of["t"])
        self.oct_a1t_war[par] = c_gmm
        self.oct_a1f_war[par] = c_dh
        self.oct_a2_war = c_gmm
        self.oct_dhf_war[par] = c_gmm
        self.oct_dh0_war[par] = c_gmm
        return c_gmm

    def grad_pairs(self, c_wk, c_wv, c_mem, c_xo, c_rn, c_knw, c_at, c_am):
        e = self.emit
        wk = self.arena_bf[:, 0:8192]
        wv = self.stage_bf[:, 0:8192]
        kf = self.stage_bf[:, 8192:10240]
        kt = self.stage_bf[:, 10240:12288]
        vt = self.stage_bf[:, 12288:14336]
        gg = self.stage_bf[:, 20480:22528]
        c_nw0 = c_nw1 = None
        c_kf_rd = None
        c_kt_war = None
        c_vt_war = None
        self.oct_a1t_war = {0: None, 1: None}
        self.oct_a1f_war = {0: None, 1: None}
        self.oct_gg_war = {0: None, 1: None}
        self.oct_a2_war = None
        self.oct_dhf_war = {0: None, 1: None}
        self.oct_dh0_war = {0: None, 1: None}
        self.g0_ps_war = self.ps_rd[3]
        self.g1_ps_war = self.ps_rd[2]
        self.g0sb_war = None
        npair = 1 if (_TRUNC[0] or "").startswith("b2") else NPAIR
        stepmax = 99
        tr = _TRUNC[0] or ""
        if tr.startswith("b2s"):
            stepmax = int(tr[3:])
        for p in range(npair):
            csqs = []
            ck = None
            for tt in range(4):
                pb = self.ps[0][:, (tt % 2) * 512:(tt % 2) * 512 + 512]
                for dd in range(DC):
                    wt = [c_wk, c_xo, self.ps_rd[0], c_kf_rd] if dd == 0 else []
                    e("t", lambda t, o=pb,
                      l=wk[:, dd * 1024 + p * 128: dd * 1024 + (p + 1) * 128],
                      r=self.xbf[:, dd * S + tt * 512: dd * S + tt * 512 + 512],
                      st=(dd == 0), sp=(dd == DC - 1):
                      t.matmul(o, l, r, start=st, stop=sp),
                      waits=wt, inc=(True if dd == DC - 1 else None))
                cmm = self.tok(self.sem_of["t"])
                ck = e("v", lambda v, o=kf[:, tt * 512:(tt + 1) * 512], pp=pb,
                       rn=self.bcast[:, tt * 512:(tt + 1) * 512]:
                       v.scalar_tensor_tensor(out=o, in0=pp, scalar=1.0,
                                              in1=rn, op0=ALU.mult,
                                              op1=ALU.mult),
                       waits=[cmm, ck] + (list(c_rn) if tt == 0 else []),
                       inc=True)
                self.ps_rd[0] = ck
                csqs.append(e("v", lambda v, o=gg[:, tt * 512:(tt + 1) * 512],
                              i_=kf[:, tt * 512:(tt + 1) * 512]:
                              v.tensor_mul(out=o, in0=i_, in1=i_),
                              waits=[ck] + ([self.oct_gg_war[0],
                                             self.oct_gg_war[1]]
                                            if tt == 0 else []),
                              inc=True))
            self.bcast0_war = ck
            if stepmax < 1:
                return None, None
            cnk = None
            for half in range(2):
                for sub in range(2):
                    wt = [csqs[half * 2 + sub], self.c_consts]
                    if sub == 0:
                        wt.append(self.ps_rd[1])
                    e("t", lambda t, o=self.ps[1][:, sub * 512:(sub + 1) * 512],
                      l=self.B64,
                      r=gg[:, half * 1024 + sub * 512: half * 1024 + (sub + 1) * 512]:
                      t.matmul(o, l, r, start=True, stop=True),
                      waits=wt, inc=(sub == 1))
                cbm = self.tok(self.sem_of["t"])
                cl = e("s", lambda s, h2=half: s.activation(
                    out=self.bcast[:, 2048 + h2 * 1024: 2048 + (h2 + 1) * 1024],
                    in_=self.ps[1][:, 0:1024],
                    func=AF.Ln, bias=1e-6, scale=1.0 / HD),
                    waits=[cbm, self.rk_war], inc=True)
                self.ps_rd[1] = cl
                cr = e("s", lambda s, h2=half: s.activation(
                    out=self.bcast[:, 2048 + h2 * 1024: 2048 + (h2 + 1) * 1024],
                    in_=self.bcast[:, 2048 + h2 * 1024: 2048 + (h2 + 1) * 1024],
                    func=AF.Exp, scale=-0.5), waits=[cl], inc=True)
                cnk = e("v", lambda v, o=kf[:, half * 1024:(half + 1) * 1024],
                        w=self.cols[:, 32:33],
                        r=self.bcast[:, 2048 + half * 1024: 2048 + (half + 1) * 1024]:
                        v.scalar_tensor_tensor(out=o, in0=o, scalar=w, in1=r,
                                               op0=ALU.mult, op1=ALU.mult),
                        waits=[cr, c_knw, cnk], inc=True)
                self.rk_war = cnk
            if stepmax < 2:
                return None, None
            vfst = gg
            cvf = [None] * 4
            for tt in range(4):
                pb = self.ps[3][:, (tt % 2) * 512:(tt % 2) * 512 + 512]
                for dd in range(DC):
                    if dd == 0:
                        if tt == 0:
                            wt = [c_wv, c_xo, self.ps_rd[3]]
                        elif tt == 1:
                            wt = [self.g0_ps_war]
                        else:
                            wt = [cvf[tt - 2]]
                    else:
                        wt = []
                    e("t", lambda t, o=pb,
                      l=wv[:, dd * 1024 + p * 128: dd * 1024 + (p + 1) * 128],
                      r=self.xbf[:, dd * S + tt * 512: dd * S + tt * 512 + 512],
                      st=(dd == 0), sp=(dd == DC - 1):
                      t.matmul(o, l, r, start=st, stop=sp),
                      waits=wt, inc=(True if dd == DC - 1 else None))
                cmm = self.tok(self.sem_of["t"])
                cvf[tt] = e("v", lambda v, o=vfst[:, tt * 512:(tt + 1) * 512],
                            pp=pb: v.tensor_copy(out=o, in_=pp),
                            waits=[cmm, cvf[tt - 1] if tt else None], inc=True)
                self.ps_rd[3] = cvf[tt]
            self.g0_ps_war = cvf[3]
            cv2 = None
            for g in range(4):
                tl = None
                for i in range(4):
                    cch = g * 4 + i
                    wt = [cvf[g], self.ps_rd[2] if i == 0 else None]
                    tl = e("t", lambda t,
                           o=self.ps_bf[2][:, i * 256:i * 256 + 128],
                           s=vfst[:, cch * 128:(cch + 1) * 128]:
                           t.transpose(out=o, in_=s, identity=self.ident),
                           waits=wt, inc=(i == 3))
                cv2 = e("v", lambda v,
                        o=vt.rearrange("p (c f) -> p c f", f=128)[:, g * 4:(g + 1) * 4, :],
                        pp=self.ps_bf[2][:, 0:1024].rearrange(
                            "p (c f) -> p c f", f=256)[:, :, 0:128]:
                        v.tensor_copy(out=o, in_=pp),
                        waits=[tl, cv2, c_vt_war if g == 0 else None],
                        inc=True)
                self.ps_rd[2] = cv2
            rnview = self.cols[:, 0:16].unsqueeze(2).broadcast_to([128, 16, 128])
            cvn = e("v", lambda v,
                    o=vt.rearrange("p (c f) -> p c f", f=128):
                    v.tensor_mul(out=o, in0=o, in1=rnview),
                    waits=[cv2, self.c_rcols_tok], inc=True)

            ctk = None
            for g in range(4):
                tl = None
                for i in range(4):
                    cch = g * 4 + i
                    wt = [cnk, self.ps_rd[2] if i == 0 else None,
                          c_kt_war if (g == 0 and i == 0) else None]
                    tl = e("t", lambda t,
                           o=self.ps_bf[2][:, i * 256:i * 256 + 128],
                           s=kf[:, cch * 128:(cch + 1) * 128]:
                           t.transpose(out=o, in_=s, identity=self.ident),
                           waits=wt, inc=(i == 3))
                ctk = e("v", lambda v,
                        o=kt.rearrange("p (c f) -> p c f", f=128)[:, g * 4:(g + 1) * 4, :],
                        pp=self.ps_bf[2][:, 0:1024].rearrange(
                            "p (c f) -> p c f", f=256)[:, :, 0:128]:
                        v.tensor_copy(out=o, in_=pp), waits=[tl, ctk], inc=True)
                self.ps_rd[2] = ctk
                cvn = e("v", lambda v, o=vt[:, g * 512:(g + 1) * 512],
                        b=kt[:, g * 512:(g + 1) * 512]:
                        v.tensor_tensor(out=o, in0=b, in1=o, op=ALU.subtract),
                        waits=[ctk, cvn], inc=True)
            if stepmax < 3:
                return None, None
            if stepmax < 4:
                return None, None
            nocts = 8
            if (_TRUNC[0] or "").startswith("b2q"):
                nocts = 2 * (int(_TRUNC[0][3:]) + 1)
            sts, chs, dh0s = {}, {}, {}
            cg = None
            for o in range(nocts):
                sts[o] = self.mem_oct_A(p, o, cnk, c_mem)
                if o >= 1:
                    dh0s[o - 1] = self.mem_oct_Cfg(p, o - 1, chs[o - 1], c_mem)
                chs[o] = self.mem_oct_B(p, o, sts[o], ctk, cvn, c_at)
                if o >= 1:
                    cg = self.mem_oct_Ch(p, o - 1, sts[o - 1], chs[o - 1],
                                         dh0s[o - 1], ctk)
            ol = nocts - 1
            dh0s[ol] = self.mem_oct_Cfg(p, ol, chs[ol], c_mem)
            cg = self.mem_oct_Ch(p, ol, sts[ol], chs[ol], dh0s[ol], ctk)
            if (_TRUNC[0] or "").startswith("b2q"):
                return None, None
            if stepmax < 8:
                return None, None
            c_nw0, c_nw1 = self.clip_pair(p, cg, c_am, c_mem)
            c_kf_rd = cg
            c_kt_war = cg
            c_vt_war = cg
        return c_nw0, c_nw1

    rk_war = None

    def clip_pair(self, p, c_gmm, c_am, c_mem):
        e = self.emit
        w0bd = self.arena_bf[:, 8192:8704]
        w1s = self.arena_bf[:, 8704:8832]
        nw0 = self.arena_bf[:, 8832:10880]
        nw1 = self.stage_bf[:, 25088:27136]
        gt0 = self.stage[:, 14592:14848]
        gt1 = self.stage[:, 14848:15104]
        tmp = self.stage[:, 15104:15360]
        fam = self.cols[:, 72:80]
        cga = e("v", lambda v, o=gt0[0:64, :], pp=self.ps[3][0:64, 512:768]:
                v.tensor_copy(out=o, in_=pp),
                waits=[c_gmm, self.gt_war], inc=True)
        cga = e("v", lambda v, o=gt0[64:128, :], pp=self.ps[3][0:64, 768:1024]:
                v.tensor_copy(out=o, in_=pp), waits=[cga], inc=True)
        cgb = e("v", lambda v, o=gt1, pp=self.ps[2][:, 512:768]:
                v.tensor_copy(out=o, in_=pp), waits=[cga], inc=True)
        self.g0_ps_war = cga
        self.g1_ps_war = cgb
        self.ps_rd[2] = cgb
        csq = e("v", lambda v, o=tmp, a=gt0: v.tensor_mul(out=o, in0=a, in1=a),
                waits=[cga], inc=True)
        cc0 = e("v", lambda v, o=self.cols[:, 88:89].unsqueeze(2),
                i_=tmp.rearrange("p (g w) -> p g w", g=1):
                v.reduce_sum(out=o, in_=i_, axis=AX.X), waits=[csq], inc=True)
        csq = e("v", lambda v, o=tmp, a=gt1: v.tensor_mul(out=o, in0=a, in1=a),
                waits=[cgb, cc0], inc=True)
        cc1 = e("v", lambda v, o=self.cols[:, 89:91].unsqueeze(2),
                i_=tmp.rearrange("p (g w) -> p g w", g=2):
                v.reduce_sum(out=o, in_=i_, axis=AX.X), waits=[csq], inc=True)
        ccb = e("v", lambda v, o=self.consts[:, 712:715],
                i_=self.cols[:, 88:91]: v.tensor_copy(out=o, in_=i_),
                waits=[cc0, cc1], inc=True)
        scol = self.consts[:, 712:715]
        for i, (lh, rc) in enumerate((
                (self.B64[:, 0:1], 0), (self.B64[:, 64:65], 0),
                (self.ones128[:, 0:1], 1), (self.ones128[:, 0:1], 2))):
            wt = [ccb, self.c_consts]
            if i == 0:
                wt.append(self.ps_rd[3])
            e("t", lambda t, o=self.ps[3][0:1, i:i + 1], l=lh,
              r=scol[:, rc:rc + 1]:
              t.matmul(o, l, r, start=True, stop=True),
              waits=wt, inc=(i == 3))
        cmm = self.tok(self.sem_of["t"])
        sqr = self.stage[0:1, 15488:15504]
        c_q = e("v", lambda v, o=sqr[:, 0:4], pp=self.ps[3][0:1, 0:4]:
                v.tensor_copy(out=o, in_=pp), waits=[cmm], inc=True)
        self.ps_rd[3] = c_q
        c_q = e("v", lambda v, o=sqr[:, 4:6],
                a=sqr.rearrange("o (g w) -> o g w", w=2)[:, 0:1, :].squeeze(1),
                b=sqr.rearrange("o (g w) -> o g w", w=2)[:, 1:2, :].squeeze(1):
                v.tensor_add(out=o, in0=a, in1=b), waits=[c_q], inc=True)
        c_q = e("s", lambda s, o=sqr[:, 4:6]: s.activation(
            out=o, in_=o, func=AF.Sqrt), waits=[c_q], inc=True)
        c_q = e("v", lambda v, o=sqr[:, 4:6]: v.tensor_scalar(
            out=o, in0=o, scalar1=1e-6, scalar2=None, op0=ALU.add),
            waits=[c_q], inc=True)
        c_q = e("v", lambda v, o=sqr[:, 4:6]: v.reciprocal(out=o, in_=o),
                waits=[c_q], inc=True)
        c_q = e("v", lambda v, o=sqr[:, 4:6]: v.tensor_scalar(
            out=o, in0=o, scalar1=-10.0, scalar2=-1.0, op0=ALU.mult,
            op1=ALU.max), waits=[c_q], inc=True)
        cfbf = self.consts[0:1, 720:722]
        c_cf = e("v", lambda v, o=cfbf, i_=sqr[:, 4:6]:
                 v.tensor_copy(out=o, in_=i_), waits=[c_q], inc=True)
        for hh in range(2):
            wt = [c_cf, self.c_consts]
            if hh == 0:
                wt.append(self.ps_rd[3])
            e("t", lambda t, o=self.ps[3][:, 8:9],
              l=(self.B64[0:1, 0:128] if hh == 0 else
                 self.consts[0:1, 752:880]),
              r=cfbf[:, hh:hh + 1], st=(hh == 0), sp=(hh == 1):
              t.matmul(o, l, r, start=st, stop=sp),
              waits=wt, inc=(hh == 1))
        ambf = self.consts[0:1, 680:712]
        for i2, cell in enumerate((cfbf[:, 0:1], cfbf[:, 1:2],
                                   ambf[:, 2 * p:2 * p + 1],
                                   ambf[:, 2 * p + 1:2 * p + 2])):
            e("t", lambda t, o=self.ps[3][:, 12 + i2:13 + i2],
              l=self.ones128[0:1, :], r=cell:
              t.matmul(o, l, r, start=True, stop=True),
              waits=([c_cf] if i2 == 0 else []), inc=(i2 == 3))
        cmm2 = self.tok(self.sem_of["t"])
        ncf = self.cols[:, 91:92]
        c_cc = e("v", lambda v, o=ncf, pp=self.ps[3][:, 8:9]:
                 v.tensor_copy(out=o, in_=pp), waits=[cmm2], inc=True)
        ncfu = self.cols[:, 92:94]
        c_cu = e("v", lambda v, o=ncfu, pp=self.ps[3][:, 12:14]:
                 v.tensor_copy(out=o, in_=pp), waits=[c_cc], inc=True)
        famu = self.cols[:, 94:96]
        c_fu = e("v", lambda v, o=famu, pp=self.ps[3][:, 14:16]:
                 v.tensor_scalar(out=o, in0=pp, scalar1=-1.0, scalar2=1.0,
                                 op0=ALU.mult, op1=ALU.add),
                 waits=[c_cu], inc=True)
        self.ps_rd[3] = c_fu
        c_tm = e("v", lambda v, o=tmp, i_=gt0, cc=ncf:
                 v.tensor_scalar(out=o, in0=i_, scalar1=cc, scalar2=None,
                                 op0=ALU.mult), waits=[c_cc], inc=True)
        c_nw0 = None
        for hh in range(2):
            c_nw0 = e("v", lambda v,
                      o=nw0[hh * 64:(hh + 1) * 64, p * 256:(p + 1) * 256],
                      w=w0bd[hh * 64:(hh + 1) * 64, hh * 256:(hh + 1) * 256],
                      f=fam[hh * 64:(hh + 1) * 64, p:p + 1],
                      tm=tmp[hh * 64:(hh + 1) * 64, :]:
                      v.scalar_tensor_tensor(out=o, in0=w, scalar=f, in1=tm,
                                             op0=ALU.mult, op1=ALU.add),
                      waits=[c_tm, c_am, c_mem[0], c_nw0], inc=True)
        tmpw2 = self.stage[:, 15360:15488]
        c_nw1 = None
        for h2 in range(2):
            c_tm = e("v", lambda v,
                     o=tmp[:, h2 * 128:(h2 + 1) * 128],
                     i_=gt1[:, h2 * 128:(h2 + 1) * 128], cc=ncfu[:, h2:h2 + 1]:
                     v.tensor_scalar(out=o, in0=i_, scalar1=cc, scalar2=None,
                                     op0=ALU.mult),
                     waits=[c_nw0, cgb, c_cu], inc=True)
            c_f2 = e("v", lambda v, o=tmpw2, i_=w1s, f=famu[:, h2:h2 + 1]:
                     v.tensor_scalar(out=o, in0=i_, scalar1=f, scalar2=None,
                                     op0=ALU.mult),
                     waits=[c_tm, c_fu, c_mem[1], self.f2_war], inc=True)
            c_nw1 = e("v", lambda v,
                      o=nw1[:, p * 256 + h2 * 128: p * 256 + (h2 + 1) * 128],
                      tm=tmp[:, h2 * 128:(h2 + 1) * 128], w2=tmpw2:
                      v.tensor_add(out=o, in0=tm, in1=w2),
                      waits=[c_f2, self.nw1_war, c_nw1], inc=True)
            self.f2_war = c_nw1
        self.gt_war = c_nw1
        self.f2_war = c_nw1
        if p == 0:
            self.tap("gt0", gt0, [c_nw0])
            self.tap("gt1", gt1, [c_nw1])
        return c_nw0, c_nw1

    gt_war = None
    vf_war = None
    rsq_war = None
    f2_war = None
    nw1_war = None
    nw1_war = None

    def retrieve_pairs(self, c_nw0, c_nw1, c_xo, c_rn, c_wr, wrcol, c_qnw,
                       barrier):
        e = self.emit
        wq = self.arena_bf[:, 0:8192]
        qfb = [self.stage_bf[:, 8192:10240], self.stage_bf[:, 4608:6656]]
        nw0 = self.arena_bf[:, 8832:10880]
        nw1 = self.stage_bf[:, 25088:27136]
        h0f = self.stage_bf[:, 16384:24576]
        sqr = self.stage_bf[:, 20480:22528] if False else None
        c_wq = self.load_w1024("L0_attn_b_W_Q", wq, wrcol, c_wr,
                               c_war=c_nw1)
        c_qf_rd = None
        c_h0_war = [None] * 4
        tmpsq = self.stage_bf[:, 2048:4096]
        self.qf_ck = None
        self.qf_sq = {}
        self.qrms_war = None
        self.sq2_war = [None, None]
        self.ps_rd_1h = [self.ps_rd[1], self.ps_rd[1]]

        def emit_qf(p2, tt):
            qf2 = qfb[p2 % 2]
            pb = self.ps[0][:, (tt % 2) * 512:(tt % 2) * 512 + 512]
            for dd in range(DC):
                wt = [c_wq, c_xo, self.ps_rd[0], c_qf_rd] if dd == 0 else []
                e("t", lambda t, o=pb,
                  l=wq[:, dd * 1024 + p2 * 128: dd * 1024 + (p2 + 1) * 128],
                  r=self.xbf[:, dd * S + tt * 512: dd * S + tt * 512 + 512],
                  st=(dd == 0), sp=(dd == DC - 1):
                  t.matmul(o, l, r, start=st, stop=sp),
                  waits=wt, inc=(True if dd == DC - 1 else None))
            cmm = self.tok(self.sem_of["t"])
            ck = e("v", lambda v, o=qf2[:, tt * 512:(tt + 1) * 512], pp=pb,
                   rn=self.bcast[:, tt * 512:(tt + 1) * 512]:
                   v.scalar_tensor_tensor(out=o, in0=pp, scalar=1.0,
                                          in1=rn, op0=ALU.mult,
                                          op1=ALU.mult),
                   waits=[cmm, self.qf_ck] + list(c_rn), inc=True)
            self.qf_ck = ck
            self.ps_rd[0] = ck
            self.qf_sq[(p2, tt)] = e(
                "v", lambda v, o=tmpsq[:, tt * 512:(tt + 1) * 512],
                i_=qf2[:, tt * 512:(tt + 1) * 512]:
                v.tensor_mul(out=o, in0=i_, in1=i_),
                waits=[ck, self.qrms_war], inc=True)
            return ck

        def emit_fin(p2, qf2, cnk2, tt, p1q, cev, csq2):
            wt = [csq2, self.ps_rd_1h[tt % 2]]
            e("t", lambda t, o=self.ps[1][:, (tt % 2) * 512:
                                          (tt % 2) * 512 + 512],
              l=self.B64,
              r=self.stage_bf[:, 1024 + (tt % 2) * 512:
                              1536 + (tt % 2) * 512]:
              t.matmul(o, l, r, start=True, stop=True),
              waits=wt, inc=True)
            self.sq2_war[tt % 2] = self.tok(self.sem_of["t"])
            cbm = self.tok(self.sem_of["t"])
            r2B = self.bcast[:, 2048 + (tt % 2) * 512:
                             2048 + (tt % 2) * 512 + 512]
            cl = e("s", lambda s, o=r2B,
                   i_=self.ps[1][:, (tt % 2) * 512:(tt % 2) * 512 + 512]:
                   s.activation(out=o, in_=i_, func=AF.Ln, bias=1e-8,
                                scale=1.0 / HD),
                   waits=[cbm, self.rk_war], inc=True)
            self.ps_rd_1h[tt % 2] = cl
            self.ps_rd[1] = cl
            cr = e("s", lambda s, o=r2B: s.activation(
                out=o, in_=o, func=AF.Exp, scale=-0.5),
                waits=[cl], inc=True)
            tmpo = self.stage_bf[:, 4096:4608]
            co = e("v", lambda v, o=tmpo, a=p1q, b=r2B:
                   v.tensor_mul(out=o, in0=a, in1=b),
                   waits=[cr, self.tmpo_war], inc=True)
            co = e("v", lambda v, o=tmpo, b=qf2[:, tt * 512:(tt + 1) * 512]:
                   v.tensor_add(out=o, in0=o, in1=b),
                   waits=[co, cnk2], inc=True)
            cx = e("v", lambda v,
                   o=self.xf[:, p2 * S + tt * 512: p2 * S + tt * 512 + 512],
                   b=tmpo: v.tensor_add(out=o, in0=o, in1=b),
                   waits=[co, self.c_x_all[p2]], inc=True)
            self.c_x_all[p2] = cx
            self.tmpo_war = cx
            self.p1q_war[tt % 2] = csq2
            self.rk_war = co
            return cev, cr, p1q, r2B

        pend = []
        for p in range(NPAIR):
            qf = qfb[p % 2]
            for tt in range(4):
                if (p, tt) not in self.qf_sq:
                    emit_qf(p, tt)
            ck = self.qf_ck
            cnk = None
            for half in range(2):
                for sub in range(2):
                    wt = [self.qf_sq[(p, half * 2 + sub)], self.c_consts]
                    if sub == 0:
                        wt.append(self.ps_rd[1])
                    e("t", lambda t, o=self.ps[1][:, sub * 512:(sub + 1) * 512],
                      l=self.B64,
                      r=tmpsq[:, half * 1024 + sub * 512: half * 1024 + (sub + 1) * 512]:
                      t.matmul(o, l, r, start=True, stop=True),
                      waits=wt, inc=(sub == 1))
                cbm = self.tok(self.sem_of["t"])
                cl = e("s", lambda s, h2=half: s.activation(
                    out=self.bcast[:, 2048 + h2 * 1024: 2048 + (h2 + 1) * 1024],
                    in_=self.ps[1][:, 0:1024],
                    func=AF.Ln, bias=1e-6, scale=1.0 / HD),
                    waits=[cbm, self.rk_war], inc=True)
                self.ps_rd[1] = cl
                cr = e("s", lambda s, h2=half: s.activation(
                    out=self.bcast[:, 2048 + h2 * 1024: 2048 + (h2 + 1) * 1024],
                    in_=self.bcast[:, 2048 + h2 * 1024: 2048 + (h2 + 1) * 1024],
                    func=AF.Exp, scale=-0.5), waits=[cl], inc=True)
                cnk = e("v", lambda v, o=qf[:, half * 1024:(half + 1) * 1024],
                        w=self.cols[:, 33:34],
                        r=self.bcast[:, 2048 + half * 1024: 2048 + (half + 1) * 1024]:
                        v.scalar_tensor_tensor(out=o, in0=o, scalar=w, in1=r,
                                               op0=ALU.mult, op1=ALU.mult),
                        waits=[cr, c_qnw, cnk], inc=True)
                self.rk_war = cnk
            self.qrms_war = self.tok(self.sem_of["t"])
            self.ps_rd_1h = [self.ps_rd[1], self.ps_rd[1]]
            h0ps = [self.ps[2][:, 0:512], self.ps[2][:, 512:1024],
                    self.ps[0][:, 0:512], self.ps[0][:, 512:1024]]
            c_h0 = [None] * 4

            def emit_h0(tt):
                for k in range(4):
                    hh, hc = k // 2, k % 2
                    pb = h0ps[k]
                    if tt == 0:
                        wt = [cnk, c_nw0, c_h0_war[k],
                              self.ps_rd[2] if k == 0 else None,
                              self.ps_rd[0] if k == 2 else None]
                    else:
                        wt = [c_h0[k]]
                    cmmi = e("t", lambda t, o=pb,
                             l=nw0[hh * 64:(hh + 1) * 64,
                                   p * 256 + hc * 128: p * 256 + (hc + 1) * 128],
                             r=qf[hh * 64:(hh + 1) * 64, tt * 512:(tt + 1) * 512]:
                             t.matmul(o, l, r, start=True, stop=True),
                             waits=wt, inc=True)
                    c_h0[k] = e("s", lambda s,
                                o=h0f[:, k * 2048 + tt * 512: k * 2048 + (tt + 1) * 512],
                                i_=pb: s.activation(out=o, in_=i_, func=AF.Gelu),
                                waits=[cmmi, c_h0[k]], inc=True)
                self.ps_rd[2] = c_h0[1]
                self.ps_rd[0] = c_h0[3]

            def emit_pre1(tt):
                pb = self.ps[3][:, (tt % 2) * 512:(tt % 2) * 512 + 512]
                for h2 in range(2):
                    for hc in range(2):
                        wt = [c_h0[h2 * 2 + hc], c_nw1]
                        if h2 == 0 and hc == 0:
                            wt.append(self.ps_rd[3])
                        e("t", lambda t, o=pb[h2 * 64:(h2 + 1) * 64, :],
                          l=nw1[:, p * 256 + (h2 * 2 + hc) * 64:
                                p * 256 + (h2 * 2 + hc + 1) * 64],
                          r=h0f[:, (h2 * 2 + hc) * 2048 + tt * 512:
                                (h2 * 2 + hc) * 2048 + (tt + 1) * 512],
                          st=(hc == 0), sp=(hc == 1):
                          t.matmul(o, l, r, start=st, stop=sp),
                          waits=wt, inc=(True if (h2 == 1 and hc == 1) else None))
                cmm = self.tok(self.sem_of["t"])
                p1q = self.stage_bf[:, (tt % 2) * 512:(tt % 2) * 512 + 512]
                cev = e("v", lambda v, o=p1q, pp=pb: v.tensor_copy(out=o, in_=pp),
                        waits=[cmm, self.p1q_war[tt % 2]], inc=True)
                self.ps_rd[3] = cev
                sq2 = self.stage_bf[:, 1024 + (tt % 2) * 512:
                                    1536 + (tt % 2) * 512]
                csq2 = e("v", lambda v, o=sq2, i_=p1q:
                         v.tensor_mul(out=o, in0=i_, in1=i_),
                         waits=[cev, self.sq2_war[tt % 2]], inc=True)
                return pb, p1q, cev, csq2

            if pend:
                emit_fin(*pend[0])
            emit_h0(0)
            if len(pend) > 1:
                emit_fin(*pend[1])
            pend = []
            emit_h0(1)
            emit_h0(2)
            pr = {}
            pr[0] = emit_pre1(0)
            pr[1] = emit_pre1(1)
            emit_h0(3)
            emit_fin(p, qf, cnk, 0, pr[0][1], pr[0][2], pr[0][3])
            emit_fin(p, qf, cnk, 1, pr[1][1], pr[1][2], pr[1][3])
            if p + 1 < NPAIR:
                emit_qf(p + 1, 0)
            pr[2] = emit_pre1(2)
            pr[3] = emit_pre1(3)
            if p + 1 < NPAIR:
                emit_qf(p + 1, 1)
            c_qf_rd = self.tok(self.sem_of["t"])
            for k in range(4):
                c_h0_war[k] = self.tok(self.sem_of["t"])
            self.ret_tmp_war = self.tok(self.sem_of["t"])
            self.bcast0_war = ck
            if p + 1 < NPAIR:
                pend = [(p, qf, cnk, 2, pr[2][1], pr[2][2], pr[2][3]),
                        (p, qf, cnk, 3, pr[3][1], pr[3][2], pr[3][3])]
            else:
                emit_fin(p, qf, cnk, 2, pr[2][1], pr[2][2], pr[2][3])
                _, cr, p1q, r2B = emit_fin(p, qf, cnk, 3, pr[3][1], pr[3][2],
                                           pr[3][3])
                if p == int(os.environ.get("TAPPAIR", NPAIR - 1)):
                    tpq = self.tap("p1q3", p1q, [pr[3][2]])
                    trb = self.tap("r2b3", r2B, [cr])
                    if trb is not None:
                        self.p1q_war[3 % 2] = trb
                        self.rk_war = trb
        self.c_x = self.c_x_all[-1]

    ret_tmp_war = None
    tmpP_war = None
    tmpo_war = None
    ps_rd_1h = None
    sq2_war = None

    def emit_store(self):
        e = self.emit
        for c in range(SC):
            pi = 2 + (c % 2)
            pb = self.ps[pi]
            tlast = None
            for dd in range(DC):
                src = self.xf[:, dd * S + c * 128: dd * S + (c + 1) * 128]
                w = [self.c_xf, self.ps_rd[pi]] if dd == 0 else []
                tlast = e("t", lambda t, o=pb[:, dd * 128:(dd + 1) * 128], s=src:
                          t.transpose(out=o, in_=s, identity=self.ident32),
                          waits=w, inc=(True if dd == DC - 1 else None))
            self.ps_wr[pi] = tlast
            dst = self.arena[:, (c % 2) * 1024:(c % 2) * 1024 + 1024]
            war = self._store_dma[c - 2] if c >= 2 else None
            cv = e("v", lambda v, d=dst, p=pb[:, :]: v.tensor_copy(out=d, in_=p),
                   waits=[tlast, war], inc=True)
            self.ps_rd[pi] = cv
            dma = e("y", lambda y, c=c, d=dst: y.dma_start(
                out=self.out.rearrange("(c p) f -> c p f", p=128)[c], in_=d),
                waits=[cv], inc=self.dsem("st" + str(c % 2)), inck=16)
            self._store_dma.append(dma)
        tailw = []
        for s2m in self._dma_sems.values():
            if self.ctr[s2m] > 0:
                tailw.append((s2m, self.ctr[s2m]))
        e("y", lambda y: y.nop(), waits=tailw)

    _store_dma = None

    def build_full(self):
        self._store_dma = []
        return self.build()


def build_program(dbg=()):
    b = Builder(dbg)
    nc = b.build_full()
    return nc, b


def kernel(**inputs):
    nc, b = build_program()
    xs = np.ascontiguousarray(inputs["L0_attn_a_x"], dtype=np.float32)
    in_maps = []
    for core in range(8):
        m = {}
        for name, _ in IN_SPECS:
            if name == "L0_attn_a_x":
                m[name] = np.ascontiguousarray(xs[core])
            else:
                m[name] = np.ascontiguousarray(inputs[name], dtype=np.float32)
        in_maps.append(m)
    res = run_bass_kernel_spmd(nc, in_maps, core_ids=list(range(8)))
    return np.stack([np.asarray(r["out"]) for r in res.results], axis=0)


# version 30 (final)
# speedup vs baseline: 1.0358x; 1.0165x over previous
import os
import numpy as np
import concourse.bass as bass
import concourse.mybir as mybir
from concourse.bass_utils import run_bass_kernel_spmd

F32 = mybir.dt.float32
I32 = mybir.dt.int32
BF16 = mybir.dt.bfloat16
AF = mybir.ActivationFunctionType
ALU = mybir.AluOpType
AX = mybir.AxisListType

S, D, H, HD, HID = 2048, 1024, 16, 64, 256
RANK, DFF = 256, 4096
SC, DC = S // 128, D // 128
NPAIR = H // 2

IN_SPECS = [
    ("L0_attn_a_x", [S, D]), ("L0_attn_a_Wq", [D, D]), ("L0_attn_a_Wk", [D, D]),
    ("L0_attn_a_Wv", [D, D]), ("L0_attn_a_Wo", [D, D]), ("L0_attn_a_gamma", [D]),
    ("L0_attn_a_beta", [D]), ("L0_attn_b_W_K", [D, D]), ("L0_attn_b_W_V", [D, D]),
    ("L0_attn_b_W_Q", [D, D]), ("L0_attn_b_mem_W0", [HD, HID]),
    ("L0_attn_b_mem_W1", [HID, HD]), ("L0_attn_b_k_norm_w", [HD]),
    ("L0_attn_b_q_norm_w", [HD]), ("L0_attn_b_store_norm_w", [D]),
    ("L0_attn_b_retrieve_norm_w", [D]), ("L0_attn_b_alpha_w", [D, H]),
    ("L0_attn_b_alpha_b", [H]), ("L0_attn_b_theta_w", [D, H]),
    ("L0_attn_b_theta_b", [H]), ("L0_attn_b_eta_w", [D, H]), ("L0_attn_b_eta_b", [H]),
    ("L0_ffn_U1", [D, RANK]), ("L0_ffn_V1", [RANK, DFF]), ("L0_ffn_b1", [DFF]),
    ("L0_ffn_U2", [DFF, RANK]), ("L0_ffn_V2", [RANK, D]), ("L0_ffn_b2", [D]),
]

ENGKEYS = ("t", "v", "s", "g", "y")
_TRUNC = [None]


class Builder:
    def __init__(self, dbg=()):
        self.c_x_all = [None] * 8
        nc = bass.Bass("TRN2", target_bir_lowering=False)
        self.nc = nc
        self.dbg = list(dbg)
        self.inp = {}
        for n, sh in IN_SPECS:
            self.inp[n] = nc.dram_tensor(n, sh, F32, kind="ExternalInput").ap()
        self.out = nc.dram_tensor("out", [S, D], F32, kind="ExternalOutput").ap()
        self.dbg_out = {}

        self.ctr = {}
        def sem(name):
            h = nc.alloc_semaphore(name)
            self.ctr[h] = 0
            return h
        self.sem_of = {"t": sem("sPE"), "v": sem("sDVE"), "s": sem("sACT"),
                       "g": sem("sPOOL"), "y": sem("sSY")}
        self.sST = sem("sST")
        self._dma_sems = {}
        self._mksem = sem
        self.q = {k: [] for k in ENGKEYS}
        self.waited = {k: {} for k in ENGKEYS}

        A = nc.alloc_sbuf_tensor
        self.xf = A("xf", [128, DC * S], F32).ap()
        self.xbf = A("xbf", [128, DC * S], BF16).ap()
        self.stage = A("stage", [128, 16384], F32).ap()
        self.arena = A("arena", [128, 6144], F32).ap()
        self.bcast = A("bcast", [128, 2 * S], F32).ap()
        self.brow = A("brow", [128, S], BF16).ap()
        self.consts = A("consts", [128, 1024], BF16).ap()
        self.cols = A("cols", [128, 96], F32).ap()
        self.ident32 = A("ident32", [128, 128], F32).ap()
        self.constf = A("constf", [128, 16], F32).ap()
        self._const_vals = {}
        self.ones128 = self.consts[:, 0:128]
        self.ident = self.consts[:, 128:256]
        self.tri = self.consts[:, 256:384]
        self.B64 = self.consts[:, 384:512]
        self.onesrow = self.consts[0:1, 0:128]
        self.onescol = self.consts[:, 0:1]

        self.stage_bf = self.stage.bitcast(BF16)
        self.arena_bf = self.arena.bitcast(BF16)
        self.bcast_bf = self.bcast.bitcast(BF16)
        self.rlbf_war = None

        self.lnvar_war = None
        self.lnmu_war = None
        self.bcast0_war = None
        self.brow_war = None
        self.ps = [nc.alloc_psum_tensor(f"ps{i}", [128, 1024], F32).ap()
                   for i in range(4)]
        self.ps_bf = [p.bitcast(BF16) for p in self.ps]
        self.ps_rd = [None] * 4
        self.ps_wr = [None] * 4

    def dsem(self, name):
        if name not in self._dma_sems:
            self._dma_sems[name] = self._mksem("sD_" + name)
        return self._dma_sems[name]

    def emit(self, k, fn, waits=(), inc=None, inck=1):
        need = {}
        for t in waits:
            if t is None:
                continue
            s, v = t
            assert v <= self.ctr[s], "wait targets unscheduled inc"
            if v > need.get(s, -1):
                need[s] = v
        for s, v in need.items():
            if s in self._dma_sems.values() and v != self.ctr[s]:
                raise AssertionError(
                    f"non-full wait {v} on DMA sem (full={self.ctr[s]})")
        todo = []
        for s, v in need.items():
            if self.waited[k].get(s, -1) < v:
                todo.append((s, v))
                self.waited[k][s] = v
        tok = None
        semh = None
        if inc is True:
            semh = self.sem_of[k]
        elif inc is None or inc is False:
            semh = None
        else:
            semh = inc
        if semh is not None:
            self.ctr[semh] += inck
            tok = (semh, self.ctr[semh])
        self.q[k].append((todo, fn, semh, inck))
        return tok

    def finalize(self):
        nc = self.nc
        with nc.Block() as blk:
            def mk(k):
                def run(eng):
                    for todo, fn, semh, inck in self.q[k]:
                        for s, v in todo:
                            eng.wait_ge(s, v)
                        ins = fn(eng)
                        if semh is not None:
                            assert ins is not None, "inc on empty emission"
                            ins.then_inc(semh, inck)
                return run
            blk.tensor(mk("t"))
            blk.vector(mk("v"))
            blk.scalar(mk("s"))
            blk.gpsimd(mk("g"))
            blk.sync(mk("y"))
        return nc

    def ps_acquire(self, i):
        return self.ps_rd[i]

    def tap(self, name, ap, toks):
        if name not in self.dbg:
            return None
        o = self.nc.dram_tensor("dbg_" + name, list(ap.shape), ap.dtype,
                                kind="ExternalOutput").ap()
        t = self.emit("y", lambda e, o=o, ap=ap: e.dma_start(out=o, in_=ap),
                      waits=toks, inc=self.dsem("tap"), inck=16)
        self.dbg_out[name] = o
        return t

    def build(self):
        self.emit_consts()
        self.gate_scalar_on_consts()
        self.emit_load_x()
        steps = [("attn_a", self.sub1_attn_a), ("ffn1", lambda: self.sub2_ffn(0)),
                 ("attn_b", self.sub3_attn_b), ("ffn2", lambda: self.sub2_ffn(1))]
        for nm, fn in steps:
            if _TRUNC[0] == nm:
                break
            fn()
            if _TRUNC[0] in ("b1", "b2", "b2p1", "b3", "ffn_P") and nm == "attn_b":
                break
        self.emit_store()
        return self.finalize()

    def emit_consts(self):
        e = self.emit
        c = None
        def g(fn, c_prev):
            return e("g", fn, waits=[c_prev], inc=True)
        c = g(lambda gp: gp.memset(self.consts[:, :], 0.0), c)
        for val in (1e-5, 1e-6, 1e-8, 0.125, -0.5, -1.0, 0.5, 0.02, 2.0,
                    -2.0, 1.0, 0.0):
            i = len(self._const_vals)
            cell = self.constf[:, i:i + 1]
            self._const_vals[val] = cell
            self.nc.const_aps.register(F32, float(val), cell)
            c = g(lambda gp, c2=cell, v2=float(val): gp.memset(c2, v2), c)
        c = g(lambda gp: gp.memset(self.ones128, 1.0), c)
        c = g(lambda gp: gp.affine_select(out=self.ident, in_=self.ident,
                                          compare_op=ALU.not_equal, fill=1.0,
                                          base=0, pattern=[[-1, 128]],
                                          channel_multiplier=1), c)
        c = g(lambda gp: gp.affine_select(out=self.tri, in_=self.tri,
                                          compare_op=ALU.is_ge, fill=1.0,
                                          base=-1, pattern=[[-1, 128]],
                                          channel_multiplier=1), c)
        c = g(lambda gp: gp.memset(self.consts[0:1, 816:880], 1.0), c)
        c = g(lambda gp: gp.memset(self.B64[0:64, 0:64], 1.0), c)
        c = g(lambda gp: gp.memset(self.B64[64:128, 64:128], 1.0), c)
        c = g(lambda gp: gp.memset(self.ident32[:, :], 0.0), c)
        self.c_consts = g(lambda gp: gp.affine_select(
            out=self.ident32, in_=self.ident32, compare_op=ALU.not_equal,
            fill=1.0, base=0, pattern=[[-1, 128]], channel_multiplier=1), c)

    def gate_scalar_on_consts(self):
        self.emit("s", lambda s: s.nop(), waits=[self.c_consts])

    def emit_load_x(self):
        e = self.emit
        x = self.inp["L0_attn_a_x"]
        stage = self.stage
        xr = x.rearrange("(c p) f -> p c f", p=128)
        c_grp = []
        for k4 in range(SC // 4):
            cts = []
            for i in range(4):
                c = k4 * 4 + i
                cts.append(e("y", lambda en, o=stage[:, c * 1024:(c + 1) * 1024],
                             i_=xr[:, c, :]: en.dma_start(out=o, in_=i_),
                             inc=self.dsem("x" + str(k4)), inck=16))
            c_grp.append([cts[3]])
        for k4 in range(SC // 4):
            cw = c_grp[k4]
            for dd in range(DC):
                pi = (k4 * DC + dd) % 2
                pb = self.ps[pi]
                tlast = None
                for c4 in range(4):
                    c = k4 * 4 + c4
                    src = stage[:, c * 1024 + dd * 128: c * 1024 + dd * 128 + 128]
                    w = cw + [self.c_consts, self.ps_rd[pi]] if c4 == 0 else []
                    tlast = e("t", lambda t, o=pb[:, c4 * 128:(c4 + 1) * 128],
                              s=src: t.transpose(out=o, in_=s, identity=self.ident32),
                              waits=w, inc=(True if c4 == 3 else None))
                self.ps_wr[pi] = tlast
                dst = self.xf[:, dd * S + k4 * 512: dd * S + (k4 + 1) * 512]
                self.ps_rd[pi] = e("v", lambda v, d=dst, p=pb[:, 0:512]:
                                   v.tensor_copy(out=d, in_=p),
                                   waits=[tlast], inc=True)
        self.c_xf = self.tok(self.sem_of["v"])

    def tok(self, semh):
        return (semh, self.ctr[semh])

    def ln_stats(self, xsrc_bf, c_src, eps, rms_only, boff=0, rcols_off=None,
                 sq_off_bf=0):
        e = self.emit
        sq = [self.stage_bf[:, sq_off_bf + i * 1024: sq_off_bf + (i + 1) * 1024]
              for i in range(2)]
        c_sq = [None, None]
        c_sqmm = [None, None]
        c_rBs, c_nmrs = [], []
        for th in range(2):
            p0, p1 = self.ps[0], self.ps[1]
            w0 = [self.ps_rd[0], self.ps_rd[1], self.c_consts] + (
                list(c_src) if isinstance(c_src, list) else [c_src])
            for dd in range(DC):
                xs = xsrc_bf[:, dd * S + th * 1024: dd * S + th * 1024 + 1024]
                csd = c_src[dd] if isinstance(c_src, list) else c_src
                c_sq[dd % 2] = e("v", lambda v, o=sq[dd % 2], i_=xs:
                                 v.tensor_mul(out=o, in0=i_, in1=i_),
                                 waits=[csd, c_sqmm[dd % 2]], inc=True)
                for sub in range(2):
                    if not rms_only:
                        e("t", lambda t, o=p0[:, sub * 512:(sub + 1) * 512],
                          r=xs[:, sub * 512:(sub + 1) * 512], st=(dd == 0),
                          sp=(dd == DC - 1):
                          t.matmul(o, self.ones128, r, start=st, stop=sp),
                          waits=w0 if dd == 0 else [])
                    e("t", lambda t, o=p1[:, sub * 512:(sub + 1) * 512],
                      r=sq[dd % 2][:, sub * 512:(sub + 1) * 512], st=(dd == 0),
                      sp=(dd == DC - 1):
                      t.matmul(o, self.ones128, r, start=st, stop=sp),
                      waits=[c_sq[dd % 2]] + (w0 if dd == 0 else []),
                      inc=(True if sub == 1 else None))
                c_sqmm[dd % 2] = self.tok(self.sem_of["t"])
            c_mm = self.tok(self.sem_of["t"])
            mu = self.arena[:, 4096:5120]
            var = self.arena[:, 5120:6144]
            rB = self.bcast[:, boff + th * 1024: boff + (th + 1) * 1024]
            inv = 1.0 / D
            if rms_only:
                c_v = e("v", lambda v, o=var, i_=p1[:, :]: v.tensor_scalar(
                    out=o, in0=i_, scalar1=inv, scalar2=float(eps),
                    op0=ALU.mult, op1=ALU.add),
                    waits=[c_mm, self.lnvar_war], inc=True)
                self.ps_rd[1] = c_v
                lbias = 0.0
            else:
                c_mu = e("v", lambda v, o=mu, i_=p0[:, :]: v.tensor_scalar(
                    out=o, in0=i_, scalar1=inv, scalar2=None, op0=ALU.mult),
                    waits=[c_mm, self.lnmu_war], inc=True)
                self.ps_rd[0] = c_mu
                c_m2 = e("v", lambda v, o=rB, a=mu: v.tensor_mul(
                    out=o, in0=a, in1=a), waits=[c_mu], inc=True)
                c_v = e("v", lambda v, o=var, i_=p1[:, :], m2=rB:
                        v.scalar_tensor_tensor(out=o, in0=i_, scalar=inv,
                                               in1=m2, op0=ALU.mult,
                                               op1=ALU.subtract),
                        waits=[c_m2], inc=True)
                self.ps_rd[1] = c_v
                lbias = float(eps)
            c_ln = e("s", lambda s, o=var: s.activation(
                out=o, in_=o, func=AF.Ln, bias=lbias, scale=1.0),
                waits=[c_v], inc=True)
            c_rB = e("s", lambda s, o=rB, i_=var: s.activation(
                out=o, in_=i_, func=AF.Exp, scale=-0.5),
                waits=[c_ln, self.bcast0_war], inc=True)
            c_rBs.append(c_rB)
            self.lnvar_war = c_rB
            if not rms_only:
                nmr = self.brow[:, th * 1024:(th + 1) * 1024]
                c_nmr1 = e("v", lambda v, o=nmr, a=mu, b=rB:
                           v.scalar_tensor_tensor(out=o, in0=a, scalar=-1.0,
                                                  in1=b, op0=ALU.mult,
                                                  op1=ALU.mult),
                           waits=[c_rB, self.brow_war], inc=True)
                c_nmrs.append(c_nmr1)
                self.lnmu_war = c_nmr1
        c_rcols = None
        if rcols_off is not None:
            rc = self.cols[:, rcols_off: rcols_off + SC]
            for g in range(2):
                tlast = None
                for i in range(8):
                    c = g * 8 + i
                    src2 = self.bcast[:, boff + c * 128: boff + (c + 1) * 128]
                    w = [c_rBs[g], self.ps_rd[2]] if i == 0 else []
                    tlast = e("t", lambda t, o=self.ps[2][:, i * 128:(i + 1) * 128],
                              s=src2: t.transpose(out=o, in_=s,
                                                  identity=self.ident32),
                              waits=w, inc=(True if i == 7 else None))
                c_rcols = e("v", lambda v, o=rc[:, g * 8:(g + 1) * 8],
                            p=self.ps[2].rearrange("q (c f) -> q c f", c=8):
                            v.tensor_copy(out=o, in_=p[:, :, 0:1].squeeze(2)),
                            waits=[tlast], inc=True)
                self.ps_rd[2] = c_rcols
        return c_rBs, c_nmrs, c_rcols

    def cast_xbf(self, c_war=None):
        cc2 = {"v": None, "s": None}
        cl = []
        for dd in range(DC):
            cx = self.c_x_all[dd] if self.c_x_all[dd] is not None else self.c_x
            k = "v" if dd % 2 == 0 else "s"
            wt = [cx, c_war, cc2[k]]
            if dd == DC - 1:
                wt.append(cc2["v" if k == "s" else "s"])
            if k == "v":
                c = self.emit("v", lambda v, o=self.xbf[:, dd * S:(dd + 1) * S],
                              i_=self.xf[:, dd * S:(dd + 1) * S]:
                              v.tensor_copy(out=o, in_=i_),
                              waits=wt, inc=True)
            else:
                c = self.emit("s", lambda s, o=self.xbf[:, dd * S:(dd + 1) * S],
                              i_=self.xf[:, dd * S:(dd + 1) * S]:
                              s.activation(out=o, in_=i_, func=AF.Copy),
                              waits=wt, inc=True)
            cc2[k] = c
            cl.append(c)
        self.c_cast_l = cl
        return c

    def apply_norm_done(self, c):
        self.bcast0_war = c
        self.brow_war = c

    def apply_norm(self, rB_t, nmr_t=None, ws_cols=None, c_prev=None):
        e = self.emit
        c = c_prev
        for dd in range(DC):
            xs = self.xbf[:, dd * S:(dd + 1) * S]
            rBs = self.bcast[:, 0:S]
            if ws_cols is not None:
                c = e("v", lambda v, o=xs, w=ws_cols[dd], b=rBs:
                      v.scalar_tensor_tensor(out=o, in0=o, scalar=w, in1=b,
                                             op0=ALU.mult, op1=ALU.mult),
                      waits=list(rB_t) + [c], inc=True)
            else:
                c = e("v", lambda v, o=xs, b=rBs:
                      v.tensor_mul(out=o, in0=o, in1=b),
                      waits=list(rB_t) + [c], inc=True)
            if nmr_t is not None:
                c = e("v", lambda v, o=xs, b=self.brow[:, 0:S]:
                      v.tensor_add(out=o, in0=o, in1=b),
                      waits=list(nmr_t) + [c], inc=True)
        return c

    def wT_slab(self, par, slab):
        return self.arena_bf[:, par * 3072 + slab * 1024: par * 3072 + slab * 1024 + 1024]

    def weight_pair_prep(self, p, specs):
        e = self.emit
        par = p % 2
        wst = self.arena[:, 3072:4096]
        wbf = self.stage_bf[:, 30784:31808]
        toks = []
        for (nm, slab, perm) in specs:
            w = self.inp[nm]
            if perm:
                wg = w.rearrange("(d2 g) d -> g d2 d", g=16)
                e("y", lambda y, o=wst[0:64, :], i_=wg[2 * p]:
                  y.dma_start(out=o, in_=i_),
                  waits=[self.wstage_free], inc=self.dsem("wp"), inck=16)
                c_d = e("y", lambda y, o=wst[64:128, :], i_=wg[2 * p + 1]:
                        y.dma_start(out=o, in_=i_), inc=self.dsem("wp"),
                        inck=16)
            else:
                c_d = e("y", lambda y, o=wst, i_=w[p * 128:(p + 1) * 128, :]:
                        y.dma_start(out=o, in_=i_),
                        waits=[self.wstage_free], inc=self.dsem("wp"), inck=16)
            c_c = e("v", lambda v, o=wbf, i_=wst: v.tensor_copy(out=o, in_=i_),
                    waits=[c_d, self.wbf_free], inc=True)
            self.wstage_free = c_c
            dst = self.wT_slab(par, slab)
            tok = None
            for half in range(2):
                tl = None
                for i in range(4):
                    ddc = half * 4 + i
                    wt = [c_c, self.ps_rd[3], self.wslab_free[par][slab]] if i == 0 else []
                    tl = e("t", lambda t, o=self.ps_bf[3][:, i * 128:(i + 1) * 128],
                           s=wbf[:, ddc * 128:(ddc + 1) * 128]:
                           t.transpose(out=o, in_=s, identity=self.ident),
                           waits=wt, inc=(True if i == 3 else None))
                tok = e("v", lambda v, o=dst[:, half * 512:(half + 1) * 512],
                        pp=self.ps_bf[3][:, 0:512]: v.tensor_copy(out=o, in_=pp),
                        waits=[tl], inc=True)
                self.ps_rd[3] = tok
            self.wbf_free = tok
            toks.append(tok)
        return toks

    def load_cols1024(self, name, dst_cols, psi=3):
        e = self.emit
        gst = self.stage[0:8, 15904:16032]
        c_d = e("y", lambda y, o=gst, i_=self.inp[name].rearrange("(c p) -> c p", c=8):
                y.dma_start(out=o, in_=i_), waits=[self.gst_free],
                inc=self.dsem("gst"), inck=16)
        ct = e("t", lambda t, o=self.ps[psi][:, 0:8], s=gst:
               t.transpose(out=o, in_=s, identity=self.ident32[0:8, 0:8]),
               waits=[c_d, self.ps_rd[psi], self.c_consts], inc=True)
        cv = e("v", lambda v, o=dst_cols, pp=self.ps[psi][:, 0:8]:
               v.tensor_copy(out=o, in_=pp), waits=[ct], inc=True)
        self.ps_rd[psi] = cv
        self.gst_free = cv
        return cv

    def sub1_attn_a(self):
        e = self.emit
        self.c_x = self.c_xf
        self.wstage_free = None
        self.wbf_free = None
        self.gst_free = None
        self.wslab_free = [[None] * 3, [None] * 3]
        self.rl_war = [None, None]
        c_cast = self.cast_xbf()
        c_rB, c_nmr, _ = self.ln_stats(self.xbf, c_cast, 1e-5, False,
                                       boff=0, sq_off_bf=28736)
        c_h1 = self.apply_norm(c_rB, c_nmr)
        self.apply_norm_done(c_h1)
        self.tap("h1bf", self.xbf, [c_h1])
        vt = self.stage_bf[:, 16384:18496]
        vt3 = vt.rearrange("p (c w) -> p c w", w=132)
        e("g", lambda g: g.memset(vt3[:, :, 64:65], 1.0), waits=[self.c_xf])
        c_vones = e("g", lambda g: g.memset(vt3[:, :, 130:131], 1.0), inc=True)
        c_ones32 = e("g", lambda g: g.memset(self.cols[:, 0:64], 1.0), inc=True)

        qf = self.stage_bf[:, 18496:20544]
        kf = self.stage_bf[:, 20544:22592]
        Esl = [self.stage_bf[:, 22592 + i * 512: 22592 + (i + 1) * 512]
               for i in range(4)]
        u = self.stage[:, 13344:14368]
        rlrow = [self.arena[0:1, 4096:5120], self.arena[0:1, 5120:6144]]

        c_qkv = [None] * DC
        wnames = [("L0_attn_a_Wq", 0, False), ("L0_attn_a_Wk", 1, False),
                  ("L0_attn_a_Wv", 2, True)]
        wtoks = self.weight_pair_prep(0, wnames)
        c_u_rd = None
        c_E_rd = [None] * 4
        self.sc_rd = [[None, None], [None, None]]
        for p in range(NPAIR):
            par = p % 2
            wq, wk, wv = (self.wT_slab(par, i) for i in range(3))
            wtq, wtk, wtv = wtoks
            c_q = c_k = None
            for (dstap, wslab, wtok, psi) in ((qf, wq, wtq, 0), (kf, wk, wtk, 1)):
                ce = None
                for tt in range(4):
                    pb = self.ps[psi][:, (tt % 2) * 512:(tt % 2) * 512 + 512]
                    for dd in range(DC):
                        wt = ([wtok, c_h1, self.ps_rd[psi]] +
                              self.sc_rd[psi]) if dd == 0 else []
                        e("t", lambda t, o=pb,
                          l=wslab[:, dd * 128:(dd + 1) * 128],
                          r=self.xbf[:, dd * S + tt * 512: dd * S + tt * 512 + 512],
                          st=(dd == 0), sp=(dd == DC - 1):
                          t.matmul(o, l, r, start=st, stop=sp),
                          waits=wt, inc=(True if dd == DC - 1 else None))
                    cmm = self.tok(self.sem_of["t"])
                    ce = e("v", lambda v, o=dstap[:, tt * 512:(tt + 1) * 512],
                           pp=pb: v.tensor_copy(out=o, in_=pp),
                           waits=[cmm, ce], inc=True)
                    self.ps_rd[psi] = ce
                if psi == 0:
                    c_q = ce
                else:
                    c_k = ce
            c_v = None
            for half in range(2):
                for i in range(8):
                    cch = half * 8 + i
                    for dd in range(DC):
                        wt = [wtv, c_h1, self.ps_rd[2], c_vones] if (i == 0 and dd == 0) else []
                        e("t", lambda t, o=self.ps[2][:, i * 128:(i + 1) * 128],
                          l=self.xbf[:, dd * S + cch * 128: dd * S + cch * 128 + 128],
                          r=wv[:, dd * 128:(dd + 1) * 128],
                          st=(dd == 0), sp=(dd == DC - 1):
                          t.matmul(o, l, r, start=st, stop=sp),
                          waits=wt, inc=(True if (dd == DC - 1 and i == 7) else None))
                cmm = self.tok(self.sem_of["t"])
                c_v = e("v", lambda v, h=half:
                        v.tensor_copy(
                            out=vt3[:, h * 8:(h + 1) * 8, :].rearrange(
                                "p c (g w) -> p c g w", w=66)[:, :, :, 0:64],
                            in_=self.ps[2].rearrange("p (c g w) -> p c g w",
                                                     g=2, w=64)),
                        waits=[cmm, c_v], inc=True)
                self.ps_rd[2] = c_v
            self.wslab_free[par] = [c_q, c_k, c_v]
            if p + 1 < NPAIR:
                wtoks = self.weight_pair_prep(p + 1, wnames)
            qkvch = self.stage_bf[:, p * 2048:(p + 1) * 2048]
            for tq in range(4):
                climit = 4 * tq + 4
                c_pv_last = [None, None]
                pend = None
                def emit_pv(cc, cpvws):
                    for hh2 in range(2):
                        wt2 = [cpvws[hh2]]
                        if cc == 0:
                            wt2.append(self.ps_rd[2])
                        e("t", lambda t,
                          o=self.ps[2][0:65, hh2 * 512:hh2 * 512 + 512],
                          l=vt[:, cc * 132 + hh2 * 66: cc * 132 + hh2 * 66 + 65],
                          r=Esl[hh2 * 2 + (cc % 2)][:, 0:512],
                          st=(cc == 0), sp=(cc == climit - 1):
                          t.matmul(o, l, r, start=st, stop=sp),
                          waits=wt2,
                          inc=(True if cc == climit - 1 else None))
                        if cc == climit - 1:
                            c_pv_last[hh2] = self.tok(self.sem_of["t"])
                        c_E_rd[hh2 * 2 + (cc % 2)] = self.tok(self.sem_of["t"])
                for c in range(climit):
                    c_rel = c - 4 * tq
                    vfrom = max(0, c_rel * 128)
                    cpvws = []
                    for hh in range(2):
                        psS = self.ps[hh][:, (c % 2) * 512:(c % 2) * 512 + 512]
                        koff = hh * 64
                        e("t", lambda t, o=psS[:, vfrom:512],
                          l=kf[koff:koff + 64, c * 128:(c + 1) * 128],
                          r=qf[koff:koff + 64, tq * 512 + vfrom: (tq + 1) * 512]:
                          t.matmul(o, l, r, start=True, stop=True),
                          waits=[c_q, c_k, self.sc_rd[hh][c % 2]], inc=True)
                        cmm = self.tok(self.sem_of["t"])
                        sl = Esl[hh * 2 + (c % 2)]
                        cex = e("s", lambda s, o=sl[:, vfrom:512],
                                i_=psS[:, vfrom:512]:
                                s.activation(out=o, in_=i_, func=AF.Exp,
                                             scale=0.125),
                                waits=[cmm, c_E_rd[hh * 2 + (c % 2)]], inc=True)
                        self.sc_rd[hh][c % 2] = cex
                        cpvw = cex
                        if c_rel >= 0:
                            cpvw = e("v", lambda v, o=sl[:, vfrom:vfrom + 128]:
                                     v.tensor_mul(out=o, in0=o, in1=self.tri),
                                     waits=[cex], inc=True)
                            if vfrom > 0:
                                cpvw = e("v", lambda v, o=sl[:, 0:vfrom]:
                                         v.memset(o, 0.0), inc=True)
                        cpvws.append(cpvw)
                    if pend is not None:
                        emit_pv(pend[0], pend[1])
                    pend = (c, cpvws)
                emit_pv(pend[0], pend[1])
                cuA = e("v", lambda v, o=u[0:64, 0:512],
                        pp=self.ps[2][0:64, 0:512]:
                        v.tensor_copy(out=o, in_=pp),
                        waits=[c_pv_last[0], c_u_rd], inc=True)
                cuB = e("v", lambda v, o=u[64:128, 0:512],
                        pp=self.ps[2][0:64, 512:1024]:
                        v.tensor_copy(out=o, in_=pp),
                        waits=[c_pv_last[1], cuA], inc=True)
                crl = []
                for hh in range(2):
                    hi = self.bcast_bf[0:1, 4096 + hh * 512:
                                       4096 + hh * 512 + 512]
                    lo = self.bcast_bf[0:1, 5120 + hh * 512:
                                       5120 + hh * 512 + 512]
                    cl1 = e("s", lambda s, o=rlrow[hh][:, 0:512],
                            pp=self.ps[2][64:65, hh * 512:hh * 512 + 512]:
                            s.activation(out=o, in_=pp, func=AF.Ln),
                            waits=[c_pv_last[hh], self.rl_war[hh]], inc=True)
                    c_rc = e("s", lambda s, o=rlrow[hh][:, 0:512]:
                            s.activation(out=o, in_=o, func=AF.Exp,
                                         scale=-1.0),
                            waits=[cl1], inc=True)
                    c_hi = e("v", lambda v, o=hi, i_=rlrow[hh][:, 0:512]:
                             v.tensor_copy(out=o, in_=i_),
                             waits=[c_rc, self.rlbf_war], inc=True)
                    c_lo = e("v", lambda v, o=lo, a=rlrow[hh][:, 0:512], b=hi:
                             v.tensor_tensor(out=o, in0=a, in1=b,
                                             op=ALU.subtract),
                             waits=[c_hi], inc=True)
                    crl.append((c_lo, hi, lo))
                for hh in range(2):
                    c_lo, hi, lo = crl[hh]
                    wt = [c_lo]
                    if hh == 0:
                        wt += [self.ps_rd[3], self.c_consts]
                    e("t", lambda t,
                      o=self.ps[3][hh * 64:hh * 64 + 64, 0:512],
                      l=self.onesrow[:, 0:64], r=hi:
                      t.matmul(o, l, r, start=True, stop=False),
                      waits=wt)
                    e("t", lambda t,
                      o=self.ps[3][hh * 64:hh * 64 + 64, 0:512],
                      l=self.onesrow[:, 0:64], r=lo:
                      t.matmul(o, l, r, start=False, stop=True),
                      waits=[], inc=(hh == 1))
                crlB = self.tok(self.sem_of["t"])
                self.rl_war = [crl[0][0], crl[1][0]]
                self.rlbf_war = crlB
                cq2 = e("v", lambda v, o=qkvch[:, tq * 512:(tq + 1) * 512],
                        a=u[:, 0:512], b=self.ps[3][:, 0:512]:
                        v.tensor_mul(out=o, in0=a, in1=b),
                        waits=[cuA, cuB, crlB], inc=True)
                c_u_rd = cq2
                self.ps_rd[3] = cq2
                self.ps_rd[2] = cq2
                c_qkv[p] = cq2
            if p == 0:
                self.tap("qf0", qf, [c_q])
                self.tap("kf0", kf, [c_k])
                self.tap("vt0", vt, [c_v])
                self.tap("u0", u, [c_qkv[0]])
        for psi in range(2):
            toks = [t for t in self.sc_rd[psi] if t is not None]
            if toks:
                self.ps_rd[psi] = (toks[0][0], max(t[1] for t in toks))
        self.tap("qkv", self.stage_bf[:, 0:16384], c_qkv)
        self.sub1_epilogue(c_qkv)

    def sub1_epilogue(self, c_qkv):
        e = self.emit
        qkv = self.stage_bf[:, 0:16384]
        c_rB, c_nmr, _ = self.ln_stats(qkv, c_qkv[-1], 1e-5, False,
                                       boff=0, sq_off_bf=28736)
        c_ln = None
        for dd in range(DC):
            xs = qkv[:, dd * S:(dd + 1) * S]
            c_ln = e("v", lambda v, o=xs, b=self.bcast[:, 0:S]:
                     v.tensor_mul(out=o, in0=o, in1=b),
                     waits=list(c_rB) + [c_ln, c_qkv[dd]], inc=True)
            c_ln = e("v", lambda v, o=xs, b=self.brow[:, 0:S]:
                     v.tensor_add(out=o, in0=o, in1=b),
                     waits=list(c_nmr) + [c_ln], inc=True)
        self.apply_norm_done(c_ln)
        self.tap("lnq", qkv, [c_ln])
        gam = self.cols[:, 16:24]
        bet = self.cols[:, 24:32]
        bob = self.cols[:, 32:40]
        c_g = self.load_cols1024("L0_attn_a_gamma", gam)
        c_b = self.load_cols1024("L0_attn_a_beta", bet)
        bet_bf = self.consts[:, 640:648]
        c_bbf = e("v", lambda v: v.tensor_copy(out=bet_bf, in_=bet),
                  waits=[c_b], inc=True)
        woT = self.stage_bf[:, 16384:24576]
        wst = self.arena[:, 3072:4096]
        wbf = self.stage_bf[:, 30784:31808]
        c_woT = None
        for eb in range(DC):
            c_d = e("y", lambda y, o=wst,
                    i_=self.inp["L0_attn_a_Wo"][eb * 128:(eb + 1) * 128, :]:
                    y.dma_start(out=o, in_=i_),
                    waits=[self.wstage_free], inc=self.dsem("wp"), inck=16)
            c_c = e("v", lambda v, o=wbf, i_=wst: v.tensor_copy(out=o, in_=i_),
                    waits=[c_d, self.wbf_free], inc=True)
            self.wstage_free = c_c
            for half in range(2):
                tl = None
                for i in range(4):
                    jj = half * 4 + i
                    wt = [c_c, self.ps_rd[3]] if i == 0 else []
                    tl = e("t", lambda t, o=self.ps_bf[3][:, i * 128:(i + 1) * 128],
                           s=wbf[:, jj * 128:(jj + 1) * 128]:
                           t.transpose(out=o, in_=s, identity=self.ident),
                           waits=wt, inc=(True if i == 3 else None))
                c_woT = e("v", lambda v, h=half, ebb=eb:
                          v.tensor_copy(
                              out=woT.rearrange("p (c q) -> p c q", q=1024)
                              [:, h * 4:(h + 1) * 4, ebb * 128:(ebb + 1) * 128],
                              in_=self.ps_bf[3][:, 0:512].rearrange(
                                  "p (c q) -> p c q", q=128)),
                          waits=[tl, c_qkv[-1] if eb == 0 and half == 0 else None,
                                 c_woT],
                          inc=True)
                self.ps_rd[3] = c_woT
        for ee in range(DC):
            for jj in range(DC):
                wt = [c_woT, c_bbf]
                if ee == 0 and jj == 0:
                    wt.append(self.ps_rd[2])
                e("t", lambda t, o=self.ps[2][:, ee:ee + 1],
                  l=woT[:, jj * 1024 + ee * 128: jj * 1024 + (ee + 1) * 128],
                  r=bet_bf[:, jj:jj + 1], st=(jj == 0), sp=(jj == DC - 1):
                  t.matmul(o, l, r, start=st, stop=sp),
                  waits=wt, inc=(True if (ee == DC - 1 and jj == DC - 1) else None))
        c_bo = e("v", lambda v, o=bob, pp=self.ps[2][:, 0:8]:
                 v.tensor_copy(out=o, in_=pp),
                 waits=[self.tok(self.sem_of["t"])], inc=True)
        self.ps_rd[2] = c_bo
        c_bo_mm = self.tok(self.sem_of["t"])
        c_wg = [None] * DC
        for jj in range(DC):
            c_wg[jj] = e("v", lambda v, o=woT[:, jj * 1024:(jj + 1) * 1024],
                         g2=gam[:, jj:jj + 1]:
                         v.tensor_scalar(out=o, in0=o, scalar1=g2,
                                         scalar2=None, op0=ALU.mult),
                         waits=[c_woT, c_g, c_bo_mm], inc=True)
        c_x1 = [None] * DC
        for ee in range(DC):
            for tt in range(4):
                psi = ee % 2
                pb = self.ps[psi][:, (tt % 2) * 512:(tt % 2) * 512 + 512]
                for jj in range(DC):
                    wt = list(c_wg) + [c_ln] if jj == 0 else []
                    if jj == 0:
                        wt.append(self.ps_rd[psi])
                    e("t", lambda t, o=pb,
                      l=woT[:, jj * 1024 + ee * 128: jj * 1024 + (ee + 1) * 128],
                      r=qkv[:, jj * 2048 + tt * 512: jj * 2048 + tt * 512 + 512],
                      st=(jj == 0), sp=(jj == DC - 1):
                      t.matmul(o, l, r, start=st, stop=sp),
                      waits=wt, inc=(True if jj == DC - 1 else None))
                cmm = self.tok(self.sem_of["t"])
                cx = e("v", lambda v,
                       o=self.xf[:, ee * S + tt * 512: ee * S + tt * 512 + 512],
                       pp=pb, bo=bob[:, ee:ee + 1]:
                       v.scalar_tensor_tensor(out=o, in0=pp, scalar=bo,
                                              in1=o, op0=ALU.add, op1=ALU.add),
                       waits=[cmm, c_bo, c_x1[ee]], inc=True)
                self.ps_rd[psi] = cx
                c_x1[ee] = cx
        self.c_x = c_x1[-1]
        self.c_x_all = c_x1
        self.tap("x1", self.xf, c_x1)

    def load_wslab(self, name, pieces, c_war=None, nbuf=2, ceng="v"):
        e = self.emit
        toks = []
        for i, (src_ap, dst) in enumerate(pieces):
            wi = i % nbuf
            wst = self.stage[:, 9216 + wi * 1024: 9216 + wi * 1024 + 1024]
            w = wst[:, 0:1024]
            if len(src_ap.shape) == 3:
                wdma = w.rearrange("p (c f) -> p c f", c=src_ap.shape[1])
            else:
                wdma = w
            c_d = e("y", lambda y, o=wdma, i_=src_ap: y.dma_start(out=o, in_=i_),
                    waits=[self.wst2_free[wi]] + (
                        c_war if isinstance(c_war, list) else [c_war]),
                    inc=self.dsem("wl" + str(wi)), inck=16)
            if ceng == "s":
                c_c = e("s", lambda s, o=dst, i_=w:
                        s.activation(out=o, in_=i_, func=AF.Copy),
                        waits=[c_d], inc=True)
            else:
                c_c = e("v", lambda v, o=dst, i_=w: v.tensor_copy(out=o, in_=i_),
                        waits=[c_d], inc=True)
            self.wst2_free[wi] = c_c
            toks.append(c_c)
        return toks

    def load_cols(self, name, n, ncols, dst_cols):
        e = self.emit
        gst = self.stage[0:ncols, 15904:15904 + 128]
        c_d = e("y", lambda y, o=gst[:, 0:n],
                i_=self.inp[name].rearrange("(c p) -> c p", c=ncols):
                y.dma_start(out=o, in_=i_), waits=[self.gst_free],
                inc=self.dsem("gst"), inck=16)
        ct = e("t", lambda t, o=self.ps[3][0:n, 0:ncols], s=gst[:, 0:n]:
               t.transpose(out=o, in_=s,
                           identity=self.ident32[0:ncols, 0:ncols]),
               waits=[c_d, self.ps_rd[3], self.c_consts], inc=True)
        cv = e("v", lambda v, o=dst_cols[0:n, :], pp=self.ps[3][0:n, 0:ncols]:
               v.tensor_copy(out=o, in_=pp), waits=[ct], inc=True)
        self.ps_rd[3] = cv
        self.gst_free = cv
        return cv

    def sub2_ffn(self, tag):
        e = self.emit
        self.wst2_free = [None] * 4
        barrier = self.tok(self.sem_of["t"])
        c_cast = self.cast_xbf(c_war=barrier)
        cast_l = self.c_cast_l
        u1 = self.arena_bf[:, 0:2048]
        v2 = self.arena_bf[:, 2048:4096]
        v1 = self.arena_bf[:, 4096:12288]
        u2 = self.stage_bf[:, 0:8192]
        P_sb = self.stage_bf[:, 8192:12288]
        Q_sb = self.stage_bf[:, 12288:16384]
        Hb = [self.stage_bf[:, 16384 + i * 1024: 16384 + (i + 1) * 1024]
              for i in range(2)]
        U1, V1, U2, V2 = (self.inp["L0_ffn_U1"], self.inp["L0_ffn_V1"],
                          self.inp["L0_ffn_U2"], self.inp["L0_ffn_V2"])
        b1c = self.cols[:, 40:72]
        b2c = self.cols[:, 72:80]
        c_b1 = self.load_cols("L0_ffn_b1", 128, 32, b1c)
        c_b2 = self.load_cols("L0_ffn_b2", 128, 8, b2c)
        pieces = []
        u1r = U1.rearrange("(c p) f -> p c f", p=128)
        for g in range(2):
            pieces.append((u1r[:, 4 * g:4 * g + 4, :],
                           u1[:, g * 1024:(g + 1) * 1024]))
        for cc in range(2):
            pieces.append((V2[cc * 128:(cc + 1) * 128, :],
                           v2[:, cc * 1024:(cc + 1) * 1024]))
        wtok = self.load_wslab("ffn_a", pieces, c_war=barrier, nbuf=4,
                               ceng="s")
        c_u1 = wtok[1]; c_v2 = wtok[3]
        s1c = self.cols[:, 80:82]
        for rc in range(2):
            for dd in range(DC):
                wt = []
                if dd == 0:
                    wt = [c_u1] + ([self.ps_rd[2]] if rc == 0 else [])
                e("t", lambda t, o=self.ps[2][:, rc:rc + 1],
                  l=u1[:, dd * 256 + rc * 128: dd * 256 + (rc + 1) * 128],
                  r=self.ones128[:, 0:1],
                  st=(dd == 0), sp=(dd == DC - 1):
                  t.matmul(o, l, r, start=st, stop=sp),
                  waits=wt, inc=(True if (rc == 1 and dd == DC - 1) else None))
        cs1m = self.tok(self.sem_of["t"])
        c_s1 = e("v", lambda v, o=s1c, pp=self.ps[2][:, 0:2]:
                 v.tensor_copy(out=o, in_=pp), waits=[cs1m], inc=True)
        self.ps_rd[2] = c_s1
        c_rB, c_nmr, _ = self.ln_stats(self.xbf, cast_l, 1e-5, False,
                                       boff=0, sq_off_bf=28736)
        tmpP = self.stage[:, 13312:13824]
        c_P = [None, None]
        for rc in range(2):
            for tt in range(4):
                pb = self.ps[rc][:, (tt % 2) * 512:(tt % 2) * 512 + 512]
                for dd in range(DC):
                    wt = [cast_l[dd]]
                    if dd == 0:
                        wt += [c_u1, self.ps_rd[rc]]
                    e("t", lambda t, o=pb,
                      l=u1[:, dd * 256 + rc * 128: dd * 256 + (rc + 1) * 128],
                      r=self.xbf[:, dd * S + tt * 512: dd * S + tt * 512 + 512],
                      st=(dd == 0), sp=(dd == DC - 1):
                      t.matmul(o, l, r, start=st, stop=sp),
                      waits=wt, inc=(True if dd == DC - 1 else None))
                cmm = self.tok(self.sem_of["t"])
                th2 = tt // 2
                cP1 = e("v", lambda v, o=tmpP,
                        a=self.brow[:, tt * 512:(tt + 1) * 512],
                        w=s1c[:, rc:rc + 1], pp=pb:
                        v.scalar_tensor_tensor(out=o, in0=a, scalar=w, in1=pp,
                                               op0=ALU.mult, op1=ALU.add),
                        waits=[cmm, c_s1, self.tmpP_war,
                               c_nmr[th2] if th2 < len(c_nmr) else None],
                        inc=True)
                self.ps_rd[rc] = cP1
                c_P[rc] = e("v", lambda v,
                            o=P_sb[:, rc * 2048 + tt * 512: rc * 2048 + tt * 512 + 512],
                            a=tmpP, b=self.bcast[:, tt * 512:(tt + 1) * 512]:
                            v.tensor_mul(out=o, in0=a, in1=b),
                            waits=[cP1, c_P[rc],
                                   c_rB[th2] if th2 < len(c_rB) else None],
                            inc=True)
                self.tmpP_war = c_P[rc]
        self.apply_norm_done(c_P[1])
        pieces_u2 = []
        u2r = U2.rearrange("(c p) f -> p c f", p=128)
        for g in range(8):
            pieces_u2.append((u2r[:, 4 * g:4 * g + 4, :],
                              u2[:, g * 1024:(g + 1) * 1024]))
        wtok2 = self.load_wslab("ffn_u2", pieces_u2, c_war=barrier, nbuf=4,
                                ceng="s")
        c_u2g = [wtok2[g] for g in range(8)]
        pieces_v1 = []
        for pc in range(4):
            for cc in range(2):
                pieces_v1.append((V1[cc * 128:(cc + 1) * 128,
                                     pc * 1024:(pc + 1) * 1024],
                                  v1[:, cc * 4096 + pc * 1024: cc * 4096 + (pc + 1) * 1024]))
        wtok3 = self.load_wslab("ffn_v1", pieces_v1,
                                c_war=[barrier, c_nmr[-1], c_rB[-1]], nbuf=4,
                                ceng="s")
        c_v1p = [(wtok3[2 * pc], wtok3[2 * pc + 1]) for pc in range(4)]
        self.tap("Pf" if tag == 0 else "Pf2", P_sb, [c_P[0], c_P[1]])
        if _TRUNC[0] == "ffn_P":
            self.c_x = self.c_x_all[-1]
            return
        c_Q = [None, None]
        for th in range(2):
            c_Hev = [None, None]

            def emit_Q(fc):
                hb = fc % 2
                for rc in range(2):
                    for sub in range(2):
                        wt = [c_Hev[hb], c_u2g[fc // 4]]
                        if fc == 0 and rc == 0 and sub == 0:
                            wt += [self.ps_rd[2], self.ps_rd[3]]
                        e("t", lambda t,
                          o=self.ps[2 + rc][:, sub * 512:(sub + 1) * 512],
                          l=u2[:, fc * 256 + rc * 128: fc * 256 + (rc + 1) * 128],
                          r=Hb[hb][:, sub * 512:(sub + 1) * 512],
                          st=(fc == 0), sp=(fc == DFF // 128 - 1):
                          t.matmul(o, l, r, start=st, stop=sp),
                          waits=wt,
                          inc=(True if (rc == 1 and sub == 1) else None))
                self.c_Qmm = getattr(self, "c_Qmm", [None, None])
                self.c_Qmm[hb] = self.tok(self.sem_of["t"])

            for fc in range(DFF // 128):
                hb = fc % 2
                psH = self.ps[hb]
                for cc in range(2):
                    for sub in range(2):
                        wt = []
                        if cc == 0 and sub == 0:
                            wt = [c_v1p[fc // 8][0], c_v1p[fc // 8][1],
                                  c_P[0], c_P[1], c_Hev[hb],
                                  self.ps_rd[hb]]
                        e("t", lambda t, o=psH[:, sub * 512:(sub + 1) * 512],
                          l=v1[:, cc * 4096 + fc * 128: cc * 4096 + (fc + 1) * 128],
                          r=P_sb[:, cc * 2048 + th * 1024 + sub * 512:
                                 cc * 2048 + th * 1024 + sub * 512 + 512],
                          st=(cc == 0), sp=(cc == 1):
                          t.matmul(o, l, r, start=st, stop=sp),
                          waits=wt,
                          inc=(True if (cc == 1 and sub == 1) else None))
                c_Hmm = self.tok(self.sem_of["t"])
                c_Hev[hb] = e("s", lambda s, o=Hb[hb], i_=psH,
                              b=b1c[:, fc:fc + 1]:
                              s.activation(out=o, in_=i_, func=AF.Gelu_apprx_tanh,
                                           bias=b, scale=1.0),
                              waits=[c_Hmm, c_b1,
                                     (self.c_Qmm[hb] if hasattr(self, "c_Qmm") else None)],
                              inc=True)
                self.ps_rd[hb] = c_Hev[hb]
                if fc >= 1:
                    emit_Q(fc - 1)
            emit_Q(DFF // 128 - 1)
            cqmm = self.tok(self.sem_of["t"])
            for rc in range(2):
                c_Q[rc] = e("v", lambda v,
                            o=Q_sb[:, rc * 2048 + th * 1024: rc * 2048 + (th + 1) * 1024],
                            pp=self.ps[2 + rc]: v.tensor_copy(out=o, in_=pp),
                            waits=[cqmm, c_Q[rc]], inc=True)
                self.ps_rd[2 + rc] = c_Q[rc]
            for ee in range(DC):
                psi = ee % 2
                pb = self.ps[psi]
                for cc in range(2):
                    for sub in range(2):
                        wt = []
                        if cc == 0 and sub == 0:
                            wt = [c_v2, c_Q[0], c_Q[1], self.ps_rd[psi]]
                        e("t", lambda t, o=pb[:, sub * 512:(sub + 1) * 512],
                          l=v2[:, cc * 1024 + ee * 128: cc * 1024 + (ee + 1) * 128],
                          r=Q_sb[:, cc * 2048 + th * 1024 + sub * 512:
                                 cc * 2048 + th * 1024 + sub * 512 + 512],
                          st=(cc == 0), sp=(cc == 1):
                          t.matmul(o, l, r, start=st, stop=sp),
                          waits=wt,
                          inc=(True if (cc == 1 and sub == 1) else None))
                cmm = self.tok(self.sem_of["t"])
                cx = e("v", lambda v,
                       o=self.xf[:, ee * S + th * 1024: ee * S + (th + 1) * 1024],
                       pp=pb, bo=b2c[:, ee:ee + 1]:
                       v.scalar_tensor_tensor(out=o, in0=pp, scalar=bo,
                                              in1=o, op0=ALU.add, op1=ALU.add),
                       waits=[cmm, c_b2, self.c_x_all[ee]], inc=True)
                self.ps_rd[psi] = cx
                self.c_x_all[ee] = cx
        del self.c_Qmm
        self.c_x = self.c_x_all[-1]
        self.tap("x2" if tag == 0 else "x4", self.xf, list(self.c_x_all))

    def sub3_attn_b(self):
        e = self.emit
        self.rk_war = None
        self.nw1_war = None
        self.ret_tmp_war = None
        self.tmpo_war = None
        self.sq2_war = [None, None]
        self.p1q_war = [None, None]
        self.gg_free = None
        barrier = self.tok(self.sem_of["t"])
        c_cast = self.cast_xbf(c_war=barrier)
        c_rB, c_nmr, _ = self.ln_stats(self.xbf, c_cast, 1e-5, False,
                                       boff=0, sq_off_bf=28736)
        c_xo = self.apply_norm(c_rB, c_nmr)
        self.apply_norm_done(c_xo)
        self.tap("xo", self.xbf, [c_xo])
        c_rn, _, c_rcols = self.ln_stats(self.xbf, c_xo, 1e-6, True,
                                         boff=0, rcols_off=0, sq_off_bf=28736)
        self.c_rcols_tok = c_rcols
        self.tap("rnB", self.bcast[:, 0:S], c_rn)
        wscol = self.cols[:, 16:24]
        wrcol = self.cols[:, 24:32]
        c_ws = self.load_cols("L0_attn_b_store_norm_w", 128, 8, wscol)
        c_wr = self.load_cols("L0_attn_b_retrieve_norm_w", 128, 8, wrcol)
        knw = self.cols[:, 32:33]
        qnw = self.cols[:, 33:34]
        c_knw = self.load_headcol("L0_attn_b_k_norm_w", knw)
        c_qnw = self.load_headcol("L0_attn_b_q_norm_w", qnw)
        self.c_magic = self.emit(
            "g", lambda g: g.memset(self.stage[:, 14088:14092], 1.3211836172961055e+19),
            waits=[barrier], inc=True)
        c_mem = self.load_mem_weights()
        c_atw = self.load_atw(wscol, c_ws, c_war=barrier)
        wk = self.arena_bf[:, 0:8192]
        wv = self.stage_bf[:, 0:8192]
        c_wk = self.load_w1024("L0_attn_b_W_K", wk, wscol, c_ws, c_war=barrier)
        c_wv = self.load_w1024("L0_attn_b_W_V", wv, wscol, c_ws, c_war=barrier)
        c_at = self.proj_alpha_theta(c_atw, c_xo, c_rn, c_rcols)
        self.tap("at", self.stage_bf[:, 24576:25088], [c_at])
        c_am = self.a_mean_row(c_at)
        if _TRUNC[0] == "b1":
            return
        c_nw0, c_nw1 = self.grad_pairs(c_wk, c_wv, c_mem, c_xo, c_rn, c_knw,
                                       c_at, c_am)
        if c_nw0 is None:
            return
        self.tap("nw0", self.arena_bf[:, 8832:10880], [c_nw0])
        self.tap("nw1", self.stage_bf[:, 25088:27136], [c_nw1])
        if (_TRUNC[0] or "").startswith("b2") or _TRUNC[0] == "b3":
            return
        self.retrieve_pairs(c_nw0, c_nw1, c_xo, c_rn, c_wr, wrcol, c_qnw,
                            barrier)
        self.tap("x3", self.xf, list(self.c_x_all))

    def load_headcol(self, name, dst):
        e = self.emit
        gst = self.stage[0:1, 15904:15968]
        c_d = e("y", lambda y, o=gst,
                i_=self.inp[name].unsqueeze(0):
                y.dma_start(out=o, in_=i_), waits=[self.gst_free],
                inc=self.dsem("gst"), inck=16)
        ct = e("t", lambda t, o=self.ps[3][0:64, 0:1], s=gst:
               t.transpose(out=o, in_=s, identity=self.ident32[0:1, 0:1]),
               waits=[c_d, self.ps_rd[3]], inc=True)
        cv = e("v", lambda v, o=dst[0:64, :], pp=self.ps[3][0:64, 0:1]:
               v.tensor_copy(out=o, in_=pp), waits=[ct], inc=True)
        cv = e("v", lambda v, o=dst[64:128, :], i_=dst[0:64, :]:
               v.tensor_copy(out=o, in_=i_), waits=[cv], inc=True)
        self.ps_rd[3] = cv
        self.gst_free = cv
        return cv

    def load_w1024(self, name, slab, ws_cols, c_ws, c_war=None):
        e = self.emit
        W = self.inp[name]
        c_c = None
        for dd in range(DC):
            wi = dd % 2
            wst = self.stage[:, 9216 + wi * 1024: 9216 + wi * 1024 + 1024]
            c_d = e("y", lambda y, o=wst, i_=W[dd * 128:(dd + 1) * 128, :]:
                    y.dma_start(out=o, in_=i_),
                    waits=[self.wst2_free[wi], c_war],
                    inc=self.dsem("wl" + str(wi)), inck=16)
            c_c = e("s", lambda s, o=slab[:, dd * 1024:(dd + 1) * 1024],
                    i_=wst, w=ws_cols[:, dd:dd + 1]:
                    s.activation(out=o, in_=i_, func=AF.Copy, scale=w),
                    waits=[c_d, c_ws, c_war, c_c], inc=True)
            self.wst2_free[wi] = c_c
        return c_c

    def load_mem_weights(self):
        e = self.emit
        w0bd = self.arena_bf[:, 8192:8704]
        w1s = self.arena_bf[:, 8704:8832]
        w1bd = self.stage_bf[:, 27648:28160]
        st0 = self.stage[0:64, 15392:15648]
        cz0 = e("g", lambda g, o=w0bd: g.memset(o, 0.0), waits=[self.gst_free],
                inc=True)
        cz1 = e("g", lambda g, o=w1bd: g.memset(o, 0.0), waits=[cz0], inc=True)
        c_d0 = e("y", lambda y, o=st0, i_=self.inp["L0_attn_b_mem_W0"]:
                 y.dma_start(out=o, in_=i_), waits=[self.gst_free],
                 inc=self.dsem("mem"), inck=16)
        c0 = e("v", lambda v, o=w0bd[0:64, 0:256], i_=st0:
               v.tensor_copy(out=o, in_=i_), waits=[c_d0, cz0], inc=True)
        c0 = e("v", lambda v, o=w0bd[64:128, 256:512], i_=w0bd[0:64, 0:256]:
               v.tensor_copy(out=o, in_=i_), waits=[c0], inc=True)
        st1 = self.stage[:, 15648:15776]
        c_d1 = e("y", lambda y, o=st1.rearrange("p (c f) -> p c f", c=2),
                 i_=self.inp["L0_attn_b_mem_W1"].rearrange("(c p) f -> p c f", p=128):
                 y.dma_start(out=o, in_=i_), waits=[self.tok(self.dsem("mem"))],
                 inc=self.dsem("mem"), inck=16)
        c1 = e("v", lambda v, o=w1s, i_=st1: v.tensor_copy(out=o, in_=i_),
               waits=[c_d1], inc=True)
        tl = None
        for hc in range(2):
            tl = e("t", lambda t, o=self.ps_bf[3][0:64, hc * 128:(hc + 1) * 128],
                   s=w1s[:, hc * 64:(hc + 1) * 64]:
                   t.transpose(out=o, in_=s, identity=self.ident),
                   waits=([c1, self.ps_rd[3]] if hc == 0 else []),
                   inc=(hc == 1))
        c2 = e("v", lambda v, o=w1bd[0:64, 0:256], pp=self.ps_bf[3][0:64, 0:256]:
               v.tensor_copy(out=o, in_=pp), waits=[tl, cz1], inc=True)
        c2 = e("v", lambda v, o=w1bd[64:128, 256:512], i_=w1bd[0:64, 0:256]:
               v.tensor_copy(out=o, in_=i_), waits=[c2], inc=True)
        self.ps_rd[3] = c2
        self.gst_free = c2
        e("y", lambda y: y.dma_start(
            out=self.stage[:, 16032:16160].rearrange("p (c f) -> p c f", c=8),
            in_=self.inp["L0_attn_b_eta_w"].rearrange("(c p) f -> p c f", p=128)),
          waits=[c2], inc=self.dsem("eta"), inck=16)
        e("y", lambda y: y.dma_start(
            out=self.stage[0:1, 16160:16176],
            in_=self.inp["L0_attn_b_eta_b"].unsqueeze(0)),
          waits=[self.tok(self.dsem("eta"))], inc=self.dsem("eta"), inck=16)
        return (c0, c1, c2)

    def load_atw(self, wscol, c_ws, c_war):
        e = self.emit
        atw = self.arena_bf[:, 11136:11392]
        st = self.stage[:, 15392:15424]
        c_all = None
        for dd in range(DC):
            cds = []
            for k, nm in ((0, "L0_attn_b_alpha_w"), (1, "L0_attn_b_theta_w")):
                cds.append(e("y", lambda y, o=st[:, k * 16:(k + 1) * 16],
                             i_=self.inp[nm][dd * 128:(dd + 1) * 128, :]:
                             y.dma_start(out=o, in_=i_),
                             waits=[c_all, c_war, self.gst_free],
                             inc=self.dsem("atw"), inck=16))
            c_all = e("v", lambda v, o=atw[:, dd * 32:(dd + 1) * 32], i_=st,
                      w=wscol[:, dd:dd + 1]:
                      v.tensor_scalar(out=o, in0=i_, scalar1=w, scalar2=None,
                                      op0=ALU.mult),
                      waits=cds + [c_ws, c_all], inc=True)
        br = self.stage[0:1, 15424:15456]
        c_b1 = e("y", lambda y, o=br[:, 0:16],
                 i_=self.inp["L0_attn_b_alpha_b"].unsqueeze(0):
                 y.dma_start(out=o, in_=i_), waits=[c_all],
                 inc=self.dsem("bias"), inck=16)
        c_b2 = e("y", lambda y, o=br[:, 16:32],
                 i_=self.inp["L0_attn_b_theta_b"].unsqueeze(0):
                 y.dma_start(out=o, in_=i_), inc=self.dsem("bias"), inck=16)
        brbf = self.consts[0:1, 648:680]
        c_bb = e("v", lambda v, o=brbf, i_=br: v.tensor_copy(out=o, in_=i_),
                 waits=[c_b1, c_b2], inc=True)
        cbm = e("t", lambda t, o=self.ps[3][:, 0:32], l=self.onesrow,
                r=brbf: t.matmul(o, l, r, start=True, stop=True),
                waits=[c_bb, self.ps_rd[3], self.c_consts], inc=True)
        self.biasB = self.cols[:, 40:72]
        c_bB = e("v", lambda v, o=self.biasB, pp=self.ps[3][:, 0:32]:
                 v.tensor_copy(out=o, in_=pp), waits=[cbm], inc=True)
        self.ps_rd[3] = c_bB
        self.c_biasB = c_bB
        return c_all

    def proj_alpha_theta(self, c_atw, c_xo, c_rn, c_rcols):
        e = self.emit
        at = self.stage_bf[:, 24576:25088]
        atw = self.arena_bf[:, 11136:11392]
        c_ev = []
        for tc in range(SC):
            psi = tc % 4
            pb = self.ps[psi][:, 0:32]
            for dd in range(DC):
                wt = [c_atw, c_xo, self.ps_rd[psi]] if dd == 0 else []
                e("t", lambda t, o=pb,
                  l=self.xbf[:, dd * S + tc * 128: dd * S + (tc + 1) * 128],
                  r=atw[:, dd * 32:(dd + 1) * 32], st=(dd == 0),
                  sp=(dd == DC - 1):
                  t.matmul(o, l, r, start=st, stop=sp),
                  waits=wt, inc=(True if dd == DC - 1 else None))
            cmm = self.tok(self.sem_of["t"])
            cv = e("v", lambda v, o=at[:, tc * 32:(tc + 1) * 32], pp=pb,
                   rn=self.cols[:, tc:tc + 1], bb=self.biasB:
                   v.scalar_tensor_tensor(out=o, in0=pp, scalar=rn, in1=bb,
                                          op0=ALU.mult, op1=ALU.add),
                   waits=[cmm, c_rcols, self.c_biasB], inc=True)
            self.ps_rd[psi] = cv
            c_ev.append(cv)
        c_at = e("s", lambda s, o=at: s.activation(out=o, in_=o,
                                                   func=AF.Sigmoid),
                 waits=c_ev, inc=True)
        return c_at

    def a_mean_row(self, c_at):
        e = self.emit
        at = self.stage_bf[:, 24576:25088]
        onescol_bf = self.ones128[:, 0:1]
        for tc in range(SC):
            wt = [c_at, self.c_consts]
            if tc == 0:
                wt.append(self.ps_rd[3])
            e("t", lambda t, o=self.ps[3][0:1, 0:32],
              l=onescol_bf, r=at[:, tc * 32:(tc + 1) * 32],
              st=(tc == 0), sp=(tc == SC - 1):
              t.matmul(o, l, r, start=st, stop=sp),
              waits=wt, inc=(True if tc == SC - 1 else None))
        cs = self.tok(self.sem_of["t"])
        amr = self.arena[0:1, 5120:5152]
        cv = e("v", lambda v, o=amr, pp=self.ps[3][0:1, 0:32]:
               v.tensor_scalar(out=o, in0=pp, scalar1=1.0 / S, scalar2=None,
                               op0=ALU.mult), waits=[cs], inc=True)
        self.ps_rd[3] = cv
        ambf = self.consts[0:1, 680:712]
        cb = e("v", lambda v, o=ambf, i_=amr: v.tensor_copy(out=o, in_=i_),
               waits=[cv], inc=True)
        selA = self.B64[0:1, 0:128]
        selB = self.consts[0:1, 752:880]
        for p in range(NPAIR):
            for hh in range(2):
                wt = [cb]
                if p == 0 and hh == 0:
                    wt.append(self.ps_rd[2])
                e("t", lambda t, o=self.ps[2][:, 48 + p:49 + p],
                  l=(selA if hh == 0 else selB),
                  r=ambf[:, 2 * p + hh:2 * p + hh + 1],
                  st=(hh == 0), sp=(hh == 1):
                  t.matmul(o, l, r, start=st, stop=sp),
                  waits=wt, inc=(True if (p == NPAIR - 1 and hh == 1) else None))
        cm = self.tok(self.sem_of["t"])
        fam = self.cols[:, 72:80]
        cf = e("v", lambda v, o=fam, pp=self.ps[2][:, 48:56]:
               v.tensor_scalar(out=o, in0=pp, scalar1=-1.0, scalar2=1.0,
                               op0=ALU.mult, op1=ALU.add),
               waits=[cm], inc=True)
        self.ps_rd[2] = cf
        return cf

    def mem_oct_A(self, p, o, c_kf, c_mem):
        e = self.emit
        kf = self.stage_bf[:, 8192:10240]
        par = o % 2
        a1t = self.stage_bf[:, 16384 + par * 1024: 17408 + par * 1024]
        a1f = self.stage_bf[:, 18432 + par * 1024: 19456 + par * 1024]
        gg = self.stage_bf[:, 20480 + par * 1024: 21504 + par * 1024]
        a2 = self.stage_bf[:, 14336:16384]
        w0bd = self.arena_bf[:, 8192:8704]
        w1s = self.arena_bf[:, 8704:8832]
        qs = o * 256
        c_a1f = None
        for k in range(4):
            hh, hc = k // 2, k % 2
            psi = 1 - (k // 2)
            pb = self.ps[psi][:, (k % 2) * 512:(k % 2) * 512 + 256]
            wt = []
            if k == 0:
                wt = [c_kf, c_mem[0], self.ps_rd[1], self.oct_a1f_war[par]]
            elif k == 2:
                wt = [self.ps_rd[0]]
            e("t", lambda t, o2=pb,
              l=w0bd[hh * 64:(hh + 1) * 64,
                     hh * 256 + hc * 128: hh * 256 + (hc + 1) * 128],
              r=kf[hh * 64:(hh + 1) * 64, qs:qs + 256]:
              t.matmul(o2, l, r, start=True, stop=True),
              waits=wt, inc=(k % 2 == 1))
            if k % 2 == 1:
                cmmp = self.tok(self.sem_of["t"])
                psi2 = 1 - (k // 2)
                c_a1f = e("s", lambda s,
                          o2=a1f[:, (k - 1) * 256:(k + 1) * 256].rearrange(
                              "p (b c) -> p b c", b=2),
                          i_=self.ps[psi2].rearrange(
                              "p (b c) -> p b c", b=2)[:, :, 0:256]:
                          s.activation(out=o2, in_=i_, func=AF.Gelu),
                          waits=[cmmp, c_a1f], inc=True)
                self.ps_rd[psi2] = c_a1f
        c_a1t = c_gg = None
        cmm2l = []
        for sc in range(2):
            cch = o * 2 + sc
            psi = 1 - (sc % 2)
            e("t", lambda t, o2=self.ps[psi][:, 0:512],
              l=kf[:, cch * 128:(cch + 1) * 128], r=w0bd:
              t.matmul(o2, l, r, start=True, stop=True),
              waits=[self.ps_rd[psi], self.oct_a1t_war[par],
                     self.oct_gg_war[par]], inc=True)
            cmm2l.append(self.tok(self.sem_of["t"]))
        for sc in range(2):
            psi = 1 - (sc % 2)
            c_a1t = e("s", lambda s,
                      o2=a1t[:, sc * 512:(sc + 1) * 512],
                      i_=self.ps[psi][:, 0:512]:
                      s.activation(out=o2, in_=i_, func=AF.Gelu),
                      waits=[cmm2l[sc]], inc=True)
        for sc in range(2):
            psi = 1 - (sc % 2)
            c_gg = e("s", lambda s,
                     o2=gg[:, sc * 512:(sc + 1) * 512],
                     i_=self.ps[psi][:, 0:512]:
                     s.activation(out=o2, in_=i_, func=AF.Derivative_Gelu),
                     waits=[c_a1t], inc=True)
            self.ps_rd[psi] = c_gg
        for sc in range(2):
            for hh in range(2):
                for hc in range(2):
                    wt = [c_a1f, c_mem[1]]
                    if sc == 0 and hh == 0 and hc == 0:
                        wt.append(self.ps_rd[3])
                    e("t", lambda t,
                      o2=self.ps[3][:, sc * 128 + hh * 64: sc * 128 + (hh + 1) * 64],
                      l=a1f[:, (hh * 2 + hc) * 256 + sc * 128:
                            (hh * 2 + hc) * 256 + (sc + 1) * 128],
                      r=w1s[:, hc * 64:(hc + 1) * 64],
                      st=(hc == 0), sp=(hc == 1):
                      t.matmul(o2, l, r, start=st, stop=sp),
                      waits=wt,
                      inc=(True if (sc == 1 and hh == 1 and hc == 1) else None))
        cmm = self.tok(self.sem_of["t"])
        c_a2 = e("s", lambda s, o2=a2[:, qs:qs + 256], pp=self.ps[3][:, 0:256]:
                 s.activation(out=o2, in_=pp, func=AF.Copy),
                 waits=[cmm, self.oct_a2_war], inc=True)
        c_sq = e("s", lambda s, o2=a1f[:, 0:256], pp=self.ps[3][:, 0:256]:
                 s.activation(out=o2, in_=pp, func=AF.Square),
                 waits=[c_a2], inc=True)
        self.ps_rd[3] = c_sq
        return {"c_a1t": c_a1t, "c_gg": c_gg, "c_a1f": c_a1f, "c_a2": c_a2,
                "c_sq": c_sq, "cmm_a2": cmm}

    def mem_oct_B(self, p, o, st_, c_kt, c_vn, c_at):
        e = self.emit
        par = o % 2
        kt = self.stage_bf[:, 10240:12288]
        vt = self.stage_bf[:, 12288:14336]
        a2 = self.stage_bf[:, 14336:16384]
        at = self.stage_bf[:, 24576:25088]
        tmp = self.stage_bf[:, 18432 + par * 1024: 19456 + par * 1024]
        rmc = self.cols[:, 80:84]
        dyc = self.cols[:, 84:88]
        qs = o * 256
        c_r = e("v", lambda v, o2=rmc.unsqueeze(2),
                i_=tmp[:, 0:256].rearrange("p (g w) -> p g w", w=64):
                v.reduce_sum(out=o2, in_=i_, axis=AX.X),
                waits=[st_["c_sq"]], inc=True)
        c_r = e("s", lambda s, o2=rmc: s.activation(out=o2, in_=o2, func=AF.Ln,
                                                    bias=1e-8, scale=1.0 / HD),
                waits=[c_r], inc=True)
        c_rr = e("s", lambda s, o2=rmc: s.activation(out=o2, in_=o2,
                                                     func=AF.Exp, scale=-0.5),
                 waits=[c_r], inc=True)
        rrB = rmc.unsqueeze(2).broadcast_to([128, 4, 64])
        a2q4 = a2[:, qs:qs + 256].rearrange("p (g w) -> p g w", w=64)
        c_hn = e("v", lambda v: v.tensor_mul(out=a2q4, in0=a2q4, in1=rrB),
                 waits=[c_rr, st_["c_a2"]], inc=True)
        vq = vt[:, qs:qs + 256]
        c_dp = e("v", lambda v, o2=vq, a=a2[:, qs:qs + 256]:
                 v.tensor_add(out=o2, in0=o2, in1=a),
                 waits=[c_hn, c_vn], inc=True)
        thv = at.rearrange("p (c w) -> p c w", w=32)[:, o * 2:o * 2 + 2,
                                                    16 + 2 * p:18 + 2 * p]
        thB = thv.unsqueeze(3).broadcast_to([128, 2, 2, 64])
        vqh = vq.rearrange("p (c h w) -> p c h w", h=2, w=64)
        c_dp = e("v", lambda v: v.tensor_mul(out=vqh, in0=vqh, in1=thB),
                 waits=[c_dp, c_at], inc=True)
        c_dy = e("v", lambda v, o2=tmp[:, 256:512], a=vq,
                 b=a2[:, qs:qs + 256]:
                 v.tensor_mul(out=o2, in0=a, in1=b),
                 waits=[c_dp], inc=True)
        c_dy = e("v", lambda v, o2=dyc,
                 i_=tmp[:, 256:512].rearrange("p (g w) -> p g w", w=64):
                 v.reduce_sum(out=o2.unsqueeze(2), in_=i_, axis=AX.X),
                 waits=[c_dy], inc=True)
        dyB = dyc.unsqueeze(2).broadcast_to([128, 4, 64])
        c_t3 = e("v", lambda v, o2=tmp[:, 512:768].rearrange(
            "p (g w) -> p g w", w=64), a=a2q4:
            v.scalar_tensor_tensor(out=o2, in0=a, scalar=1.0 / HD, in1=dyB,
                                   op0=ALU.mult, op1=ALU.mult),
            waits=[c_dy], inc=True)
        c_dh = e("v", lambda v, o2=vq, b=tmp[:, 512:768]:
                 v.tensor_tensor(out=o2, in0=o2, in1=b, op=ALU.subtract),
                 waits=[c_t3], inc=True)
        vq3 = vq.rearrange("p (g w) -> p g w", w=64)
        c_dh = e("v", lambda v: v.scalar_tensor_tensor(
            out=vq3, in0=vq3, scalar=0.02, in1=rrB, op0=ALU.mult,
            op1=ALU.mult), waits=[c_dh, c_rr], inc=True)
        return c_dh

    def mem_oct_Cfg(self, p, o, c_dh, c_mem):
        e = self.emit
        par = o % 2
        vt = self.stage_bf[:, 12288:14336]
        gg = self.stage_bf[:, 20480 + par * 1024: 21504 + par * 1024]
        dh0 = self.stage_bf[:, 22528 + par * 1024: 23552 + par * 1024]
        dhf = self.stage_bf[:, 31232 + par * 256: 31488 + par * 256]
        w1bd = self.stage_bf[:, 27648:28160]
        ctl = None
        for sc in range(2):
            cch = o * 2 + sc
            wt = [c_dh, self.ps_rd[2] if sc == 0 else None,
                  self.oct_dhf_war[par] if sc == 0 else None]
            ctl = e("t", lambda t, o2=self.ps_bf[2][:, sc * 256:sc * 256 + 128],
                    s=vt[:, cch * 128:(cch + 1) * 128]:
                    t.transpose(out=o2, in_=s, identity=self.ident),
                    waits=wt, inc=(sc == 1))
        c_dhf = e("v", lambda v, o2=dhf.rearrange("p (c f) -> p c f", f=128),
                  pp=self.ps_bf[2][:, 0:512].rearrange(
                      "p (c f) -> p c f", f=256)[:, :, 0:128]:
                  v.tensor_copy(out=o2, in_=pp), waits=[ctl], inc=True)
        self.ps_rd[2] = c_dhf
        c_dh0 = None
        for sc in range(2):
            psi = 1 - (sc % 2)
            psx = self.ps[psi]
            e("t", lambda t, o2=psx[:, 0:512],
              l=dhf[:, sc * 128:(sc + 1) * 128], r=w1bd:
              t.matmul(o2, l, r, start=True, stop=True),
              waits=[c_dhf, c_mem[2], self.ps_rd[psi],
                     self.oct_dh0_war[par] if sc == 0 else None],
              inc=True)
            cmm2 = self.tok(self.sem_of["t"])
            c_dh0 = e("v", lambda v,
                      o2=dh0[:, sc * 512:(sc + 1) * 512],
                      pp=psx[:, 0:512], g2=gg[:, sc * 512:(sc + 1) * 512]:
                      v.tensor_tensor(out=o2, in0=pp, in1=g2, op=ALU.mult),
                      waits=[cmm2, c_dh0], inc=True)
            self.ps_rd[psi] = c_dh0
        self.oct_gg_war[par] = c_dh0
        return c_dh0

    def mem_oct_Ch(self, p, o, st_, c_dh, c_dh0, c_kt):
        e = self.emit
        par = o % 2
        kt = self.stage_bf[:, 10240:12288]
        vt = self.stage_bf[:, 12288:14336]
        a1t = self.stage_bf[:, 16384 + par * 1024: 17408 + par * 1024]
        dh0 = self.stage_bf[:, 22528 + par * 1024: 23552 + par * 1024]
        for sc in range(2):
            cch = o * 2 + sc
            for hh in range(2):
                for hc in range(2):
                    last = (sc == 1 and hh == 1 and hc == 1)
                    e("t", lambda t,
                      o2=self.ps[2][:, 512 + (hh * 2 + hc) * 64:
                                    512 + (hh * 2 + hc + 1) * 64],
                      l=a1t[:, sc * 512 + hh * 256 + hc * 128:
                            sc * 512 + hh * 256 + (hc + 1) * 128],
                      r=vt[:, cch * 128 + hh * 64: cch * 128 + (hh + 1) * 64],
                      st=(o == 0 and sc == 0 and hh == 0 and hc == 0),
                      sp=(o == 7 and sc == 1 and hh == 1 and hc == 1):
                      t.matmul(o2, l, r, start=st, stop=sp),
                      waits=[st_["c_a1t"], c_dh] + (
                          [self.g1_ps_war] if (o == 0 and sc == 0 and
                                               hh == 0 and hc == 0) else []),
                      inc=(True if last else None))
            for hh in range(2):
                wt = [c_dh0, c_kt]
                if o == 0 and sc == 0:
                    wt.append(self.g0_ps_war)
                e("t", lambda t,
                  o2=self.ps[3][0:64, 512 + hh * 256: 768 + hh * 256],
                  l=kt[:, cch * 128 + hh * 64: cch * 128 + (hh + 1) * 64],
                  r=dh0[:, sc * 512 + hh * 256: sc * 512 + (hh + 1) * 256],
                  st=(o == 0 and sc == 0 and hh == 0),
                  sp=(o == 7 and sc == 1 and hh == 1):
                  t.matmul(o2, l, r, start=st, stop=sp),
                  waits=wt, inc=(o == 7 and sc == 1 and hh == 1))
        c_gmm = self.tok(self.sem_of["t"])
        self.oct_a1t_war[par] = c_gmm
        self.oct_a1f_war[par] = c_dh
        self.oct_a2_war = c_gmm
        self.oct_dhf_war[par] = c_gmm
        self.oct_dh0_war[par] = c_gmm
        return c_gmm

    def grad_pairs(self, c_wk, c_wv, c_mem, c_xo, c_rn, c_knw, c_at, c_am):
        e = self.emit
        wk = self.arena_bf[:, 0:8192]
        wv = self.stage_bf[:, 0:8192]
        kf = self.stage_bf[:, 8192:10240]
        kt = self.stage_bf[:, 10240:12288]
        vt = self.stage_bf[:, 12288:14336]
        gg = self.stage_bf[:, 20480:22528]
        c_nw0 = c_nw1 = None
        c_kf_rd = None
        c_kt_war = None
        c_vt_war = None
        self.oct_a1t_war = {0: None, 1: None}
        self.oct_a1f_war = {0: None, 1: None}
        self.oct_gg_war = {0: None, 1: None}
        self.oct_a2_war = None
        self.oct_dhf_war = {0: None, 1: None}
        self.oct_dh0_war = {0: None, 1: None}
        self.g0_ps_war = self.ps_rd[3]
        self.g1_ps_war = self.ps_rd[2]
        self.g0sb_war = None
        npair = 1 if (_TRUNC[0] or "").startswith("b2") else NPAIR
        stepmax = 99
        tr = _TRUNC[0] or ""
        if tr.startswith("b2s"):
            stepmax = int(tr[3:])
        for p in range(npair):
            csqs = []
            ck = None
            for tt in range(4):
                pb = self.ps[0][:, (tt % 2) * 512:(tt % 2) * 512 + 512]
                for dd in range(DC):
                    wt = [c_wk, c_xo, self.ps_rd[0], c_kf_rd] if dd == 0 else []
                    e("t", lambda t, o=pb,
                      l=wk[:, dd * 1024 + p * 128: dd * 1024 + (p + 1) * 128],
                      r=self.xbf[:, dd * S + tt * 512: dd * S + tt * 512 + 512],
                      st=(dd == 0), sp=(dd == DC - 1):
                      t.matmul(o, l, r, start=st, stop=sp),
                      waits=wt, inc=(True if dd == DC - 1 else None))
                cmm = self.tok(self.sem_of["t"])
                ck = e("v", lambda v, o=kf[:, tt * 512:(tt + 1) * 512], pp=pb,
                       rn=self.bcast[:, tt * 512:(tt + 1) * 512]:
                       v.scalar_tensor_tensor(out=o, in0=pp, scalar=1.0,
                                              in1=rn, op0=ALU.mult,
                                              op1=ALU.mult),
                       waits=[cmm, ck] + (list(c_rn) if tt == 0 else []),
                       inc=True)
                self.ps_rd[0] = ck
                csqs.append(e("v", lambda v, o=gg[:, tt * 512:(tt + 1) * 512],
                              i_=kf[:, tt * 512:(tt + 1) * 512]:
                              v.tensor_mul(out=o, in0=i_, in1=i_),
                              waits=[ck] + ([self.oct_gg_war[0],
                                             self.oct_gg_war[1]]
                                            if tt == 0 else []),
                              inc=True))
            self.bcast0_war = ck
            if stepmax < 1:
                return None, None
            cnk = None
            for half in range(2):
                for sub in range(2):
                    wt = [csqs[half * 2 + sub], self.c_consts]
                    if sub == 0:
                        wt.append(self.ps_rd[1])
                    e("t", lambda t, o=self.ps[1][:, sub * 512:(sub + 1) * 512],
                      l=self.B64,
                      r=gg[:, half * 1024 + sub * 512: half * 1024 + (sub + 1) * 512]:
                      t.matmul(o, l, r, start=True, stop=True),
                      waits=wt, inc=(sub == 1))
                cbm = self.tok(self.sem_of["t"])
                cl = e("s", lambda s, h2=half: s.activation(
                    out=self.bcast[:, 2048 + h2 * 1024: 2048 + (h2 + 1) * 1024],
                    in_=self.ps[1][:, 0:1024],
                    func=AF.Ln, bias=1e-6, scale=1.0 / HD),
                    waits=[cbm, self.rk_war], inc=True)
                self.ps_rd[1] = cl
                cr = e("s", lambda s, h2=half: s.activation(
                    out=self.bcast[:, 2048 + h2 * 1024: 2048 + (h2 + 1) * 1024],
                    in_=self.bcast[:, 2048 + h2 * 1024: 2048 + (h2 + 1) * 1024],
                    func=AF.Exp, scale=-0.5), waits=[cl], inc=True)
                cnk = e("v", lambda v, o=kf[:, half * 1024:(half + 1) * 1024],
                        w=self.cols[:, 32:33],
                        r=self.bcast[:, 2048 + half * 1024: 2048 + (half + 1) * 1024]:
                        v.scalar_tensor_tensor(out=o, in0=o, scalar=w, in1=r,
                                               op0=ALU.mult, op1=ALU.mult),
                        waits=[cr, c_knw, cnk], inc=True)
                self.rk_war = cnk
            if stepmax < 2:
                return None, None
            vfst = gg
            cvf = [None] * 4
            for tt in range(4):
                pb = self.ps[3][:, (tt % 2) * 512:(tt % 2) * 512 + 512]
                for dd in range(DC):
                    if dd == 0:
                        if tt == 0:
                            wt = [c_wv, c_xo, self.ps_rd[3]]
                        elif tt == 1:
                            wt = [self.g0_ps_war]
                        else:
                            wt = [cvf[tt - 2]]
                    else:
                        wt = []
                    e("t", lambda t, o=pb,
                      l=wv[:, dd * 1024 + p * 128: dd * 1024 + (p + 1) * 128],
                      r=self.xbf[:, dd * S + tt * 512: dd * S + tt * 512 + 512],
                      st=(dd == 0), sp=(dd == DC - 1):
                      t.matmul(o, l, r, start=st, stop=sp),
                      waits=wt, inc=(True if dd == DC - 1 else None))
                cmm = self.tok(self.sem_of["t"])
                cvf[tt] = e("v", lambda v, o=vfst[:, tt * 512:(tt + 1) * 512],
                            pp=pb: v.tensor_copy(out=o, in_=pp),
                            waits=[cmm, cvf[tt - 1] if tt else None], inc=True)
                self.ps_rd[3] = cvf[tt]
            self.g0_ps_war = cvf[3]
            cv2 = None
            for g in range(4):
                tl = None
                for i in range(4):
                    cch = g * 4 + i
                    wt = [cvf[g], self.ps_rd[2] if i == 0 else None]
                    tl = e("t", lambda t,
                           o=self.ps_bf[2][:, i * 256:i * 256 + 128],
                           s=vfst[:, cch * 128:(cch + 1) * 128]:
                           t.transpose(out=o, in_=s, identity=self.ident),
                           waits=wt, inc=(i == 3))
                cv2 = e("v", lambda v,
                        o=vt.rearrange("p (c f) -> p c f", f=128)[:, g * 4:(g + 1) * 4, :],
                        pp=self.ps_bf[2][:, 0:1024].rearrange(
                            "p (c f) -> p c f", f=256)[:, :, 0:128]:
                        v.tensor_copy(out=o, in_=pp),
                        waits=[tl, cv2, c_vt_war if g == 0 else None],
                        inc=True)
                self.ps_rd[2] = cv2
            rnview = self.cols[:, 0:16].unsqueeze(2).broadcast_to([128, 16, 128])
            cvn = e("v", lambda v,
                    o=vt.rearrange("p (c f) -> p c f", f=128):
                    v.tensor_mul(out=o, in0=o, in1=rnview),
                    waits=[cv2, self.c_rcols_tok], inc=True)

            ctk = None
            for g in range(4):
                tl = None
                for i in range(4):
                    cch = g * 4 + i
                    wt = [cnk, self.ps_rd[2] if i == 0 else None,
                          c_kt_war if (g == 0 and i == 0) else None]
                    tl = e("t", lambda t,
                           o=self.ps_bf[2][:, i * 256:i * 256 + 128],
                           s=kf[:, cch * 128:(cch + 1) * 128]:
                           t.transpose(out=o, in_=s, identity=self.ident),
                           waits=wt, inc=(i == 3))
                ctk = e("v", lambda v,
                        o=kt.rearrange("p (c f) -> p c f", f=128)[:, g * 4:(g + 1) * 4, :],
                        pp=self.ps_bf[2][:, 0:1024].rearrange(
                            "p (c f) -> p c f", f=256)[:, :, 0:128]:
                        v.tensor_copy(out=o, in_=pp), waits=[tl, ctk], inc=True)
                self.ps_rd[2] = ctk
                cvn = e("v", lambda v, o=vt[:, g * 512:(g + 1) * 512],
                        b=kt[:, g * 512:(g + 1) * 512]:
                        v.tensor_tensor(out=o, in0=b, in1=o, op=ALU.subtract),
                        waits=[ctk, cvn], inc=True)
            if stepmax < 3:
                return None, None
            if stepmax < 4:
                return None, None
            nocts = 8
            if (_TRUNC[0] or "").startswith("b2q"):
                nocts = 2 * (int(_TRUNC[0][3:]) + 1)
            sts, chs, dh0s = {}, {}, {}
            cg = None
            for o in range(nocts):
                sts[o] = self.mem_oct_A(p, o, cnk, c_mem)
                if o >= 1:
                    dh0s[o - 1] = self.mem_oct_Cfg(p, o - 1, chs[o - 1], c_mem)
                chs[o] = self.mem_oct_B(p, o, sts[o], ctk, cvn, c_at)
                if o >= 1:
                    cg = self.mem_oct_Ch(p, o - 1, sts[o - 1], chs[o - 1],
                                         dh0s[o - 1], ctk)
            ol = nocts - 1
            dh0s[ol] = self.mem_oct_Cfg(p, ol, chs[ol], c_mem)
            cg = self.mem_oct_Ch(p, ol, sts[ol], chs[ol], dh0s[ol], ctk)
            if (_TRUNC[0] or "").startswith("b2q"):
                return None, None
            if stepmax < 8:
                return None, None
            c_nw0, c_nw1 = self.clip_pair(p, cg, c_am, c_mem)
            c_kf_rd = cg
            c_kt_war = cg
            c_vt_war = cg
        return c_nw0, c_nw1

    rk_war = None

    def clip_pair(self, p, c_gmm, c_am, c_mem):
        e = self.emit
        w0bd = self.arena_bf[:, 8192:8704]
        w1s = self.arena_bf[:, 8704:8832]
        nw0 = self.arena_bf[:, 8832:10880]
        nw1 = self.stage_bf[:, 25088:27136]
        gt0 = self.stage[:, 14592:14848]
        gt1 = self.stage[:, 14848:15104]
        tmp = self.stage[:, 15104:15360]
        fam = self.cols[:, 72:80]
        cga = e("v", lambda v, o=gt0[0:64, :], pp=self.ps[3][0:64, 512:768]:
                v.tensor_copy(out=o, in_=pp),
                waits=[c_gmm, self.gt_war], inc=True)
        cga = e("v", lambda v, o=gt0[64:128, :], pp=self.ps[3][0:64, 768:1024]:
                v.tensor_copy(out=o, in_=pp), waits=[cga], inc=True)
        cgb = e("v", lambda v, o=gt1, pp=self.ps[2][:, 512:768]:
                v.tensor_copy(out=o, in_=pp), waits=[cga], inc=True)
        self.g0_ps_war = cga
        self.g1_ps_war = cgb
        self.ps_rd[2] = cgb
        csq = e("v", lambda v, o=tmp, a=gt0: v.tensor_mul(out=o, in0=a, in1=a),
                waits=[cga], inc=True)
        cc0 = e("v", lambda v, o=self.cols[:, 88:89].unsqueeze(2),
                i_=tmp.rearrange("p (g w) -> p g w", g=1):
                v.reduce_sum(out=o, in_=i_, axis=AX.X), waits=[csq], inc=True)
        csq = e("v", lambda v, o=tmp, a=gt1: v.tensor_mul(out=o, in0=a, in1=a),
                waits=[cgb, cc0], inc=True)
        cc1 = e("v", lambda v, o=self.cols[:, 89:91].unsqueeze(2),
                i_=tmp.rearrange("p (g w) -> p g w", g=2):
                v.reduce_sum(out=o, in_=i_, axis=AX.X), waits=[csq], inc=True)
        ccb = e("v", lambda v, o=self.consts[:, 712:715],
                i_=self.cols[:, 88:91]: v.tensor_copy(out=o, in_=i_),
                waits=[cc0, cc1], inc=True)
        scol = self.consts[:, 712:715]
        for i, (lh, rc) in enumerate((
                (self.B64[:, 0:1], 0), (self.B64[:, 64:65], 0),
                (self.ones128[:, 0:1], 1), (self.ones128[:, 0:1], 2))):
            wt = [ccb, self.c_consts]
            if i == 0:
                wt.append(self.ps_rd[3])
            e("t", lambda t, o=self.ps[3][0:1, i:i + 1], l=lh,
              r=scol[:, rc:rc + 1]:
              t.matmul(o, l, r, start=True, stop=True),
              waits=wt, inc=(i == 3))
        cmm = self.tok(self.sem_of["t"])
        sqr = self.stage[0:1, 15488:15504]
        c_q = e("v", lambda v, o=sqr[:, 0:4], pp=self.ps[3][0:1, 0:4]:
                v.tensor_copy(out=o, in_=pp), waits=[cmm], inc=True)
        self.ps_rd[3] = c_q
        c_q = e("v", lambda v, o=sqr[:, 4:6],
                a=sqr.rearrange("o (g w) -> o g w", w=2)[:, 0:1, :].squeeze(1),
                b=sqr.rearrange("o (g w) -> o g w", w=2)[:, 1:2, :].squeeze(1):
                v.tensor_add(out=o, in0=a, in1=b), waits=[c_q], inc=True)
        c_q = e("s", lambda s, o=sqr[:, 4:6]: s.activation(
            out=o, in_=o, func=AF.Sqrt), waits=[c_q], inc=True)
        c_q = e("v", lambda v, o=sqr[:, 4:6]: v.tensor_scalar(
            out=o, in0=o, scalar1=1e-6, scalar2=None, op0=ALU.add),
            waits=[c_q], inc=True)
        c_q = e("v", lambda v, o=sqr[:, 4:6]: v.reciprocal(out=o, in_=o),
                waits=[c_q], inc=True)
        c_q = e("v", lambda v, o=sqr[:, 4:6]: v.tensor_scalar(
            out=o, in0=o, scalar1=-10.0, scalar2=-1.0, op0=ALU.mult,
            op1=ALU.max), waits=[c_q], inc=True)
        cfbf = self.consts[0:1, 720:722]
        c_cf = e("v", lambda v, o=cfbf, i_=sqr[:, 4:6]:
                 v.tensor_copy(out=o, in_=i_), waits=[c_q], inc=True)
        for hh in range(2):
            wt = [c_cf, self.c_consts]
            if hh == 0:
                wt.append(self.ps_rd[3])
            e("t", lambda t, o=self.ps[3][:, 8:9],
              l=(self.B64[0:1, 0:128] if hh == 0 else
                 self.consts[0:1, 752:880]),
              r=cfbf[:, hh:hh + 1], st=(hh == 0), sp=(hh == 1):
              t.matmul(o, l, r, start=st, stop=sp),
              waits=wt, inc=(hh == 1))
        ambf = self.consts[0:1, 680:712]
        for i2, cell in enumerate((cfbf[:, 0:1], cfbf[:, 1:2],
                                   ambf[:, 2 * p:2 * p + 1],
                                   ambf[:, 2 * p + 1:2 * p + 2])):
            e("t", lambda t, o=self.ps[3][:, 12 + i2:13 + i2],
              l=self.ones128[0:1, :], r=cell:
              t.matmul(o, l, r, start=True, stop=True),
              waits=([c_cf] if i2 == 0 else []), inc=(i2 == 3))
        cmm2 = self.tok(self.sem_of["t"])
        ncf = self.cols[:, 91:92]
        c_cc = e("v", lambda v, o=ncf, pp=self.ps[3][:, 8:9]:
                 v.tensor_copy(out=o, in_=pp), waits=[cmm2], inc=True)
        ncfu = self.cols[:, 92:94]
        c_cu = e("v", lambda v, o=ncfu, pp=self.ps[3][:, 12:14]:
                 v.tensor_copy(out=o, in_=pp), waits=[c_cc], inc=True)
        famu = self.cols[:, 94:96]
        c_fu = e("v", lambda v, o=famu, pp=self.ps[3][:, 14:16]:
                 v.tensor_scalar(out=o, in0=pp, scalar1=-1.0, scalar2=1.0,
                                 op0=ALU.mult, op1=ALU.add),
                 waits=[c_cu], inc=True)
        self.ps_rd[3] = c_fu
        c_tm = e("v", lambda v, o=tmp, i_=gt0, cc=ncf:
                 v.tensor_scalar(out=o, in0=i_, scalar1=cc, scalar2=None,
                                 op0=ALU.mult), waits=[c_cc], inc=True)
        c_nw0 = None
        for hh in range(2):
            c_nw0 = e("v", lambda v,
                      o=nw0[hh * 64:(hh + 1) * 64, p * 256:(p + 1) * 256],
                      w=w0bd[hh * 64:(hh + 1) * 64, hh * 256:(hh + 1) * 256],
                      f=fam[hh * 64:(hh + 1) * 64, p:p + 1],
                      tm=tmp[hh * 64:(hh + 1) * 64, :]:
                      v.scalar_tensor_tensor(out=o, in0=w, scalar=f, in1=tm,
                                             op0=ALU.mult, op1=ALU.add),
                      waits=[c_tm, c_am, c_mem[0], c_nw0], inc=True)
        tmpw2 = self.stage[:, 15360:15488]
        c_nw1 = None
        for h2 in range(2):
            c_tm = e("v", lambda v,
                     o=tmp[:, h2 * 128:(h2 + 1) * 128],
                     i_=gt1[:, h2 * 128:(h2 + 1) * 128], cc=ncfu[:, h2:h2 + 1]:
                     v.tensor_scalar(out=o, in0=i_, scalar1=cc, scalar2=None,
                                     op0=ALU.mult),
                     waits=[c_nw0, cgb, c_cu], inc=True)
            c_f2 = e("v", lambda v, o=tmpw2, i_=w1s, f=famu[:, h2:h2 + 1]:
                     v.tensor_scalar(out=o, in0=i_, scalar1=f, scalar2=None,
                                     op0=ALU.mult),
                     waits=[c_tm, c_fu, c_mem[1], self.f2_war], inc=True)
            c_nw1 = e("v", lambda v,
                      o=nw1[:, p * 256 + h2 * 128: p * 256 + (h2 + 1) * 128],
                      tm=tmp[:, h2 * 128:(h2 + 1) * 128], w2=tmpw2:
                      v.tensor_add(out=o, in0=tm, in1=w2),
                      waits=[c_f2, self.nw1_war, c_nw1], inc=True)
            self.f2_war = c_nw1
        self.gt_war = c_nw1
        self.f2_war = c_nw1
        if p == 0:
            self.tap("gt0", gt0, [c_nw0])
            self.tap("gt1", gt1, [c_nw1])
        return c_nw0, c_nw1

    gt_war = None
    vf_war = None
    rsq_war = None
    f2_war = None
    nw1_war = None
    nw1_war = None

    def retrieve_pairs(self, c_nw0, c_nw1, c_xo, c_rn, c_wr, wrcol, c_qnw,
                       barrier):
        e = self.emit
        wq = self.arena_bf[:, 0:8192]
        qfb = [self.stage_bf[:, 8192:10240], self.stage_bf[:, 4608:6656]]
        nw0 = self.arena_bf[:, 8832:10880]
        nw1 = self.stage_bf[:, 25088:27136]
        h0f = self.stage_bf[:, 16384:24576]
        sqr = self.stage_bf[:, 20480:22528] if False else None
        c_wq = self.load_w1024("L0_attn_b_W_Q", wq, wrcol, c_wr,
                               c_war=c_nw1)
        c_qf_rd = None
        c_h0_war = [None] * 4
        tmpsq = self.stage_bf[:, 2048:4096]
        self.qf_ck = None
        self.qf_sq = {}
        self.qrms_war = None
        self.sq2_war = [None, None]
        self.ps_rd_1h = [self.ps_rd[1], self.ps_rd[1]]

        def emit_qf(p2, tt):
            qf2 = qfb[p2 % 2]
            pb = self.ps[0][:, (tt % 2) * 512:(tt % 2) * 512 + 512]
            for dd in range(DC):
                wt = [c_wq, c_xo, self.ps_rd[0], c_qf_rd] if dd == 0 else []
                e("t", lambda t, o=pb,
                  l=wq[:, dd * 1024 + p2 * 128: dd * 1024 + (p2 + 1) * 128],
                  r=self.xbf[:, dd * S + tt * 512: dd * S + tt * 512 + 512],
                  st=(dd == 0), sp=(dd == DC - 1):
                  t.matmul(o, l, r, start=st, stop=sp),
                  waits=wt, inc=(True if dd == DC - 1 else None))
            cmm = self.tok(self.sem_of["t"])
            ck = e("v", lambda v, o=qf2[:, tt * 512:(tt + 1) * 512], pp=pb,
                   rn=self.bcast[:, tt * 512:(tt + 1) * 512]:
                   v.scalar_tensor_tensor(out=o, in0=pp, scalar=1.0,
                                          in1=rn, op0=ALU.mult,
                                          op1=ALU.mult),
                   waits=[cmm, self.qf_ck] + list(c_rn), inc=True)
            self.qf_ck = ck
            self.ps_rd[0] = ck
            self.qf_sq[(p2, tt)] = e(
                "v", lambda v, o=tmpsq[:, tt * 512:(tt + 1) * 512],
                i_=qf2[:, tt * 512:(tt + 1) * 512]:
                v.tensor_mul(out=o, in0=i_, in1=i_),
                waits=[ck, self.qrms_war], inc=True)
            return ck

        def emit_fin(p2, qf2, cnk2, tt, p1q, cev, csq2):
            wt = [csq2, self.ps_rd_1h[tt % 2]]
            e("t", lambda t, o=self.ps[1][:, (tt % 2) * 512:
                                          (tt % 2) * 512 + 512],
              l=self.B64,
              r=self.stage_bf[:, 1024 + (tt % 2) * 512:
                              1536 + (tt % 2) * 512]:
              t.matmul(o, l, r, start=True, stop=True),
              waits=wt, inc=True)
            self.sq2_war[tt % 2] = self.tok(self.sem_of["t"])
            cbm = self.tok(self.sem_of["t"])
            r2B = self.bcast[:, 2048 + (tt % 2) * 512:
                             2048 + (tt % 2) * 512 + 512]
            cl = e("s", lambda s, o=r2B,
                   i_=self.ps[1][:, (tt % 2) * 512:(tt % 2) * 512 + 512]:
                   s.activation(out=o, in_=i_, func=AF.Ln, bias=1e-8,
                                scale=1.0 / HD),
                   waits=[cbm, self.rk_war], inc=True)
            self.ps_rd_1h[tt % 2] = cl
            self.ps_rd[1] = cl
            cr = e("s", lambda s, o=r2B: s.activation(
                out=o, in_=o, func=AF.Exp, scale=-0.5),
                waits=[cl], inc=True)
            tmpo = self.stage_bf[:, 4096:4608]
            co = e("v", lambda v, o=tmpo, a=p1q, b=r2B:
                   v.tensor_mul(out=o, in0=a, in1=b),
                   waits=[cr, self.tmpo_war], inc=True)
            co = e("v", lambda v, o=tmpo, b=qf2[:, tt * 512:(tt + 1) * 512]:
                   v.tensor_add(out=o, in0=o, in1=b),
                   waits=[co, cnk2], inc=True)
            cx = e("v", lambda v,
                   o=self.xf[:, p2 * S + tt * 512: p2 * S + tt * 512 + 512],
                   b=tmpo: v.tensor_add(out=o, in0=o, in1=b),
                   waits=[co, self.c_x_all[p2]], inc=True)
            self.c_x_all[p2] = cx
            self.tmpo_war = cx
            self.p1q_war[tt % 2] = csq2
            self.rk_war = co
            return cev, cr, p1q, r2B

        pend = []
        for p in range(NPAIR):
            qf = qfb[p % 2]
            for tt in range(4):
                if (p, tt) not in self.qf_sq:
                    emit_qf(p, tt)
            ck = self.qf_ck
            cnk = None
            for half in range(2):
                for sub in range(2):
                    wt = [self.qf_sq[(p, half * 2 + sub)], self.c_consts]
                    if sub == 0:
                        wt.append(self.ps_rd[1])
                    e("t", lambda t, o=self.ps[1][:, sub * 512:(sub + 1) * 512],
                      l=self.B64,
                      r=tmpsq[:, half * 1024 + sub * 512: half * 1024 + (sub + 1) * 512]:
                      t.matmul(o, l, r, start=True, stop=True),
                      waits=wt, inc=(sub == 1))
                cbm = self.tok(self.sem_of["t"])
                cl = e("s", lambda s, h2=half: s.activation(
                    out=self.bcast[:, 2048 + h2 * 1024: 2048 + (h2 + 1) * 1024],
                    in_=self.ps[1][:, 0:1024],
                    func=AF.Ln, bias=1e-6, scale=1.0 / HD),
                    waits=[cbm, self.rk_war], inc=True)
                self.ps_rd[1] = cl
                cr = e("s", lambda s, h2=half: s.activation(
                    out=self.bcast[:, 2048 + h2 * 1024: 2048 + (h2 + 1) * 1024],
                    in_=self.bcast[:, 2048 + h2 * 1024: 2048 + (h2 + 1) * 1024],
                    func=AF.Exp, scale=-0.5), waits=[cl], inc=True)
                cnk = e("v", lambda v, o=qf[:, half * 1024:(half + 1) * 1024],
                        w=self.cols[:, 33:34],
                        r=self.bcast[:, 2048 + half * 1024: 2048 + (half + 1) * 1024]:
                        v.scalar_tensor_tensor(out=o, in0=o, scalar=w, in1=r,
                                               op0=ALU.mult, op1=ALU.mult),
                        waits=[cr, c_qnw, cnk], inc=True)
                self.rk_war = cnk
            self.qrms_war = self.tok(self.sem_of["t"])
            self.ps_rd_1h = [self.ps_rd[1], self.ps_rd[1]]
            h0ps = [self.ps[2][:, 0:512], self.ps[2][:, 512:1024],
                    self.ps[0][:, 0:512], self.ps[0][:, 512:1024]]
            c_h0 = [None] * 4

            def emit_h0(tt):
                for k in range(4):
                    hh, hc = k // 2, k % 2
                    pb = h0ps[k]
                    if tt == 0:
                        wt = [cnk, c_nw0, c_h0_war[k],
                              self.ps_rd[2] if k == 0 else None,
                              self.ps_rd[0] if k == 2 else None]
                    else:
                        wt = [c_h0[k]]
                    cmmi = e("t", lambda t, o=pb,
                             l=nw0[hh * 64:(hh + 1) * 64,
                                   p * 256 + hc * 128: p * 256 + (hc + 1) * 128],
                             r=qf[hh * 64:(hh + 1) * 64, tt * 512:(tt + 1) * 512]:
                             t.matmul(o, l, r, start=True, stop=True),
                             waits=wt, inc=True)
                    c_h0[k] = e("s", lambda s,
                                o=h0f[:, k * 2048 + tt * 512: k * 2048 + (tt + 1) * 512],
                                i_=pb: s.activation(out=o, in_=i_, func=AF.Gelu),
                                waits=[cmmi, c_h0[k]], inc=True)
                self.ps_rd[2] = c_h0[1]
                self.ps_rd[0] = c_h0[3]

            def emit_pre1(tt):
                pb = self.ps[3][:, (tt % 2) * 512:(tt % 2) * 512 + 512]
                for h2 in range(2):
                    for hc in range(2):
                        wt = [c_h0[h2 * 2 + hc], c_nw1]
                        if h2 == 0 and hc == 0:
                            wt.append(self.ps_rd[3])
                        e("t", lambda t, o=pb[h2 * 64:(h2 + 1) * 64, :],
                          l=nw1[:, p * 256 + (h2 * 2 + hc) * 64:
                                p * 256 + (h2 * 2 + hc + 1) * 64],
                          r=h0f[:, (h2 * 2 + hc) * 2048 + tt * 512:
                                (h2 * 2 + hc) * 2048 + (tt + 1) * 512],
                          st=(hc == 0), sp=(hc == 1):
                          t.matmul(o, l, r, start=st, stop=sp),
                          waits=wt, inc=(True if (h2 == 1 and hc == 1) else None))
                cmm = self.tok(self.sem_of["t"])
                p1q = self.stage_bf[:, (tt % 2) * 512:(tt % 2) * 512 + 512]
                cev = e("v", lambda v, o=p1q, pp=pb: v.tensor_copy(out=o, in_=pp),
                        waits=[cmm, self.p1q_war[tt % 2]], inc=True)
                self.ps_rd[3] = cev
                sq2 = self.stage_bf[:, 1024 + (tt % 2) * 512:
                                    1536 + (tt % 2) * 512]
                csq2 = e("v", lambda v, o=sq2, i_=p1q:
                         v.tensor_mul(out=o, in0=i_, in1=i_),
                         waits=[cev, self.sq2_war[tt % 2]], inc=True)
                return pb, p1q, cev, csq2

            if pend:
                emit_fin(*pend[0])
            emit_h0(0)
            if len(pend) > 1:
                emit_fin(*pend[1])
            pend = []
            emit_h0(1)
            emit_h0(2)
            pr = {}
            pr[0] = emit_pre1(0)
            pr[1] = emit_pre1(1)
            emit_h0(3)
            emit_fin(p, qf, cnk, 0, pr[0][1], pr[0][2], pr[0][3])
            emit_fin(p, qf, cnk, 1, pr[1][1], pr[1][2], pr[1][3])
            if p + 1 < NPAIR:
                emit_qf(p + 1, 0)
            pr[2] = emit_pre1(2)
            pr[3] = emit_pre1(3)
            if p + 1 < NPAIR:
                emit_qf(p + 1, 1)
            c_qf_rd = self.tok(self.sem_of["t"])
            for k in range(4):
                c_h0_war[k] = self.tok(self.sem_of["t"])
            self.ret_tmp_war = self.tok(self.sem_of["t"])
            self.bcast0_war = ck
            if p + 1 < NPAIR:
                pend = [(p, qf, cnk, 2, pr[2][1], pr[2][2], pr[2][3]),
                        (p, qf, cnk, 3, pr[3][1], pr[3][2], pr[3][3])]
            else:
                emit_fin(p, qf, cnk, 2, pr[2][1], pr[2][2], pr[2][3])
                _, cr, p1q, r2B = emit_fin(p, qf, cnk, 3, pr[3][1], pr[3][2],
                                           pr[3][3])
                if p == int(os.environ.get("TAPPAIR", NPAIR - 1)):
                    tpq = self.tap("p1q3", p1q, [pr[3][2]])
                    trb = self.tap("r2b3", r2B, [cr])
                    if trb is not None:
                        self.p1q_war[3 % 2] = trb
                        self.rk_war = trb
        self.c_x = self.c_x_all[-1]

    ret_tmp_war = None
    tmpP_war = None
    tmpo_war = None
    ps_rd_1h = None
    sq2_war = None

    def emit_store(self):
        e = self.emit
        for c in range(SC):
            pi = 2 + (c % 2)
            pb = self.ps[pi]
            tlast = None
            for dd in range(DC):
                src = self.xf[:, dd * S + c * 128: dd * S + (c + 1) * 128]
                w = [self.c_xf, self.ps_rd[pi]] if dd == 0 else []
                tlast = e("t", lambda t, o=pb[:, dd * 128:(dd + 1) * 128], s=src:
                          t.transpose(out=o, in_=s, identity=self.ident32),
                          waits=w, inc=(True if dd == DC - 1 else None))
            self.ps_wr[pi] = tlast
            dst = self.arena[:, (c % 2) * 1024:(c % 2) * 1024 + 1024]
            war = self._store_dma[c - 2] if c >= 2 else None
            cv = e("v", lambda v, d=dst, p=pb[:, :]: v.tensor_copy(out=d, in_=p),
                   waits=[tlast, war], inc=True)
            self.ps_rd[pi] = cv
            dma = e("y", lambda y, c=c, d=dst: y.dma_start(
                out=self.out.rearrange("(c p) f -> c p f", p=128)[c], in_=d),
                waits=[cv], inc=self.dsem("st" + str(c % 2)), inck=16)
            self._store_dma.append(dma)
        tailw = []
        for s2m in self._dma_sems.values():
            if self.ctr[s2m] > 0:
                tailw.append((s2m, self.ctr[s2m]))
        e("y", lambda y: y.nop(), waits=tailw)

    _store_dma = None

    def build_full(self):
        self._store_dma = []
        return self.build()


def build_program(dbg=()):
    b = Builder(dbg)
    nc = b.build_full()
    return nc, b


def kernel(**inputs):
    nc, b = build_program()
    xs = np.ascontiguousarray(inputs["L0_attn_a_x"], dtype=np.float32)
    in_maps = []
    for core in range(8):
        m = {}
        for name, _ in IN_SPECS:
            if name == "L0_attn_a_x":
                m[name] = np.ascontiguousarray(xs[core])
            else:
                m[name] = np.ascontiguousarray(inputs[name], dtype=np.float32)
        in_maps.append(m)
    res = run_bass_kernel_spmd(nc, in_maps, core_ids=list(range(8)))
    return np.stack([np.asarray(r["out"]) for r in res.results], axis=0)
